# version 09
# baseline (speedup 1.0000x reference)
import math
from contextlib import ExitStack

import numpy as np
import ml_dtypes

import concourse.bass as bass
import concourse.mybir as mybir
from concourse.bass_utils import run_bass_kernel_spmd

F32 = mybir.dt.float32
BF16 = mybir.dt.bfloat16
I32 = mybir.dt.int32
AF = mybir.ActivationFunctionType
ALU = mybir.AluOpType
AX = mybir.AxisListType

N_CORES = 8
B = 4
T = 2048
TT = B * T
D = 1024
DT = D // 128

BF16_NP = ml_dtypes.bfloat16



class Sem:

    def __init__(self, h):
        self.h = h


class Prog:
    def __init__(self, nc=None):
        self.nc = nc or bass.Bass("TRN2", target_bir_lowering=False)
        self.stack = ExitStack()
        self.scopes = []
        self._sem_i = 0
        self._free = []
        self._stage_sems = []
        self._sclear = None
        self.stage_idx = 0

    def _alloc_sem(self, name=None):
        self._sem_i += 1
        return self.stack.enter_context(
            self.nc.semaphore(name or f"s{self._sem_i}"))

    def sem(self, name=None):
        h = self._free.pop() if self._free else self._alloc_sem(name)
        self._stage_sems.append(h)
        return Sem(h)

    def begin_stage(self):
        self._free.extend(self._stage_sems)
        self._stage_sems = []
        self.stage_idx += 1
        self._entry_done = False
        if self._sclear is None:
            self._sclear = self._alloc_sem("sclear")

    def emit_entry(self, sync_eng, gate):
        if self._entry_done:
            return
        self._entry_done = True
        if gate is not None:
            sync_eng.wait_ge(gate[0].h, gate[1])
        for h in self._stage_sems:
            sync_eng.sem_clear(h)
        sync_eng.nop().then_inc(self._sclear, 1)

    def entry_wait(self, eng):
        eng.wait_ge(self._sclear, self.stage_idx)

    def push_scope(self):
        self.scopes.append(ExitStack())

    def pop_scope(self):
        self.scopes.pop().close()

    def _scope(self):
        return self.scopes[-1] if self.scopes else self.stack

    def sbuf(self, name, shape, dtype):
        return self._scope().enter_context(
            self.nc.sbuf_tensor(name, shape, dtype))

    def psum(self, name, shape, dtype=F32):
        return self._scope().enter_context(
            self.nc.psum_tensor(name, shape, dtype))

    def dram(self, name, shape, dtype, kind):
        return self.nc.dram_tensor(name, shape, dtype, kind=kind).ap()

    def block(self):
        return self.nc.Block()

    def close(self):
        self.stack.close()
        return self.nc



def emit_ln(p, x_hbm, xn_bf_hbm, xn_f32_hbm=None, tag="ln", gate=None):
    nc = p.nc
    p.begin_stage()
    p.push_scope()
    C = 512
    NCH = TT // C
    EPS = 1e-5
    wantf32 = xn_f32_hbm is not None
    SO = 16 * DT * (2 if wantf32 else 1)

    xc = [[p.sbuf(f"{tag}_x{j}_{i}", [128, C], F32) for i in range(DT)]
          for j in range(2)]
    xb = [p.sbuf(f"{tag}_xb{i}", [128, C], BF16) for i in range(DT)]
    sq = [p.sbuf(f"{tag}_sq{i}", [128, C], BF16) for i in range(DT)]
    ones_col = p.sbuf(f"{tag}_ones", [128, 1], BF16)
    ones_row = p.sbuf(f"{tag}_onesr", [1, 128], F32)
    rw = p.sbuf(f"{tag}_rows", [1, 6 * C], F32)
    a_b = p.sbuf(f"{tag}_ab", [128, C], F32)
    b_b = p.sbuf(f"{tag}_bb", [128, C], F32)
    tmp = p.sbuf(f"{tag}_tmp", [128, C], F32)
    out_bf = [p.sbuf(f"{tag}_ob{i}", [128, C], BF16) for i in range(DT)]
    out_f = [p.sbuf(f"{tag}_of{i}", [128, C], F32) for i in range(DT)] \
        if wantf32 else None

    ps_sx = p.psum(f"{tag}_psx", [1, C])
    ps_sxx = p.psum(f"{tag}_psxx", [1, C])
    ps_a = p.psum(f"{tag}_pa", [128, C])
    ps_b = p.psum(f"{tag}_pb", [128, C])

    sd2 = [p.sem(), p.sem()]
    ss, st, sv, so = (p.sem() for _ in range(4))
    sdone = p.sem()
    R_M, R_E, R_V, R_SD, R_RS, R_B = (i * C for i in range(6))

    with p.block() as blk:
        @blk.sync
        def _(sync):
            p.emit_entry(sync, gate)
            for c in range(NCH + 1):
                if c < NCH:
                    if c >= 2:
                        sync.wait_ge(sv.h, 3 * (c - 2) + 3)
                    for i in range(DT):
                        sync.dma_start(
                            out=xc[c % 2][i][:, :],
                            in_=x_hbm[i * 128:(i + 1) * 128, c * C:(c + 1) * C],
                        ).then_inc(sd2[c % 2].h, 16)
                if c >= 1:
                    cc = c - 1
                    sync.wait_ge(sv.h, 3 * cc + 3)
                    for i in range(DT):
                        sync.dma_start(
                            out=xn_bf_hbm[i * 128:(i + 1) * 128, cc * C:(cc + 1) * C],
                            in_=out_bf[i][:, :],
                        ).then_inc(so.h, 16)
                        if wantf32:
                            sync.dma_start(
                                out=xn_f32_hbm[i * 128:(i + 1) * 128, cc * C:(cc + 1) * C],
                                in_=out_f[i][:, :],
                            ).then_inc(so.h, 16)
            sync.wait_ge(so.h, NCH * SO)
            sync.nop().then_inc(sdone.h, 1)

        @blk.gpsimd
        def _(g):
            p.entry_wait(g)
            g.memset(ones_col[:, :], 1.0)
            g.memset(ones_row[:, :], 1.0).then_inc(st.h, 1)

        @blk.scalar
        def _(sc):
            p.entry_wait(sc)
            for c in range(NCH):
                sc.wait_ge(sd2[c % 2].h, (c // 2 + 1) * DT * 16)
                if c >= 1:
                    sc.wait_ge(st.h, 2 * (c - 1) + 2)
                for i in range(DT):
                    sc.copy(xb[i][:, :], xc[c % 2][i][:, :])
                    last = sc.activation(sq[i][:, :], xc[c % 2][i][:, :], AF.Square)
                last.then_inc(ss.h, 1)
                sc.wait_ge(sv.h, 3 * c + 1)
                sc.activation(rw[0:1, R_SD:R_SD + C], rw[0:1, R_V:R_V + C],
                              AF.Sqrt).then_inc(ss.h, 1)

        @blk.tensor
        def _(te):
            p.entry_wait(te)
            te.wait_ge(st.h, 1)
            for c in range(NCH):
                te.wait_ge(ss.h, 2 * c + 1)
                if c >= 1:
                    te.wait_ge(sv.h, 3 * (c - 1) + 1)
                for i in range(DT):
                    te.matmul(ps_sx[:, :], ones_col[:, :], xb[i][:, :],
                              start=(i == 0), stop=(i == DT - 1))
                    m2 = te.matmul(ps_sxx[:, :], ones_col[:, :], sq[i][:, :],
                                   start=(i == 0), stop=(i == DT - 1))
                m2.then_inc(st.h, 1)
                te.wait_ge(sv.h, 3 * c + 2)
                te.matmul(ps_a[:, :], ones_row[:, :], rw[0:1, R_RS:R_RS + C],
                          start=True, stop=True)
                te.matmul(ps_b[:, :], ones_row[:, :], rw[0:1, R_B:R_B + C],
                          start=True, stop=True).then_inc(st.h, 1)

        @blk.vector
        def _(v):
            p.entry_wait(v)
            for c in range(NCH):
                v.wait_ge(st.h, 2 * c + 1 + 1)
                v.tensor_scalar_mul(rw[0:1, R_M:R_M + C], ps_sx[:, :], 1.0 / D)
                v.tensor_scalar_mul(rw[0:1, R_E:R_E + C], ps_sxx[:, :], 1.0 / D)
                v.drain()
                v.tensor_tensor(rw[0:1, R_V:R_V + C], rw[0:1, R_M:R_M + C],
                                rw[0:1, R_M:R_M + C], ALU.mult)
                v.drain()
                v.tensor_tensor(rw[0:1, R_V:R_V + C], rw[0:1, R_E:R_E + C],
                                rw[0:1, R_V:R_V + C], ALU.subtract)
                v.drain()
                v.tensor_scalar_add(rw[0:1, R_V:R_V + C], rw[0:1, R_V:R_V + C],
                                    EPS).then_inc(sv.h, 1)
                v.wait_ge(ss.h, 2 * c + 2)
                v.reciprocal(rw[0:1, R_RS:R_RS + C], rw[0:1, R_SD:R_SD + C])
                v.drain()
                v.scalar_tensor_tensor(
                    rw[0:1, R_B:R_B + C], rw[0:1, R_M:R_M + C], -1.0,
                    rw[0:1, R_RS:R_RS + C], ALU.mult, ALU.mult).then_inc(sv.h, 1)
                v.wait_ge(st.h, 2 * c + 2 + 1)
                v.tensor_copy(a_b[:, :], ps_a[:, :])
                v.tensor_copy(b_b[:, :], ps_b[:, :])
                v.drain()
                if c >= 1:
                    v.wait_ge(so.h, c * SO)
                for i in range(DT):
                    v.tensor_tensor(tmp[:, :], xc[c % 2][i][:, :], a_b[:, :],
                                    ALU.mult)
                    v.drain()
                    last = v.tensor_tensor(out_bf[i][:, :], tmp[:, :], b_b[:, :],
                                           ALU.add)
                    if wantf32:
                        last = v.tensor_tensor(out_f[i][:, :], tmp[:, :],
                                               b_b[:, :], ALU.add)
                    v.drain()
                last.then_inc(sv.h, 1)

    p.pop_scope()
    return (sdone, 1)





def emit_l0_attn(p, xn_bf_hbm, x_hbm, kt_hbm, v_hbm, acc_hbm, tag="l0a", gate=None):
    p.begin_stage()
    p.push_scope()
    NP = 4096
    NT = NP // 128
    C = 256
    NCH = TT // C
    ST_C, SS_C, SV_C = NT + 2 + DT, 2 * NT + 1, 2 * NT + 2 + DT

    KT = [p.sbuf(f"{tag}_kt{i}", [128, NP], BF16) for i in range(DT)]
    V = [p.sbuf(f"{tag}_v{j}", [128, D], BF16) for j in range(NT)]
    stag = [p.sbuf(f"{tag}_stag{j}", [128, 2048], F32) for j in range(2)]
    witems = [(KT[i][:, h * 2048:(h + 1) * 2048],
               kt_hbm[i * 128:(i + 1) * 128, h * 2048:(h + 1) * 2048], 2048)
              for i in range(DT) for h in range(2)] +              [(V[j][:, :], v_hbm[j * 128:(j + 1) * 128, :], D)
              for j in range(NT)]
    NWI = len(witems)
    xnc = [p.sbuf(f"{tag}_xn{j}", [128, DT * C], BF16) for j in range(2)]
    xcc = p.sbuf(f"{tag}_xc", [128, DT * C], F32)
    accs = xcc
    aw = p.sbuf(f"{tag}_aw", [128, NT * C], BF16)
    sqb = [p.sbuf(f"{tag}_sq{j}", [128, C], BF16) for j in range(2)]
    gtmp = [p.sbuf(f"{tag}_gt{j}", [128, C], BF16) for j in range(2)]
    s3b = p.sbuf(f"{tag}_s3b", [128, C], F32)
    rw = p.sbuf(f"{tag}_rw", [1, 2 * C], F32)
    ones_col = p.sbuf(f"{tag}_ones", [128, 1], BF16)
    ones_row = p.sbuf(f"{tag}_onesr", [1, 128], F32)

    ps_aw = [p.psum(f"{tag}_paw{j}", [128, C]) for j in range(2)]
    ps_ss = p.psum(f"{tag}_pss", [1, C])
    ps_bc = p.psum(f"{tag}_pbc", [128, C])
    ps_o = [p.psum(f"{tag}_po{j}", [128, C]) for j in range(2)]

    sdw2 = [p.sem(), p.sem()]
    sxn2 = [p.sem(), p.sem()]
    sx, sw, ss, st, sv, so = (p.sem() for _ in range(6))
    sdone = p.sem()

    def fm(hbm, c):
        return hbm[:, c * C:(c + 1) * C].rearrange("(a p) t -> p a t", p=128)

    def fm_sb(tile):
        return tile[:, :].rearrange("p (a t) -> p a t", t=C)

    with p.block() as blk:
        @blk.sync
        def _(sync):
            p.emit_entry(sync, gate)
            for i, (dstt, src, w_) in enumerate(witems):
                if i >= 2:
                    sync.wait_ge(sw.h, i - 1)
                sync.dma_start(out=stag[i % 2][:, 0:w_], in_=src
                               ).then_inc(sdw2[i % 2].h, 16)
            for c in range(NCH):
                if c >= 2:
                    sync.wait_ge(st.h, 1 + (c - 1) * ST_C)
                sync.dma_start(out=fm_sb(xnc[c % 2]), in_=fm(xn_bf_hbm, c)
                               ).then_inc(sxn2[c % 2].h, 16)
                if c >= 1:
                    sync.wait_ge(sv.h, c * SV_C)
                    sync.wait_ge(so.h, c * 16)
                sync.dma_start(out=fm_sb(xcc), in_=fm(x_hbm, c)
                               ).then_inc(sx.h, 16)
                sync.wait_ge(sv.h, (c + 1) * SV_C)
                sync.dma_start(out=fm(acc_hbm, c), in_=fm_sb(accs)
                               ).then_inc(so.h, 16)
            sync.wait_ge(so.h, NCH * 16)
            sync.nop().then_inc(sdone.h, 1)

        @blk.tensor
        def _(te):
            p.entry_wait(te)
            te.wait_ge(st.h, 1)
            te.wait_ge(sw.h, NWI)
            for c in range(NCH):
                cs, cv, ct = c * SS_C, c * SV_C, 1 + c * ST_C
                te.wait_ge(sxn2[c % 2].h, (c // 2 + 1) * 16)
                for nt in range(NT):
                    if nt >= 2:
                        te.wait_ge(ss.h, cs + nt - 1)
                    elif c >= 1:
                        te.wait_ge(ss.h, (c - 1) * SS_C + NT)
                    for i in range(DT):
                        mm = te.matmul(ps_aw[nt % 2][:, :],
                                       KT[i][:, nt * 128:(nt + 1) * 128],
                                       xnc[c % 2][:, i * C:(i + 1) * C],
                                       start=(i == 0), stop=(i == DT - 1))
                    mm.then_inc(st.h, 1)
                    for rnt in ([nt - 1] if nt >= 1 else []) + ([NT - 1] if nt == NT - 1 else []):
                        if rnt == 0 and c >= 1:
                            te.wait_ge(ss.h, (c - 1) * SS_C + NT + 1)
                        te.wait_ge(sv.h, cv + rnt + 1)
                        mm = te.matmul(ps_ss[:, :], ones_col[:, :],
                                       sqb[rnt % 2][:, :],
                                       start=(rnt == 0), stop=(rnt == NT - 1),
                                       skip_group_check=True)
                        if rnt == NT - 1:
                            mm.then_inc(st.h, 1)
                te.wait_ge(sv.h, cv + NT + 1)
                te.matmul(ps_bc[:, :], ones_row[:, :], rw[0:1, C:2 * C],
                          start=True, stop=True).then_inc(st.h, 1)
                te.wait_ge(ss.h, cs + SS_C)
                for j in range(DT):
                    if j >= 2:
                        te.wait_ge(sv.h, cv + NT + 2 + NT + j - 1)
                    elif c >= 1:
                        te.wait_ge(sv.h, c * SV_C)
                    for nt in range(NT):
                        mm = te.matmul(ps_o[j % 2][:, :],
                                       V[nt][:, j * 128:(j + 1) * 128],
                                       aw[:, nt * C:(nt + 1) * C],
                                       start=(nt == 0), stop=(nt == NT - 1))
                    mm.then_inc(st.h, 1)

        @blk.scalar
        def _(sc):
            p.entry_wait(sc)
            for i, (dstt, src, w_) in enumerate(witems):
                sc.wait_ge(sdw2[i % 2].h, (i // 2 + 1) * 16)
                sc.copy(dstt, stag[i % 2][:, 0:w_]).then_inc(sw.h, 1)
            for c in range(NCH):
                cs, cv, ct = c * SS_C, c * SV_C, 1 + c * ST_C
                for nt in range(NT):
                    sc.wait_ge(st.h, ct + nt + 1)
                    sc.copy(aw[:, nt * C:(nt + 1) * C],
                            ps_aw[nt % 2][:, :]).then_inc(ss.h, 1)
                sc.wait_ge(st.h, ct + NT + 1)
                sc.activation(rw[0:1, 0:C], ps_ss[:, :], AF.Sqrt
                              ).then_inc(ss.h, 1)
                for nt in range(NT):
                    sc.wait_ge(sv.h, cv + NT + 2 + nt + 1)
                    sc.activation(aw[:, nt * C:(nt + 1) * C],
                                  gtmp[nt % 2][:, :], AF.Gelu
                                  ).then_inc(ss.h, 1)

        @blk.vector
        def _(v):
            p.entry_wait(v)
            v.memset(ones_col[:, :], 1.0)
            v.memset(ones_row[:, :], 1.0).then_inc(st.h, 1)
            for c in range(NCH):
                cs, cv, ct = c * SS_C, c * SV_C, 1 + c * ST_C
                for nt in range(NT):
                    v.wait_ge(ss.h, cs + nt + 1)
                    v.tensor_tensor(sqb[nt % 2][:, :],
                                    aw[:, nt * C:(nt + 1) * C],
                                    aw[:, nt * C:(nt + 1) * C], ALU.mult
                                    ).then_inc(sv.h, 1)
                v.wait_ge(ss.h, cs + NT + 1)
                v.reciprocal(rw[0:1, C:2 * C], rw[0:1, 0:C])
                v.drain()
                v.tensor_scalar_mul(rw[0:1, C:2 * C], rw[0:1, C:2 * C],
                                    64.0).then_inc(sv.h, 1)
                v.wait_ge(st.h, ct + NT + 2)
                v.tensor_copy(s3b[:, :], ps_bc[:, :]).then_inc(sv.h, 1)
                v.drain()
                for nt in range(NT):
                    if nt >= 2:
                        v.wait_ge(ss.h, cs + NT + 1 + nt - 1)
                    v.tensor_tensor(gtmp[nt % 2][:, :],
                                    aw[:, nt * C:(nt + 1) * C], s3b[:, :],
                                    ALU.mult).then_inc(sv.h, 1)
                if c >= 1:
                    v.wait_ge(so.h, c * 16)
                v.wait_ge(sx.h, (c + 1) * 16)
                for j in range(DT):
                    v.wait_ge(st.h, ct + NT + 2 + j + 1)
                    v.tensor_tensor(xcc[:, j * C:(j + 1) * C],
                                    xcc[:, j * C:(j + 1) * C],
                                    ps_o[j % 2][:, :], ALU.add
                                    ).then_inc(sv.h, 1)

    p.pop_scope()
    return (sdone, 1)



def emit_router(p, xn_bf_hbm, wr_hbm, br_hbm, wrows_hbm, tag="rt", gate=None):
    p.begin_stage()
    p.push_scope()
    C = 512
    NCH = TT // C
    KT4 = C // 128

    wrf = p.sbuf(f"{tag}_wrf", [128, DT * 8], F32)
    wrb = p.sbuf(f"{tag}_wrb", [128, DT * 8], BF16)
    brc = p.sbuf(f"{tag}_brc", [8, 1], F32)
    xnc = [p.sbuf(f"{tag}_xn{j}", [128, DT * C], BF16) for j in range(2)]
    lg = p.sbuf(f"{tag}_lg", [8, C], F32)
    wrow = p.sbuf(f"{tag}_wrow", [8, C], F32)
    idp = p.sbuf(f"{tag}_idp", [128, 128], I32)
    idf = p.sbuf(f"{tag}_idf", [128, 128], I32)
    iden = p.sbuf(f"{tag}_iden", [128, 128], F32)
    lt = [p.sbuf(f"{tag}_lt{j}", [128, 8], F32) for j in range(4)]
    small = [p.sbuf(f"{tag}_sm{j}", [128, 8 * 8], F32) for j in range(4)]

    ps_lg = p.psum(f"{tag}_plg", [8, C])
    ps_t = [p.psum(f"{tag}_pt{j}", [128, 8]) for j in range(2)]
    ps_w = [p.psum(f"{tag}_pw{j}", [8, 128]) for j in range(2)]

    sini = p.sem()
    sxn2 = [p.sem(), p.sem()]
    ss, st, sv, so, sg = (p.sem() for _ in range(5))
    sdone = p.sem()
    ST_C, SS_C, SV_C = 1 + 2 * KT4, 1 + KT4, 3 * KT4

    def fm(hbm, c):
        return hbm[:, c * C:(c + 1) * C].rearrange("(a p) t -> p a t", p=128)

    with p.block() as blk:
        @blk.sync
        def _(sync):
            p.emit_entry(sync, gate)
            sync.dma_start(out=wrf[:, :].rearrange("p (a e) -> p a e", e=8),
                           in_=wr_hbm.rearrange("(a p) e -> p a e", p=128)
                           ).then_inc(sini.h, 16)
            sync.dma_start(out=brc[:, :], in_=br_hbm).then_inc(sini.h, 16)
            for c in range(NCH):
                if c >= 2:
                    sync.wait_ge(st.h, 1 + (c - 1) * ST_C)
                sync.dma_start(out=xnc[c % 2][:, :].rearrange(
                    "p (a t) -> p a t", t=C), in_=fm(xn_bf_hbm, c)
                    ).then_inc(sxn2[c % 2].h, 16)
                sync.wait_ge(sv.h, 1 + (c + 1) * SV_C)
                sync.dma_start(out=wrows_hbm[:, c * C:(c + 1) * C],
                               in_=wrow[:, :]).then_inc(so.h, 16)
            sync.wait_ge(so.h, NCH * 16)
            sync.nop().then_inc(sdone.h, 1)

        @blk.gpsimd
        def _(g):
            p.entry_wait(g)
            g.iota(idp[:, :], [[0, 128]], channel_multiplier=1)
            g.iota(idf[:, :], [[1, 128]], channel_multiplier=0).then_inc(sg.h, 1)
            g.wait_ge(sini.h, 32)
            g.tensor_copy(wrb[:, :], wrf[:, :]).then_inc(st.h, 1)

        @blk.vector
        def _(v):
            p.entry_wait(v)
            v.wait_ge(sg.h, 1)
            v.tensor_tensor(iden[:, :], idp[:, :], idf[:, :], ALU.is_equal
                            ).then_inc(sv.h, 1)
            for c in range(NCH):
                cv, ct = 1 + c * SV_C, 1 + c * ST_C
                for kk in range(KT4):
                    s = small[kk]
                    l = lt[kk]
                    v.wait_ge(st.h, ct + 1 + kk + 1)
                    v.tensor_copy(l[:, :], ps_t[kk % 2][:, :])
                    v.drain()
                    v.reduce_max(s[:, 0:1], l[:, :], AX.X)
                    v.drain()
                    v.tensor_scalar_mul(s[:, 1:2], s[:, 0:1], -1.0)
                    v.tensor_scalar(s[:, 8:16], l[:, :], s[:, 0:1], None,
                                    ALU.is_ge)
                    v.drain()
                    v.scalar_tensor_tensor(s[:, 16:24], s[:, 8:16], -1e30,
                                           l[:, :], ALU.mult, ALU.add)
                    v.drain()
                    v.reduce_max(s[:, 2:3], s[:, 16:24], AX.X)
                    v.drain()
                    v.tensor_scalar(s[:, 24:32], l[:, :], s[:, 2:3], None,
                                    ALU.is_ge).then_inc(sv.h, 1)
                    v.wait_ge(ss.h, c * SS_C + 2 + kk)
                    v.tensor_scalar_add(s[:, 4:5], s[:, 3:4], 1.0)
                    v.drain()
                    v.reciprocal(s[:, 5:6], s[:, 4:5])
                    v.drain()
                    v.tensor_tensor(s[:, 40:48], s[:, 32:40], s[:, 24:32],
                                    ALU.mult)
                    v.drain()
                    v.tensor_scalar(s[:, 48:56], s[:, 40:48], s[:, 5:6], None,
                                    ALU.mult).then_inc(sv.h, 1)
                if c >= 1:
                    v.wait_ge(so.h, c * 16)
                for kk in range(KT4):
                    v.wait_ge(st.h, ct + 1 + KT4 + kk + 1)
                    v.tensor_copy(wrow[:, kk * 128:(kk + 1) * 128],
                                  ps_w[kk % 2][:, :]).then_inc(sv.h, 1)

        @blk.scalar
        def _(sc):
            p.entry_wait(sc)
            sc.wait_ge(sini.h, 32)
            for c in range(NCH):
                cv, ct = 1 + c * SV_C, 1 + c * ST_C
                sc.wait_ge(st.h, ct + 1)
                sc.activation(lg[:, :], ps_lg[:, :], AF.Identity, bias=brc[:, 0:1]
                              ).then_inc(ss.h, 1)
                for kk in range(KT4):
                    s = small[kk]
                    l = lt[kk]
                    sc.wait_ge(sv.h, cv + 2 * kk + 1)
                    sc.activation(s[:, 32:40], l[:, :], AF.Exp, bias=s[:, 1:2])
                    sc.activation(s[:, 3:4], s[:, 2:3], AF.Exp, bias=s[:, 1:2]
                                  ).then_inc(ss.h, 1)

        @blk.tensor
        def _(te):
            p.entry_wait(te)
            te.wait_ge(st.h, 1)
            te.wait_ge(sv.h, 1)
            for c in range(NCH):
                cv, cs, ct = 1 + c * SV_C, 1 + c * SS_C, 1 + c * ST_C
                te.wait_ge(sxn2[c % 2].h, (c // 2 + 1) * 16)
                if c >= 1:
                    te.wait_ge(ss.h, (c - 1) * SS_C + 1)
                for i in range(DT):
                    mm = te.matmul(ps_lg[:, :], wrb[:, i * 8:(i + 1) * 8],
                                   xnc[c % 2][:, i * C:(i + 1) * C],
                                   start=(i == 0), stop=(i == DT - 1))
                mm.then_inc(st.h, 1)
                te.wait_ge(ss.h, c * SS_C + 1)
                for kk in range(KT4):
                    if kk >= 2:
                        te.wait_ge(sv.h, cv + 2 * (kk - 2) + 1)
                    te.transpose(ps_t[kk % 2][:, :],
                                 lg[0:8, kk * 128:(kk + 1) * 128],
                                 iden[0:8, 0:8]).then_inc(st.h, 1)
                for kk in range(KT4):
                    te.wait_ge(sv.h, cv + 2 * kk + 2)
                    if kk >= 2:
                        te.wait_ge(sv.h, cv + 2 * KT4 + (kk - 2) + 1)
                    te.transpose(ps_w[kk % 2][:, :],
                                 small[kk][:, 48:56], iden[:, :]
                                 ).then_inc(st.h, 1)

    p.pop_scope()
    return (sdone, 1)



def emit_wprep(p, pairs, tag="wp", gate=None, blocks=None):
    es = ExitStack()
    sb = lambda n, sh, dt: es.enter_context(p.nc.sbuf_tensor(n, sh, dt))
    stag = [sb(f"{tag}_s{j}", [128, 4096], F32) for j in range(2)]
    stbf = [sb(f"{tag}_b{j}", [128, 4096], BF16) for j in range(2)]
    sd2 = [Sem(p._alloc_sem()), Sem(p._alloc_sem())]
    so2 = [Sem(p._alloc_sem()), Sem(p._alloc_sem())]
    done = Sem(p._alloc_sem())

    if blocks is None:
        blocks = []
        for src, dst in pairs:
            R, Cc = src.shape
            for r in range(R // 128):
                for c0 in range(0, Cc, 4096):
                    w_ = min(4096, Cc - c0)
                    blocks.append((src[r * 128:(r + 1) * 128, c0:c0 + w_],
                                   dst[r * 128:(r + 1) * 128, c0:c0 + w_],
                                   w_, None))
    NB = len(blocks)

    with p.block() as blk:
        @blk.sync
        def _(sync):
            p.emit_entry(sync, gate)
            for i, (src, dst, Cc, _l) in enumerate(blocks):
                if i >= 2:
                    sync.wait_ge(so2[i % 2].h, (i // 2) * 16)
                sync.dma_start(out=stag[i % 2][:, 0:Cc], in_=src
                               ).then_inc(sd2[i % 2].h, 16)
            sync.wait_ge(so2[0].h, ((NB + 1) // 2) * 16)
            sync.wait_ge(so2[1].h, (NB // 2) * 16)
            sync.nop().then_inc(done.h, 1)

        @blk.gpsimd
        def _(g):
            p.entry_wait(g)
            for i, (src, dst, Cc, l_) in enumerate(blocks):
                g.wait_ge(sd2[i % 2].h, (i // 2 + 1) * 16)
                if i >= 2:
                    g.wait_ge(so2[i % 2].h, (i // 2) * 16)
                g.tensor_copy(stbf[i % 2][:, 0:Cc], stag[i % 2][:, 0:Cc])
                g.drain()
                src_sb = stbf[i % 2][:, 0:Cc]
                if l_ is not None:
                    src_sb = src_sb.rearrange("p (g l) -> p g l", l=l_)
                g.dma_start(out=dst, in_=src_sb
                            ).then_inc(so2[i % 2].h, 16)

    es.close()
    return done, so2



def emit_moewp(p, w1f_hbm, w2f_hbm, w1bf, w2bf, stag, stbf, tag="mwp",
               experts=range(8), prior=None):
    spd2 = [Sem(p._alloc_sem()), Sem(p._alloc_sem())]
    spo2 = [Sem(p._alloc_sem()), Sem(p._alloc_sem())]
    DFF = 4096
    blocks = []
    for e in experts:
        for r in range(DT):
            row = e * D + r * 128
            for cc in (0, 1024, 2048, 3072):
                blocks.append((w1f_hbm[row:row + 128, cc:cc + 1024],
                               w1bf[row:row + 128, cc:cc + 1024], 1024))
        for r in range(DFF // 128):
            row = e * DFF + r * 128
            blocks.append((w2f_hbm[row:row + 128, :],
                           w2bf[row:row + 128, :], D))
    assert len(blocks) == 64 * len(list(experts))

    with p.block() as blk:
        @blk.gpsimd
        def _(g):
            for sem, tgt in (prior or []):
                g.wait_ge(sem.h, tgt)
            g.dma_start(out=stag[0][:, 0:blocks[0][2]], in_=blocks[0][0]
                        ).then_inc(spd2[0].h, 16)
            g.dma_start(out=stag[1][:, 0:blocks[1][2]], in_=blocks[1][0]
                        ).then_inc(spd2[1].h, 16)
            for i, (src, dst, w_) in enumerate(blocks):
                g.wait_ge(spd2[i % 2].h, (i // 2 + 1) * 16)
                if i >= 2:
                    g.wait_ge(spo2[i % 2].h, (i // 2) * 16)
                g.tensor_copy(stbf[i % 2][:, 0:w_], stag[i % 2][:, 0:w_])
                g.drain()
                g.dma_start(out=dst, in_=stbf[i % 2][:, 0:w_]
                            ).then_inc(spo2[i % 2].h, 16)
                if i + 2 < len(blocks):
                    nsrc, ndst, nw = blocks[i + 2]
                    g.dma_start(out=stag[i % 2][:, 0:nw], in_=nsrc
                                ).then_inc(spd2[i % 2].h, 16)
    return spo2



def emit_moe(p, xn_bf_hbm, wrows_hbm, prep, b1c_hbm, b2c_hbm,
             acc_in_hbm, x1_hbm, tag="moe", gate=None):
    p.begin_stage()
    p.push_scope()
    E = 8
    DFF = 4096
    FT = DFF // 128
    C = 512
    NTC = TT // C
    ST_C, SS_C, SV_C = FT + DT, FT, DT

    y_scr = p.dram(f"{tag}_yscr", [D, TT], F32, kind="Internal")

    sini, sxn, sacc, swb = (p.sem() for _ in range(4))
    sw, ss, st, sv, so = (p.sem() for _ in range(5))
    sdone = p.sem()
    w1bf, w2bf, wpso2 = prep

    WA = [p.sbuf(f"{tag}_wa{i}", [128, DFF], BF16) for i in range(DT)]
    WB = [p.sbuf(f"{tag}_wb{i}", [128, D], BF16) for i in range(FT)]
    hsb = p.sbuf(f"{tag}_h", [128, FT * C], BF16)
    xnc = p.sbuf(f"{tag}_xn", [128, DT * C], BF16)
    acc = p.sbuf(f"{tag}_acc", [128, DT * C], F32)
    wbc = p.sbuf(f"{tag}_wbc", [128, C], F32)
    tmp = p.sbuf(f"{tag}_tmp", [128, C], F32)
    b1c = p.sbuf(f"{tag}_b1c", [128, E * FT], F32)
    b2c = p.sbuf(f"{tag}_b2c", [128, E * DT], F32)

    ps_h = [p.psum(f"{tag}_ph{j}", [128, C]) for j in range(2)]
    ps_y = [p.psum(f"{tag}_py{j}", [128, C]) for j in range(2)]


    def fm(hbm, c):
        return hbm[:, c * C:(c + 1) * C].rearrange("(a p) t -> p a t", p=128)

    def sb3(tile, w=C):
        return tile[:, :].rearrange("p (a t) -> p a t", t=w)

    with p.block() as blk:
        @blk.sync
        def _(sync):
            p.emit_entry(sync, gate)
            sync.dma_start(out=b1c[:, :], in_=b1c_hbm).then_inc(sini.h, 16)
            sync.dma_start(out=b2c[:, :], in_=b2c_hbm).then_inc(sini.h, 16)
            for e in range(E):
                for _sp, _fn in wpso2:
                    _t = _fn(e)
                    if _t > 0:
                        sync.wait_ge(_sp[0].h, _t)
                        sync.wait_ge(_sp[1].h, _t)
                if e >= 1:
                    sync.wait_ge(st.h, e * NTC * ST_C)
                for i in range(DT):
                    sync.dma_start(
                        out=WA[i][:, :],
                        in_=w1bf[(e * DT + i) * 128:(e * DT + i + 1) * 128, :]
                        ).then_inc(sw.h, 16)
                for f in range(FT):
                    sync.dma_start(
                        out=WB[f][:, :],
                        in_=w2bf[(e * FT + f) * 128:(e * FT + f + 1) * 128, :]
                        ).then_inc(sw.h, 16)
                for tc in range(NTC):
                    gi = e * NTC + tc
                    if gi >= 1:
                        sync.wait_ge(st.h, (gi - 1) * ST_C + FT)
                    sync.dma_start(out=sb3(xnc), in_=fm(xn_bf_hbm, tc)
                                   ).then_inc(sxn.h, 16)
                    if gi >= 1:
                        sync.wait_ge(so.h, gi * 16)
                    src = acc_in_hbm if e == 0 else y_scr
                    sync.dma_start(out=sb3(acc), in_=fm(src, tc)
                                   ).then_inc(sacc.h, 16)
                    if gi >= 1:
                        sync.wait_ge(sv.h, gi * SV_C)
                    sync.dma_start(
                        out=wbc[:, :],
                        in_=wrows_hbm[e:e + 1, tc * C:(tc + 1) * C]
                        .partition_broadcast(128)).then_inc(swb.h, 16)
                    dst = x1_hbm if e == E - 1 else y_scr
                    sync.wait_ge(sv.h, (gi + 1) * SV_C)
                    sync.dma_start(out=fm(dst, tc), in_=sb3(acc)
                                   ).then_inc(so.h, 16)
            sync.wait_ge(so.h, E * NTC * 16)
            sync.nop().then_inc(sdone.h, 1)

        @blk.tensor
        def _(te):
            p.entry_wait(te)
            for e in range(E):
                te.wait_ge(sw.h, (e + 1) * 640)
                for tc in range(NTC):
                    gi = e * NTC + tc
                    ct = gi * ST_C
                    te.wait_ge(sxn.h, (gi + 1) * 16)
                    for f in range(FT):
                        if f >= 2:
                            te.wait_ge(ss.h, gi * SS_C + f - 1)
                        elif gi >= 1:
                            te.wait_ge(ss.h, gi * SS_C)
                        for i in range(DT):
                            mm = te.matmul(ps_h[f % 2][:, :],
                                           WA[i][:, f * 128:(f + 1) * 128],
                                           xnc[:, i * C:(i + 1) * C],
                                           start=(i == 0), stop=(i == DT - 1))
                        mm.then_inc(st.h, 1)
                    te.wait_ge(ss.h, (gi + 1) * SS_C)
                    for dt in range(DT):
                        if dt >= 2:
                            te.wait_ge(sv.h, gi * SV_C + dt - 1)
                        elif gi >= 1:
                            te.wait_ge(sv.h, gi * SV_C)
                        for f in range(FT):
                            mm = te.matmul(ps_y[dt % 2][:, :],
                                           WB[f][:, dt * 128:(dt + 1) * 128],
                                           hsb[:, f * C:(f + 1) * C],
                                           start=(f == 0), stop=(f == FT - 1))
                        mm.then_inc(st.h, 1)

        @blk.scalar
        def _(sc):
            p.entry_wait(sc)
            sc.wait_ge(sini.h, 32)
            for e in range(E):
                for tc in range(NTC):
                    gi = e * NTC + tc
                    ct = gi * ST_C
                    for f in range(FT):
                        sc.wait_ge(st.h, ct + f + 1)
                        sc.activation(hsb[:, f * C:(f + 1) * C],
                                      ps_h[f % 2][:, :], AF.Relu,
                                      bias=b1c[:, e * FT + f:e * FT + f + 1]
                                      ).then_inc(ss.h, 1)

        @blk.vector
        def _(v):
            p.entry_wait(v)
            v.wait_ge(sini.h, 32)
            for e in range(E):
                for tc in range(NTC):
                    gi = e * NTC + tc
                    ct = gi * ST_C
                    v.wait_ge(sacc.h, (gi + 1) * 16)
                    v.wait_ge(swb.h, (gi + 1) * 16)
                    for dt in range(DT):
                        v.wait_ge(st.h, ct + FT + dt + 1)
                        v.scalar_tensor_tensor(
                            tmp[:, :], ps_y[dt % 2][:, :],
                            b2c[:, e * DT + dt:e * DT + dt + 1], wbc[:, :],
                            ALU.add, ALU.mult)
                        v.drain()
                        v.tensor_tensor(acc[:, dt * C:(dt + 1) * C],
                                        acc[:, dt * C:(dt + 1) * C],
                                        tmp[:, :], ALU.add)
                        v.drain()
                        v.engine_nop().then_inc(sv.h, 1)

    p.pop_scope()
    return (sdone, 1)



def emit_l1_attn(p, xn1f_hbm, x1_hbm, wkT_hbm, wvT_hbm, wrT_hbm, woT_hbm,
                 params_hbm, acc1_hbm, tag="l1a", gate=None):
    p.begin_stage()
    p.push_scope()
    C = 512
    NCH = getattr(p, "l1a_nch", TT // C)
    PSEQ = T // C
    ST_C, SS_C, SV_C = 3 * DT + DT, DT, 1 + DT + DT

    wbfs = {}
    for nm in ("wk", "wv", "wr", "wo"):
        wbfs[nm] = [p.sbuf(f"{tag}_{nm}{i}", [128, D], BF16) for i in range(DT)]
    xnf = p.sbuf(f"{tag}_xnf", [128, DT * (C + 1)], F32)
    kin = p.sbuf(f"{tag}_kin", [128, DT * C], BF16)
    vin = p.sbuf(f"{tag}_vin", [128, DT * C], BF16)
    rin = p.sbuf(f"{tag}_rin", [128, DT * C], BF16)
    dif = p.sbuf(f"{tag}_dif", [128, C], F32)
    ekb = [p.sbuf(f"{tag}_ek{j}", [128, C], F32) for j in range(2)]
    vvb = [p.sbuf(f"{tag}_vv{j}", [128, C], F32) for j in range(2)]
    rsb = [p.sbuf(f"{tag}_rs{j}", [128, C], BF16) for j in range(2)]
    ekvb = p.sbuf(f"{tag}_ekv", [128, C], F32)
    numb = p.sbuf(f"{tag}_num", [128, DT * (C + 1)], F32)
    denb = p.sbuf(f"{tag}_den", [128, DT * (C + 1)], F32)
    numer = p.sbuf(f"{tag}_numer", [128, C], F32)
    denom = p.sbuf(f"{tag}_denom", [128, C], F32)
    wkvb = p.sbuf(f"{tag}_wkv", [128, C], F32)
    rcpb = p.sbuf(f"{tag}_rcp", [128, C], F32)
    rwkv = p.sbuf(f"{tag}_rwkv", [128, DT * C], BF16)
    x1c = p.sbuf(f"{tag}_x1c", [128, DT * C], F32)
    prm = p.sbuf(f"{tag}_prm", [128, 48], F32)
    drv = p.sbuf(f"{tag}_drv", [128, 16], F32)

    ps_k = [p.psum(f"{tag}_pk{j}", [128, C]) for j in range(2)]
    ps_v = [p.psum(f"{tag}_pv{j}", [128, C]) for j in range(2)]
    ps_r = [p.psum(f"{tag}_pr{j}", [128, C]) for j in range(2)]
    ps_o = [p.psum(f"{tag}_po{j}", [128, C]) for j in range(2)]

    sini, sxnf, sx1 = (p.sem() for _ in range(3))
    sdw2 = [p.sem(), p.sem()]
    sw, ss, st, sv, so = (p.sem() for _ in range(5))
    sdone = p.sem()

    def xn3(c0, w):
        return xn1f_hbm[:, c0:c0 + w].rearrange("(a p) t -> p a t", p=128)

    def fm(hbm, c):
        return hbm[:, c * C:(c + 1) * C].rearrange("(a p) t -> p a t", p=128)

    def sb3(tile, w=C):
        return tile[:, :].rearrange("p (a t) -> p a t", t=w)

    es = ExitStack()
    stag = [es.enter_context(p.nc.sbuf_tensor(f"{tag}_st{j}", [128, D], F32))
            for j in range(2)]
    wl = [(wbfs[nm][i],
           {"wk": wkT_hbm, "wv": wvT_hbm, "wr": wrT_hbm, "wo": woT_hbm}[nm]
           [i * 128:(i + 1) * 128, :])
          for nm in ("wk", "wv", "wr", "wo") for i in range(DT)]
    NW = len(wl)

    with p.block() as blk:
        @blk.sync
        def _(sync):
            p.emit_entry(sync, gate)
            sync.dma_start(out=prm[:, :], in_=params_hbm).then_inc(sini.h, 16)
            for i, (dstt, src) in enumerate(wl):
                if i >= 2:
                    sync.wait_ge(sw.h, i - 1)
                sync.dma_start(out=stag[i % 2][:, :], in_=src
                               ).then_inc(sdw2[i % 2].h, 16)
            for c in range(NCH):
                seq, tcs = c // PSEQ, c % PSEQ
                c0 = c * C
                if c >= 1:
                    sync.wait_ge(sv.h, (c - 1) * SV_C + 1)
                if tcs == 0:
                    sync.dma_start(out=sb3(xnf, C + 1)[:, :, 1:C + 1],
                                   in_=xn3(c0, C)).then_inc(sxnf.h, 16)
                else:
                    sync.dma_start(out=sb3(xnf, C + 1),
                                   in_=xn3(c0 - 1, C + 1)).then_inc(sxnf.h, 16)
                if c >= 1:
                    sync.wait_ge(so.h, c * 16)
                sync.dma_start(out=sb3(x1c), in_=fm(x1_hbm, c)
                               ).then_inc(sx1.h, 16)
                sync.wait_ge(sv.h, (c + 1) * SV_C)
                sync.dma_start(out=fm(acc1_hbm, c), in_=sb3(x1c)
                               ).then_inc(so.h, 16)
            sync.wait_ge(so.h, NCH * 16)
            sync.nop().then_inc(sdone.h, 1)

        @blk.gpsimd
        def _(g):
            p.entry_wait(g)
            for i in range(NW):
                g.wait_ge(sdw2[i % 2].h, (i // 2 + 1) * 16)
                g.tensor_copy(wl[i][0][:, :], stag[i % 2][:, :]
                              ).then_inc(sw.h, 1)

        @blk.scalar
        def _(sc):
            p.entry_wait(sc)
            sc.wait_ge(sini.h, 16)
            sc.activation(drv[:, 0:8], prm[:, 24:32], AF.Exp)
            sc.drain()
            sc.activation(drv[:, 0:8], drv[:, 0:8], AF.Exp, scale=-1.0)
            sc.activation(drv[:, 8:16], prm[:, 32:40], AF.Exp).then_inc(ss.h, 1)
            SB = 1
            for c in range(NCH):
                cs, ct = SB + c * SS_C, c * ST_C
                for a in range(DT):
                    sc.wait_ge(st.h, ct + 3 * a + 3)
                    if a >= 2:
                        sc.wait_ge(sv.h, 1 + c * SV_C + 1 + (a - 2) + 1)
                    elif c >= 1:
                        sc.wait_ge(sv.h, 1 + (c - 1) * SV_C + 1 + DT)
                    sc.activation(ekb[a % 2][:, :], ps_k[a % 2][:, :], AF.Exp)
                    sc.copy(vvb[a % 2][:, :], ps_v[a % 2][:, :])
                    sc.activation(rsb[a % 2][:, :], ps_r[a % 2][:, :],
                                  AF.Sigmoid).then_inc(ss.h, 1)

        @blk.tensor
        def _(te):
            p.entry_wait(te)
            te.wait_ge(sw.h, NW)
            for c in range(NCH):
                cv, cs, ct = 1 + c * SV_C, 1 + c * SS_C, c * ST_C
                te.wait_ge(sv.h, cv)
                for a in range(DT):
                    if a >= 2:
                        te.wait_ge(ss.h, cs + a - 1)
                    elif c >= 1:
                        te.wait_ge(ss.h, 1 + (c - 1) * SS_C + DT)
                    for i in range(DT):
                        mm = te.matmul(ps_k[a % 2][:, :],
                                       wbfs["wk"][i][:, a * 128:(a + 1) * 128],
                                       kin[:, i * C:(i + 1) * C],
                                       start=(i == 0), stop=(i == DT - 1))
                    mm.then_inc(st.h, 1)
                    for i in range(DT):
                        mm = te.matmul(ps_v[a % 2][:, :],
                                       wbfs["wv"][i][:, a * 128:(a + 1) * 128],
                                       vin[:, i * C:(i + 1) * C],
                                       start=(i == 0), stop=(i == DT - 1))
                    mm.then_inc(st.h, 1)
                    for i in range(DT):
                        mm = te.matmul(ps_r[a % 2][:, :],
                                       wbfs["wr"][i][:, a * 128:(a + 1) * 128],
                                       rin[:, i * C:(i + 1) * C],
                                       start=(i == 0), stop=(i == DT - 1))
                    mm.then_inc(st.h, 1)
                te.wait_ge(sv.h, cv + DT)
                for m in range(DT):
                    if m >= 2:
                        te.wait_ge(sv.h, cv + 7 + m)
                    elif c >= 1:
                        te.wait_ge(sv.h, c * SV_C)
                    for i in range(DT):
                        mm = te.matmul(ps_o[m % 2][:, :],
                                       wbfs["wo"][i][:, m * 128:(m + 1) * 128],
                                       rwkv[:, i * C:(i + 1) * C],
                                       start=(i == 0), stop=(i == DT - 1))
                    mm.then_inc(st.h, 1)

        @blk.vector
        def _(v):
            p.entry_wait(v)
            v.wait_ge(ss.h, 1)
            for c in range(NCH):
                seq, tcs = c // PSEQ, c % PSEQ
                cv, cs, ct = 1 + c * SV_C, 1 + c * SS_C, c * ST_C
                v.drain()
                for a in range(DT):
                    if tcs == 0:
                        v.memset(numb[:, a * (C + 1):a * (C + 1) + 1], 0.0)
                        v.memset(denb[:, a * (C + 1):a * (C + 1) + 1], 0.0)
                    else:
                        v.tensor_copy(numb[:, a * (C + 1):a * (C + 1) + 1],
                                      numb[:, a * (C + 1) + C:a * (C + 1) + C + 1])
                        v.tensor_copy(denb[:, a * (C + 1):a * (C + 1) + 1],
                                      denb[:, a * (C + 1) + C:a * (C + 1) + C + 1])
                v.drain()
                v.wait_ge(sxnf.h, (c + 1) * 16)
                if c >= 1:
                    v.wait_ge(st.h, (c - 1) * ST_C + 3 * DT)
                if tcs == 0:
                    for a in range(DT):
                        v.memset(xnf[:, a * (C + 1):a * (C + 1) + 1], 0.0)
                    v.drain()
                for a in range(DT):
                    cur = xnf[:, a * (C + 1) + 1:a * (C + 1) + C + 1]
                    sh = xnf[:, a * (C + 1):a * (C + 1) + C]
                    v.tensor_tensor(dif[:, :], cur, sh, ALU.subtract)
                    v.drain()
                    v.scalar_tensor_tensor(kin[:, a * C:(a + 1) * C], dif[:, :],
                                           prm[:, a:a + 1], sh,
                                           ALU.mult, ALU.add)
                    v.scalar_tensor_tensor(vin[:, a * C:(a + 1) * C], dif[:, :],
                                           prm[:, 8 + a:9 + a], sh,
                                           ALU.mult, ALU.add)
                    v.scalar_tensor_tensor(rin[:, a * C:(a + 1) * C], dif[:, :],
                                           prm[:, 16 + a:17 + a], sh,
                                           ALU.mult, ALU.add)
                    v.drain()
                v.engine_nop().then_inc(sv.h, 1)
                for a in range(DT):
                    o = a * (C + 1)
                    v.wait_ge(ss.h, cs + a + 1)
                    v.tensor_tensor(ekvb[:, :], ekb[a % 2][:, :],
                                    vvb[a % 2][:, :], ALU.mult)
                    v.drain()
                    v.tensor_tensor_scan(
                        numb[:, o + 1:o + C + 1],
                        drv[:, a:a + 1].broadcast_to([128, C]), ekvb[:, :],
                        numb[:, o:o + 1], ALU.mult, ALU.add)
                    v.tensor_tensor_scan(
                        denb[:, o + 1:o + C + 1],
                        drv[:, a:a + 1].broadcast_to([128, C]),
                        ekb[a % 2][:, :],
                        denb[:, o:o + 1], ALU.mult, ALU.add)
                    v.drain()
                    v.scalar_tensor_tensor(numer[:, :], ekvb[:, :],
                                           drv[:, 8 + a:9 + a],
                                           numb[:, o:o + C],
                                           ALU.mult, ALU.add)
                    v.scalar_tensor_tensor(denom[:, :], ekb[a % 2][:, :],
                                           drv[:, 8 + a:9 + a],
                                           denb[:, o:o + C],
                                           ALU.mult, ALU.add)
                    v.drain()
                    v.reciprocal(rcpb[:, :], denom[:, :])
                    v.drain()
                    v.tensor_tensor(wkvb[:, :], numer[:, :], rcpb[:, :],
                                    ALU.mult)
                    v.drain()
                    v.tensor_tensor(rwkv[:, a * C:(a + 1) * C], wkvb[:, :],
                                    rsb[a % 2][:, :], ALU.mult
                                    ).then_inc(sv.h, 1)
                v.wait_ge(sx1.h, (c + 1) * 16)
                for m in range(DT):
                    v.wait_ge(st.h, ct + 3 * DT + m + 1)
                    v.scalar_tensor_tensor(x1c[:, m * C:(m + 1) * C],
                                           ps_o[m % 2][:, :],
                                           prm[:, 40 + m:41 + m],
                                           x1c[:, m * C:(m + 1) * C],
                                           ALU.add, ALU.add
                                           ).then_inc(sv.h, 1)

    es.close()
    p.pop_scope()
    return (sdone, 1)



def emit_lff_tables(p, qs_col, ks_col, Aq, Bq, Ak, Bk, Rbf, masks, sg, tag,
                    wait=None, gate=None):
    p.push_scope()
    scr = p.sbuf(f"{tag}_scr", [128, 8 * T], F32)
    col = p.sbuf(f"{tag}_col", [128, 8], F32)
    i128 = p.sbuf(f"{tag}_i128", [128, 1024], F32)
    S_T0, S_TH, S_SIN, S_COS, S_SC, S_T1, S_T2, S_FR = (i * T for i in range(8))
    TWO_PI = float(2.0 * math.pi)

    with p.block() as blk:
        @blk.gpsimd
        def _(g):
            p.entry_wait(g)
            g.iota(scr[:, S_T0:S_T0 + T], [[1, T]], channel_multiplier=0,
                   allow_small_or_imprecise_dtypes=True)
            g.iota(col[:, 0:1], [[0, 1]], channel_multiplier=1,
                   allow_small_or_imprecise_dtypes=True)
            g.iota(i128[:, 0:1024], [[1, 1024]], channel_multiplier=0,
                   allow_small_or_imprecise_dtypes=True)
            g.engine_nop().then_inc(sg.h, 1)

        @blk.vector
        def _(v):
            p.entry_wait(v)
            v.wait_ge(sg.h, 1)
            v.tensor_single_scalar(col[:, 1:2], col[:, 0:1], 64.0, ALU.is_ge)
            v.drain()
            v.scalar_tensor_tensor(col[:, 1:2], col[:, 1:2], -64.0,
                                   col[:, 0:1], ALU.mult, ALU.add)
            v.drain()
            v.tensor_single_scalar(col[:, 2:3], col[:, 1:2], 32.0, ALU.is_ge)
            v.drain()
            v.scalar_tensor_tensor(col[:, 2:3], col[:, 2:3], -32.0,
                                   col[:, 1:2], ALU.mult, ALU.add)
            v.drain()
            v.tensor_scalar_mul(col[:, 2:3], col[:, 2:3], 2.0)
            v.drain()
            v.tensor_scalar_mul(col[:, 3:4], col[:, 2:3],
                                -math.log(10000.0) / 64.0)
            v.tensor_scalar(col[:, 4:5], col[:, 2:3], 1.0 / 89.6, 25.6 / 89.6,
                            ALU.mult, ALU.add)
            v.tensor_single_scalar(col[:, 5:6], col[:, 1:2], 32.0, ALU.is_ge)
            v.drain()
            v.tensor_scalar(col[:, 5:6], col[:, 5:6], 2.0, -1.0,
                            ALU.mult, ALU.add)
            v.drain()
            v.engine_nop().then_inc(sg.h, 1)
            v.wait_ge(sg.h, 3)
            v.tensor_copy(scr[:, S_SC:S_SC + T], scr[:, S_T0:S_T0 + T])
            v.tensor_scalar(scr[:, S_FR:S_FR + T], scr[:, S_T0:S_T0 + T],
                            col[:, 3:4], None, ALU.mult)
            v.drain()
            v.tensor_scalar_mul(scr[:, S_T1:S_T1 + T], scr[:, S_FR:S_FR + T],
                                1.0 / TWO_PI)
            v.drain()
            v.engine_nop().then_inc(sg.h, 1)
            v.wait_ge(sg.h, 5)
            v.tensor_tensor(scr[:, S_T1:S_T1 + T], scr[:, S_T1:S_T1 + T],
                            scr[:, S_T2:S_T2 + T], ALU.subtract)
            v.drain()
            v.tensor_scalar_mul(scr[:, S_TH:S_TH + T], scr[:, S_T1:S_T1 + T],
                                TWO_PI)
            v.drain()
            v.tensor_single_scalar(scr[:, S_T1:S_T1 + T],
                                   scr[:, S_TH:S_TH + T], math.pi, ALU.is_ge)
            v.drain()
            v.scalar_tensor_tensor(scr[:, S_TH:S_TH + T],
                                   scr[:, S_T1:S_T1 + T], -TWO_PI,
                                   scr[:, S_TH:S_TH + T], ALU.mult, ALU.add)
            v.drain()
            v.tensor_scalar(scr[:, S_T1:S_T1 + T], scr[:, S_TH:S_TH + T],
                            -1.0, math.pi / 2.0, ALU.mult, ALU.add)
            v.drain()
            v.tensor_single_scalar(scr[:, S_T2:S_T2 + T],
                                   scr[:, S_T1:S_T1 + T], math.pi, ALU.is_ge)
            v.drain()
            v.scalar_tensor_tensor(scr[:, S_T1:S_T1 + T],
                                   scr[:, S_T2:S_T2 + T], -TWO_PI,
                                   scr[:, S_T1:S_T1 + T], ALU.mult, ALU.add)
            v.drain()
            v.tensor_scalar(scr[:, S_SC:S_SC + T], scr[:, S_SC:S_SC + T],
                            1.0 / 256.0, -4.0, ALU.mult, ALU.add)
            v.drain()
            v.tensor_scalar(scr[:, S_SC:S_SC + T], scr[:, S_SC:S_SC + T],
                            col[:, 4:5], None, ALU.mult)
            v.drain()
            v.engine_nop().then_inc(sg.h, 1)
            v.wait_ge(sg.h, 7)
            if wait is not None:
                v.wait_ge(wait[0].h, wait[1])
            v.tensor_tensor(scr[:, S_T1:S_T1 + T], scr[:, S_COS:S_COS + T],
                            scr[:, S_SC:S_SC + T], ALU.mult)
            v.tensor_tensor(scr[:, S_T2:S_T2 + T], scr[:, S_SIN:S_SIN + T],
                            scr[:, S_SC:S_SC + T], ALU.mult)
            v.drain()
            v.tensor_scalar(Aq[:, :], scr[:, S_T1:S_T1 + T], qs_col[:, 0:1],
                            None, ALU.mult)
            v.tensor_scalar(Bq[:, :], scr[:, S_T2:S_T2 + T],
                            qs_col[:, 1:2], None, ALU.mult)
            v.drain()
            v.reciprocal(scr[:, S_T1:S_T1 + T], scr[:, S_SC:S_SC + T])
            v.drain()
            v.tensor_tensor(scr[:, S_T2:S_T2 + T], scr[:, S_COS:S_COS + T],
                            scr[:, S_T1:S_T1 + T], ALU.mult)
            v.drain()
            v.tensor_scalar(Ak[:, :], scr[:, S_T2:S_T2 + T], ks_col[:, 0:1],
                            None, ALU.mult)
            v.drain()
            v.tensor_tensor(scr[:, S_T2:S_T2 + T], scr[:, S_SIN:S_SIN + T],
                            scr[:, S_T1:S_T1 + T], ALU.mult)
            v.drain()
            v.tensor_scalar(Bk[:, :], scr[:, S_T2:S_T2 + T],
                            ks_col[:, 1:2], None, ALU.mult)
            v.drain()
            v.tensor_single_scalar(col[:, 7:8], col[:, 1:2], 32.0, ALU.is_ge)
            v.drain()
            v.tensor_scalar(col[:, 6:7], col[:, 7:8], -2.0, 1.0,
                            ALU.mult, ALU.add)
            v.tensor_scalar(col[:, 5:6], col[:, 7:8], -1.0, 1.0,
                            ALU.mult, ALU.add)
            v.tensor_scalar(scr[:, S_T1:S_T1 + 128], i128[:, 0:128],
                            col[:, 0:1], None, ALU.subtract)
            v.drain()
            v.tensor_single_scalar(scr[:, S_T2:S_T2 + 128],
                                   scr[:, S_T1:S_T1 + 128], 32.0,
                                   ALU.is_equal)
            v.tensor_single_scalar(scr[:, S_T2 + 128:S_T2 + 256],
                                   scr[:, S_T1:S_T1 + 128], -32.0,
                                   ALU.is_equal)
            v.drain()
            v.tensor_scalar(scr[:, S_T2:S_T2 + 128], scr[:, S_T2:S_T2 + 128],
                            col[:, 5:6], None, ALU.mult)
            v.tensor_scalar(scr[:, S_T2 + 128:S_T2 + 256],
                            scr[:, S_T2 + 128:S_T2 + 256],
                            col[:, 7:8], None, ALU.mult)
            v.drain()
            v.tensor_tensor(scr[:, S_T1:S_T1 + 128], scr[:, S_T2:S_T2 + 128],
                            scr[:, S_T2 + 128:S_T2 + 256], ALU.add)
            v.drain()
            v.tensor_scalar(Rbf[:, :], scr[:, S_T1:S_T1 + 128], col[:, 6:7],
                            None, ALU.mult)
            v.tensor_scalar(i128[:, 0:1024], i128[:, 0:1024], col[:, 0:1],
                            None, ALU.subtract)
            v.drain()
            for a in range(4):
                v.tensor_single_scalar(scr[:, S_T1:S_T1 + 1024],
                                       i128[:, 0:1024], 128.0 * a,
                                       ALU.subtract)
                v.drain()
                v.tensor_single_scalar(scr[:, S_T2:S_T2 + 1024],
                                       scr[:, S_T1:S_T1 + 1024], 0.0,
                                       ALU.is_ge)
                v.tensor_single_scalar(scr[:, S_T1:S_T1 + 1024],
                                       scr[:, S_T1:S_T1 + 1024], 512.0,
                                       ALU.is_le)
                v.drain()
                v.tensor_tensor(scr[:, S_T2:S_T2 + 1024],
                                scr[:, S_T2:S_T2 + 1024],
                                scr[:, S_T1:S_T1 + 1024], ALU.mult)
                v.drain()
                v.tensor_scalar(masks[a][:, :], scr[:, S_T2:S_T2 + 1024],
                                1e30, -1e30, ALU.mult, ALU.add)
                v.drain()
            v.engine_nop().then_inc(sg.h, 1)

        @blk.scalar
        def _(sc):
            p.entry_wait(sc)
            sc.wait_ge(sg.h, 2)
            sc.activation(col[:, 3:4], col[:, 3:4], AF.Exp)
            sc.activation(col[:, 4:5], col[:, 4:5], AF.Ln)
            sc.drain()
            sc.nop().then_inc(sg.h, 1)
            sc.wait_ge(sg.h, 4)
            sc.copy(scr[:, S_T2:S_T2 + T].bitcast(I32), scr[:, S_T1:S_T1 + T])
            sc.drain()
            sc.copy(scr[:, S_T2:S_T2 + T],
                    scr[:, S_T2:S_T2 + T].bitcast(I32))
            sc.drain()
            sc.nop().then_inc(sg.h, 1)
            sc.wait_ge(sg.h, 6)
            sc.activation(scr[:, S_SIN:S_SIN + T], scr[:, S_TH:S_TH + T],
                          AF.Sin)
            sc.activation(scr[:, S_COS:S_COS + T], scr[:, S_T1:S_T1 + T],
                          AF.Sin)
            sc.activation(scr[:, S_SC:S_SC + T], scr[:, S_SC:S_SC + T],
                          AF.Exp)
            sc.drain()
            sc.nop().then_inc(sg.h, 1)
    p.pop_scope()
    return



def emit_lff_attn(p, z_hbm, zA_hbm, wqkv_hbm, wo_hbm, qk_cols_hbm,
                  tag="lfa", gate=None):
    p.begin_stage()
    p.push_scope()
    C = 512
    NW = 16
    WSEQ = 4

    Aq = p.sbuf(f"{tag}_Aq", [128, T], BF16)
    Bq = p.sbuf(f"{tag}_Bq", [128, T], BF16)
    Ak = p.sbuf(f"{tag}_Ak", [128, T], BF16)
    Bk = p.sbuf(f"{tag}_Bk", [128, T], BF16)
    Rbf = p.sbuf(f"{tag}_R", [128, 128], BF16)
    masks = [p.sbuf(f"{tag}_m{a}", [128, 1024], F32) for a in range(4)]
    qs_col = p.sbuf(f"{tag}_qsc", [128, 2], F32)
    ks_col = p.sbuf(f"{tag}_ksc", [128, 2], F32)

    sg, sqs = p.sem(), p.sem()
    sqk2 = [p.sem(), p.sem()]
    so2 = [p.sem(), p.sem()]
    szc, sw, ss, st, sv = (p.sem() for _ in range(5))
    sdone = p.sem()

    with p.block() as blk0:
        @blk0.sync
        def _(sync):
            p.emit_entry(sync, gate)
            sync.dma_start(out=qs_col[:, :], in_=qk_cols_hbm[:, 0:2]
                           ).then_inc(sqs.h, 16)
            sync.dma_start(out=ks_col[:, :], in_=qk_cols_hbm[:, 2:4]
                           ).then_inc(sqs.h, 16)

    emit_lff_tables(p, qs_col, ks_col, Aq, Bq, Ak, Bk, Rbf, masks, sg,
                    tag + "tb", wait=(sqs, 32), gate=gate)
    TBSG = 8

    wq = [p.sbuf(f"{tag}_wq{i}", [128, 1536], BF16) for i in range(DT)]
    wo_b = [p.sbuf(f"{tag}_wo{i}", [128, D], BF16) for i in range(4)]
    zc = [p.sbuf(f"{tag}_zc{j}", [128, DT * (C + 1)], F32) for j in range(2)]
    hq = p.sbuf(f"{tag}_hq", [128, DT * C], BF16)
    qkbf = p.sbuf(f"{tag}_qkbf", [128, 8 * C], BF16)
    sqb = p.sbuf(f"{tag}_sqb", [128, 8 * C], BF16)
    rows = p.sbuf(f"{tag}_rows", [2, 2 * C], F32)
    rows2 = p.sbuf(f"{tag}_rows2", [2, 2 * C], F32)
    rstd_b = [p.sbuf(f"{tag}_rsb{j}", [128, C], F32) for j in range(2)]
    rvb = [p.sbuf(f"{tag}_rvb{j}", [128, C], BF16) for j in range(2)]
    t1b = p.sbuf(f"{tag}_t1", [128, C], BF16)
    t2b = p.sbuf(f"{tag}_t2", [128, C], BF16)
    u_sb = p.sbuf(f"{tag}_u", [128, 4 * C], BF16)
    w_sb = [p.sbuf(f"{tag}_w{j}", [128, 4 * C], BF16) for j in range(2)]
    v_tm = [p.sbuf(f"{tag}_vtm{j}", [128, 4 * C], BF16) for j in range(2)]
    vtmp = [p.sbuf(f"{tag}_vt{j}", [128, C], BF16) for j in range(2)]
    S_sb = [p.sbuf(f"{tag}_S{j}", [128, 1024], F32) for j in range(2)]
    P_sb = p.sbuf(f"{tag}_P", [128, 4 * 1024], BF16)
    pcol = p.sbuf(f"{tag}_pcol", [128, 8], F32)
    PT_sb = p.sbuf(f"{tag}_PT", [128, 8 * C], BF16)
    osb = p.sbuf(f"{tag}_osb", [128, 4 * C], BF16)
    E2 = p.sbuf(f"{tag}_E2", [128, 2], BF16)
    E2f = p.sbuf(f"{tag}_E2f", [2, 128], F32)
    e2j = p.sbuf(f"{tag}_e2j", [128, 2], F32)
    e2p = p.sbuf(f"{tag}_e2p", [128, 1], F32)
    e2jv = p.sbuf(f"{tag}_e2jv", [2, 128], F32)
    e2pv = p.sbuf(f"{tag}_e2pv", [2, 1], F32)
    idp = p.sbuf(f"{tag}_idp", [128, 128], I32)
    idf = p.sbuf(f"{tag}_idf", [128, 128], I32)
    iden = p.sbuf(f"{tag}_iden", [128, 128], BF16)


    ps_f = [p.psum(f"{tag}_pf{j}", [128, C]) for j in range(4)]
    ps_b = [p.psum(f"{tag}_pb{j}", [128, C], BF16) for j in range(2)]
    ps_o = p.psum(f"{tag}_po", [64, C])
    ps_row = p.psum(f"{tag}_prow", [2, C])

    def mk_sched():
        st_n = ss_n = sv_n = 0
        L = []
        for wi in range(NW):
            w = wi % WSEQ
            ng = 8 if w > 0 else 4
            d = {"ng": ng}
            d["qkv"] = [st_n + m + 1 for m in range(12)]; st_n += 12
            d["vtr"] = [st_n + b + 1 for b in range(4)]; st_n += 4
            d["ssmm"], d["rmm"], d["bc"] = {}, {}, {}
            for t in range(8):
                d["ssmm"][t] = st_n + 1
                d["rmm"][t] = st_n + 2
                d["bc"][t] = st_n + 3
                st_n += 3
            d["smm"], d["pt"], d["omm"] = {}, {}, {}
            for h in range(8):
                for i4 in range(4):
                    d["smm"][(h, i4)] = st_n + 1; st_n += 1
                for g in range(ng):
                    d["pt"][(h, g)] = st_n + 1; st_n += 1
                d["omm"][h] = st_n + 1; st_n += 1
            d["wo"] = [st_n + m + 1 for m in range(DT)]; st_n += DT
            d["ev"] = {}
            for m in range(12):
                d["ev"][m] = ss_n + 1; ss_n += 1
            d["rsq"] = {}
            for t in range(8):
                d["rsq"][t] = ss_n + 1; ss_n += 1
            d["pexp"] = {}
            for h in range(8):
                for i4 in range(4):
                    d["pexp"][(h, i4)] = ss_n + 1; ss_n += 1
            d["mix"] = sv_n + 1; sv_n += 1
            d["rrec"], d["prep"], d["vte"] = {}, {}, {}
            for b in range(4):
                d["vte"][b] = sv_n + 1; sv_n += 1
            for t in range(8):
                d["rrec"][t] = sv_n + 1
                d["prep"][t] = sv_n + 2
                sv_n += 2
            d["sev"], d["pdiv"], d["ptev"], d["oev"] = {}, {}, {}, {}
            for h in range(8):
                for i4 in range(4):
                    d["sev"][(h, i4)] = sv_n + 1; sv_n += 1
                    d["pdiv"][(h, i4)] = sv_n + 1; sv_n += 1
                for g in range(ng):
                    d["ptev"][(h, g)] = sv_n + 1; sv_n += 1
                d["oev"][h] = sv_n + 1; sv_n += 1
            d["woev"] = [sv_n + m + 1 for m in range(DT)]; sv_n += DT
            L.append(d)
        return L

    SCH = mk_sched()
    HI = [(h, i4) for h in range(8) for i4 in range(4)]

    def sb3(tile, w=C):
        return tile[:, :].rearrange("p (a t) -> p a t", t=w)

    es = ExitStack()
    stag = [es.enter_context(p.nc.sbuf_tensor(f"{tag}_st{j}", [128, 768],
                                              F32)) for j in range(2)]
    wblocks = [(wq[i][:, h * 768:(h + 1) * 768],
                wqkv_hbm[i * 128:(i + 1) * 128, h * 768:(h + 1) * 768], 768)
               for i in range(DT) for h in range(2)] + \
              [(wo_b[i][:, h * 512:(h + 1) * 512],
                wo_hbm[i * 128:(i + 1) * 128, h * 512:(h + 1) * 512], 512)
               for i in range(4) for h in range(2)]

    with p.block() as blk:
        @blk.sync
        def _(sync):
            sync.wait_ge(sg.h, TBSG)
            for i, (dstt, src, w_) in enumerate(wblocks):
                if i >= 2:
                    sync.wait_ge(sw.h, i - 1)
                sync.dma_start(out=stag[i % 2][:, 0:w_], in_=src
                               ).then_inc(sqk2[i % 2].h, 16)
            for wi in range(NW):
                w = wi % WSEQ
                c0 = wi * C
                if wi >= 2:
                    sync.wait_ge(so2[wi % 2].h, (wi // 2) * 16)
                if w == 0:
                    sync.dma_start(
                        out=sb3(zc[wi % 2], C + 1)[:, :, 1:C + 1],
                        in_=z_hbm[:, c0:c0 + C].rearrange(
                            "(a p) t -> p a t", p=128)).then_inc(szc.h, 16)
                else:
                    sync.dma_start(
                        out=sb3(zc[wi % 2], C + 1),
                        in_=z_hbm[:, c0 - 1:c0 + C].rearrange(
                            "(a p) t -> p a t", p=128)).then_inc(szc.h, 16)
                sync.wait_ge(sv.h, SCH[wi]["woev"][-1])
                sync.dma_start(out=zA_hbm[:, c0:c0 + C].rearrange(
                    "(a p) t -> p a t", p=128),
                    in_=sb3(zc[wi % 2], C + 1)[:, :, 1:C + 1]
                    ).then_inc(so2[wi % 2].h, 16)
            sync.wait_ge(so2[0].h, NW * 8)
            sync.wait_ge(so2[1].h, NW * 8)
            sync.nop().then_inc(sdone.h, 1)

        @blk.gpsimd
        def _(g):
            p.entry_wait(g)
            g.wait_ge(sg.h, TBSG)
            g.iota(e2j[:, :], [[1, 2]], channel_multiplier=0,
                   allow_small_or_imprecise_dtypes=True)
            g.iota(e2p[:, :], [[0, 1]], channel_multiplier=1,
                   allow_small_or_imprecise_dtypes=True)
            g.iota(e2jv[:, :], [[1, 128]], channel_multiplier=0,
                   allow_small_or_imprecise_dtypes=True)
            g.iota(e2pv[:, :], [[0, 1]], channel_multiplier=1,
                   allow_small_or_imprecise_dtypes=True)
            g.iota(idp[:, :], [[0, 128]], channel_multiplier=1)
            g.iota(idf[:, :], [[1, 128]], channel_multiplier=0)
            g.drain()
            g.engine_nop().then_inc(st.h, 1)
            for i, (dstt, src, w_) in enumerate(wblocks):
                g.wait_ge(sqk2[i % 2].h, (i // 2 + 1) * 16)
                g.tensor_copy(dstt, stag[i % 2][:, 0:w_]
                              ).then_inc(sw.h, 1)

        @blk.scalar
        def _(sc):
            p.entry_wait(sc)
            for wi in range(NW):
                w = wi % WSEQ
                d = SCH[wi]
                pd = SCH[wi - 1] if wi >= 1 else None
                for m in range(12):
                    sc.wait_ge(st.h, 1 + d["qkv"][m])
                    if m < 8:
                        if pd is not None:
                            sc.wait_ge(st.h, 1 + pd["ssmm"][m])
                        if pd is not None and m == 0:
                            sc.wait_ge(sv.h, pd["prep"][7])
                        sc.copy(qkbf[:, m * C:(m + 1) * C],
                                ps_f[m % 2][:, :])
                        sc.activation(sqb[:, m * C:(m + 1) * C],
                                      ps_f[m % 2][:, :],
                                      AF.Square).then_inc(ss.h, 1)
                    else:
                        b = m - 8
                        if b >= 2:
                            sc.wait_ge(st.h, 1 + d["vtr"][b - 2])
                        elif pd is not None:
                            sc.wait_ge(st.h, 1 + pd["vtr"][3])
                        sc.copy(vtmp[b % 2][:, :], ps_f[m % 2][:, :]
                                ).then_inc(ss.h, 1)
                for t in range(8):
                    sc.wait_ge(st.h, 1 + d["ssmm"][t])
                    if t >= 2:
                        sc.wait_ge(sv.h, d["rrec"][t - 2])
                    elif pd is not None:
                        sc.wait_ge(sv.h, pd["rrec"][6 + t])
                    sc.activation(rows[0:2, (t % 2) * C:(t % 2 + 1) * C],
                                  ps_row[:, :], AF.Sqrt).then_inc(ss.h, 1)
                for h in range(8):
                    for i4 in range(4):
                        hi = h * 4 + i4
                        sc.wait_ge(sv.h, d["sev"][(h, i4)])
                        if h >= 1:
                            sc.wait_ge(st.h, 1 + d["pt"][(h - 1, d["ng"] - 1)])
                        elif pd is not None:
                            sc.wait_ge(st.h, 1 + pd["pt"][(7, pd["ng"] - 1)])
                        if hi >= 2:
                            sc.wait_ge(sv.h, d["pdiv"][HI[hi - 2]])
                        elif pd is not None:
                            sc.wait_ge(sv.h, pd["pdiv"][(7, 3)])
                        lo = 0 if w > 0 else C
                        pc = (hi % 2) * 4
                        sc.activation(
                            P_sb[:, i4 * 1024 + lo:i4 * 1024 + 1024],
                            S_sb[hi % 2][:, lo:1024], AF.Exp,
                            bias=pcol[:, pc + 1:pc + 2],
                            accum_out=pcol[:, pc + 2:pc + 3]
                            ).then_inc(ss.h, 1)

        @blk.tensor
        def _(te):
            p.entry_wait(te)
            te.wait_ge(st.h, 1)
            te.wait_ge(sg.h, TBSG + 1)
            te.wait_ge(sw.h, len(wblocks))
            for wi in range(NW):
                w = wi % WSEQ
                d = SCH[wi]
                pd = SCH[wi - 1] if wi >= 1 else None
                te.wait_ge(sv.h, d["mix"])
                for m in range(12):
                    if m >= 2:
                        te.wait_ge(ss.h, d["ev"][m - 2])
                    elif pd is not None:
                        te.wait_ge(ss.h, pd["ev"][11])
                    for i in range(DT):
                        mm = te.matmul(ps_f[m % 2][:, :],
                                       wq[i][:, m * 128:(m + 1) * 128],
                                       hq[:, i * C:(i + 1) * C],
                                       start=(i == 0), stop=(i == DT - 1))
                    mm.then_inc(st.h, 1)
                for b in range(4):
                    te.wait_ge(ss.h, d["ev"][8 + b])
                    if b >= 2:
                        te.wait_ge(sv.h, d["vte"][b - 2])
                    elif pd is not None:
                        te.wait_ge(sv.h, pd["vte"][3])
                    for i4 in range(4):
                        mm = te.transpose(
                            ps_b[b % 2][:, i4 * 128:(i4 + 1) * 128],
                            vtmp[b % 2][:, i4 * 128:(i4 + 1) * 128],
                            iden[:, :])
                    mm.then_inc(st.h, 1)
                for t in range(8):
                    te.wait_ge(ss.h, d["ev"][t])
                    if t >= 1:
                        te.wait_ge(ss.h, d["rsq"][t - 1])
                    elif pd is not None:
                        te.wait_ge(ss.h, pd["rsq"][7])
                    te.matmul(ps_row[:, :], E2[:, :],
                              sqb[:, t * C:(t + 1) * C],
                              start=True, stop=True).then_inc(st.h, 1)
                    if pd is not None and t < 2:
                        te.wait_ge(sv.h, pd["prep"][6 + t])
                    elif t >= 2:
                        te.wait_ge(sv.h, d["prep"][t - 2])
                    te.matmul(ps_f[2 + t % 2][:, :], Rbf[:, :],
                              qkbf[:, t * C:(t + 1) * C],
                              start=True, stop=True).then_inc(st.h, 1)
                    te.wait_ge(sv.h, d["rrec"][t])
                    te.matmul(ps_f[t % 2][:, :], E2f[:, :],
                              rows2[0:2, (t % 2) * C:(t % 2 + 1) * C],
                              start=True, stop=True).then_inc(st.h, 1)
                te.wait_ge(sv.h, d["prep"][7])
                for h in range(8):
                    hb = (h % 2) * 64
                    hc = (h // 2) * C
                    for i4 in range(4):
                        hi = h * 4 + i4
                        if hi >= 2:
                            te.wait_ge(sv.h, d["sev"][HI[hi - 2]])
                        elif pd is not None:
                            te.wait_ge(sv.h, pd["sev"][(7, 3)])
                        lhs = u_sb[hb:hb + 64, hc + i4 * 128:hc + (i4 + 1) * 128]
                        pr = 2 * (hi % 2)
                        if w > 0:
                            te.matmul(ps_f[pr][:, :],
                                      lhs,
                                      w_sb[(wi - 1) % 2][hb:hb + 64,
                                                         hc:hc + C],
                                      start=True, stop=True)
                        mm = te.matmul(ps_f[pr + 1][:, :], lhs,
                                       w_sb[wi % 2][hb:hb + 64, hc:hc + C],
                                       start=True, stop=True)
                        mm.then_inc(st.h, 1)
                    te.wait_ge(sv.h, d["pdiv"][(h, 3)])
                    for g in range(d["ng"]):
                        gj = g if w > 0 else g + 4
                        if g >= 2:
                            te.wait_ge(sv.h, d["ptev"][(h, g - 2)])
                        elif h >= 1:
                            te.wait_ge(sv.h, d["ptev"][(h - 1, d["ng"] - 1 - (1 - g))])
                        elif pd is not None:
                            te.wait_ge(sv.h, pd["ptev"][(7, pd["ng"] - 1 - (1 - g))])
                        for i4 in range(4):
                            mm = te.transpose(
                                ps_b[g % 2][:, i4 * 128:(i4 + 1) * 128],
                                P_sb[:, i4 * 1024 + gj * 128:
                                     i4 * 1024 + (gj + 1) * 128],
                                iden[:, :])
                        mm.then_inc(st.h, 1)
                    te.wait_ge(sv.h, d["ptev"][(h, d["ng"] - 1)])
                    if h >= 1:
                        te.wait_ge(sv.h, d["oev"][h - 1])
                    elif pd is not None:
                        te.wait_ge(sv.h, pd["oev"][7])
                    for g in range(d["ng"]):
                        if w > 0:
                            vsrc = v_tm[(wi - 1) % 2] if g < 4 else v_tm[wi % 2]
                            vt = g % 4
                        else:
                            vsrc = v_tm[wi % 2]
                            vt = g
                        vcol = (h // 2) * C + vt * 128 + hb
                        mm = te.matmul(ps_o[:, :],
                                       vsrc[:, vcol:vcol + 64],
                                       PT_sb[:, g * C:(g + 1) * C],
                                       start=(g == 0),
                                       stop=(g == d["ng"] - 1))
                    mm.then_inc(st.h, 1)
                te.wait_ge(sv.h, d["oev"][7])
                for m in range(DT):
                    if m >= 2:
                        te.wait_ge(sv.h, d["woev"][m - 2])
                    elif pd is not None:
                        te.wait_ge(sv.h, pd["woev"][DT - 1])
                    for i in range(4):
                        mm = te.matmul(ps_f[m % 2][:, :],
                                       wo_b[i][:, m * 128:(m + 1) * 128],
                                       osb[:, i * C:(i + 1) * C],
                                       start=(i == 0), stop=(i == 3))
                    mm.then_inc(st.h, 1)

        @blk.vector
        def _(v):
            p.entry_wait(v)
            v.wait_ge(st.h, 1)
            v.tensor_tensor(iden[:, :], idp[:, :], idf[:, :], ALU.is_equal)
            v.tensor_single_scalar(e2p[:, :], e2p[:, :], 64.0, ALU.is_ge)
            v.tensor_single_scalar(e2jv[:, :], e2jv[:, :], 64.0, ALU.is_ge)
            v.drain()
            v.tensor_scalar(E2[:, :], e2j[:, :], e2p[:, 0:1], None,
                            ALU.is_equal)
            v.tensor_scalar(E2f[:, :], e2jv[:, :], e2pv[:, 0:1], None,
                            ALU.is_equal)
            v.drain()
            v.engine_nop().then_inc(sg.h, 1)
            for wi in range(NW):
                w = wi % WSEQ
                d = SCH[wi]
                pd = SCH[wi - 1] if wi >= 1 else None
                v.wait_ge(szc.h, (wi + 1) * 16)
                if pd is not None:
                    v.wait_ge(st.h, 1 + pd["qkv"][11])
                if w == 0:
                    for i in range(DT):
                        v.memset(zc[wi % 2][:, i * (C + 1):i * (C + 1) + 1],
                                 0.0)
                    v.drain()
                for i in range(DT):
                    off = i * (C + 1) + (1 if i < 4 else 0)
                    v.tensor_copy(hq[:, i * C:(i + 1) * C],
                                  zc[wi % 2][:, off:off + C])
                v.drain()
                v.engine_nop().then_inc(sv.h, 1)
                for b in range(4):
                    v.wait_ge(st.h, 1 + d["vtr"][b])
                    v.tensor_copy(v_tm[wi % 2][:, b * C:(b + 1) * C],
                                  ps_b[b % 2][:, :]).then_inc(sv.h, 1)
                for t in range(8):
                    v.wait_ge(ss.h, d["rsq"][t])
                    if t >= 2:
                        v.wait_ge(st.h, 1 + d["bc"][t - 2])
                    elif pd is not None:
                        v.wait_ge(st.h, 1 + pd["bc"][6 + t])
                    v.reciprocal(rows2[0:2, (t % 2) * C:(t % 2 + 1) * C],
                                 rows[0:2, (t % 2) * C:(t % 2 + 1) * C])
                    v.drain()
                    if t < 4:
                        v.tensor_scalar_mul(
                            rows2[0:2, (t % 2) * C:(t % 2 + 1) * C],
                            rows2[0:2, (t % 2) * C:(t % 2 + 1) * C], 8.0)
                        v.drain()
                    v.engine_nop().then_inc(sv.h, 1)
                    A = Aq if t < 4 else Ak
                    Bt = Bq if t < 4 else Bk
                    v.wait_ge(st.h, 1 + d["bc"][t])
                    v.tensor_copy(rvb[t % 2][:, :], ps_f[2 + t % 2][:, :])
                    v.tensor_copy(rstd_b[t % 2][:, :], ps_f[t % 2][:, :])
                    v.drain()
                    v.tensor_tensor(t1b[:, :], qkbf[:, t * C:(t + 1) * C],
                                    A[:, w * C:(w + 1) * C], ALU.mult)
                    v.tensor_tensor(t2b[:, :], rvb[t % 2][:, :],
                                    Bt[:, w * C:(w + 1) * C], ALU.mult)
                    v.drain()
                    v.tensor_tensor(t1b[:, :], t1b[:, :], t2b[:, :], ALU.add)
                    v.drain()
                    dst = u_sb if t < 4 else w_sb[wi % 2]
                    v.tensor_tensor(dst[:, (t % 4) * C:(t % 4 + 1) * C],
                                    t1b[:, :], rstd_b[t % 2][:, :], ALU.mult)
                    v.drain()
                    v.engine_nop().then_inc(sv.h, 1)
                for h in range(8):
                    for i4 in range(4):
                        hi = h * 4 + i4
                        pc = (hi % 2) * 4
                        v.wait_ge(st.h, 1 + d["smm"][(h, i4)])
                        if hi >= 2:
                            v.wait_ge(ss.h, d["pexp"][HI[hi - 2]])
                        elif pd is not None:
                            v.wait_ge(ss.h, pd["pexp"][(7, 3)])
                        pr = 2 * (hi % 2)
                        if w > 0:
                            v.tensor_tensor(S_sb[hi % 2][:, 0:C],
                                            ps_f[pr][:, :],
                                            masks[i4][:, 0:C], ALU.add)
                        v.tensor_tensor(S_sb[hi % 2][:, C:1024],
                                        ps_f[pr + 1][:, :],
                                        masks[i4][:, C:1024], ALU.add)
                        v.drain()
                        lo = 0 if w > 0 else C
                        v.reduce_max(pcol[:, pc:pc + 1],
                                     S_sb[hi % 2][:, lo:1024], AX.X)
                        v.drain()
                        v.tensor_scalar_mul(pcol[:, pc + 1:pc + 2],
                                            pcol[:, pc:pc + 1], -1.0)
                        v.drain()
                        v.engine_nop().then_inc(sv.h, 1)
                        v.wait_ge(ss.h, d["pexp"][(h, i4)])
                        v.reciprocal(pcol[:, pc + 3:pc + 4],
                                     pcol[:, pc + 2:pc + 3])
                        v.drain()
                        v.tensor_scalar(
                            P_sb[:, i4 * 1024 + lo:i4 * 1024 + 1024],
                            P_sb[:, i4 * 1024 + lo:i4 * 1024 + 1024],
                            pcol[:, pc + 3:pc + 4], None, ALU.mult)
                        v.drain()
                        v.engine_nop().then_inc(sv.h, 1)
                    for g in range(d["ng"]):
                        v.wait_ge(st.h, 1 + d["pt"][(h, g)])
                        v.tensor_copy(PT_sb[:, g * C:(g + 1) * C],
                                      ps_b[g % 2][:, :]).then_inc(sv.h, 1)
                    v.wait_ge(st.h, 1 + d["omm"][h])
                    v.tensor_copy(osb[(h % 2) * 64:(h % 2) * 64 + 64,
                                      (h // 2) * C:(h // 2 + 1) * C],
                                  ps_o[:, :]).then_inc(sv.h, 1)
                if wi >= 2:
                    v.wait_ge(so2[wi % 2].h, (wi // 2) * 16)
                for m in range(DT):
                    v.wait_ge(st.h, 1 + d["wo"][m])
                    v.tensor_tensor(
                        zc[wi % 2][:, m * (C + 1) + 1:m * (C + 1) + C + 1],
                        ps_f[m % 2][:, :],
                        zc[wi % 2][:, m * (C + 1) + 1:m * (C + 1) + C + 1],
                        ALU.add).then_inc(sv.h, 1)

    es.close()
    p.pop_scope()
    return (sdone, 1)



def emit_lff_ffn(p, zA_hbm, zout_hbm, w1p_hbm, w2p_hbm, lnb_hbm, acc_hbm=None,
                 tag="lff", gate=None):
    p.begin_stage()
    p.push_scope()
    C = 512
    NCH = TT // C
    PSEQ = T // C
    FT1 = 44
    HF = 22
    EPS = 1e-5
    SS_C = 2 + DT + FT1
    SV_C = 3 + DT + HF + DT

    def st_w1(c, pos):
        return 2 + 54 * c + pos + 1

    def st_w2(c, dt):
        return 2 + 54 * c + 45 + dt + 1

    def st_stats(c):
        return 1 if c == 0 else 2 + 54 * (c - 1) + 45

    def st_bcast(c):
        return 2 if c == 0 else 2 + 54 * (c - 1) + 54

    LO = [5, 6, 7, 8, 9, 10, 0, 1, 2, 3, 4, 5]
    NLG = len(LO)

    def pos_ft(pos):
        return 22 + pos if pos < HF else pos - HF

    def pos_oi(pos):
        ft = pos_ft(pos)
        if pos < HF:
            return 0 if ft < 24 else 1 + (ft - 24) // 4
        return 11 if ft >= 20 else 6 + ft // 4

    first_use, last_use = {}, {}
    for _pos in range(FT1):
        _o = pos_oi(_pos)
        first_use.setdefault(_o, _pos)
        last_use[_o] = _pos

    w1bf = p.dram(f"{tag}_w1bf", [128, 11 * DT * C], BF16, kind="Internal")
    w1bf4 = w1bf.rearrange("p (g a l) -> p g a l", a=DT, l=C)
    w2bf = p.dram(f"{tag}_w2bf", [128, HF * D], BF16, kind="Internal")
    sini = p.sem()
    swg = [p.sem(), p.sem()]
    sw2l = p.sem()
    szc2 = [p.sem(), p.sem()]
    so2 = [Sem(p._alloc_sem()), Sem(p._alloc_sem())]
    sac = p.sem()
    ss, st, sv = (p.sem() for _ in range(3))
    sdone = p.sem()
    sgz = p.sem()
    wp_blocks = []
    for a in range(DT):
        wp_blocks.append((w1p_hbm[a * 128:(a + 1) * 128, 0:4096],
                          w1bf4[:, 0:8, a, :], 4096, C))
        wp_blocks.append((w1p_hbm[a * 128:(a + 1) * 128, 4096:5632],
                          w1bf4[:, 8:11, a, :], 1536, C))
    for ftt in range(HF):
        wp_blocks.append((w2p_hbm[ftt * 128:(ftt + 1) * 128, :],
                          w2bf[:, ftt * D:(ftt + 1) * D], D, None))
    done, _ = emit_wprep(p, None, tag=tag + "wp", gate=gate,
                         blocks=wp_blocks)

    w1g = [p.sbuf(f"{tag}_w1g{j}", [128, DT * C], BF16) for j in range(2)]
    w2b = p.sbuf(f"{tag}_w2b", [128, HF * D], BF16)
    zc = [p.sbuf(f"{tag}_zc{j}", [128, DT * (C + 1)], F32) for j in range(2)]
    hb = p.sbuf(f"{tag}_hb", [128, DT * C], BF16)
    sqf = p.sbuf(f"{tag}_sqf", [128, DT * C], BF16)
    hn = [p.sbuf(f"{tag}_hn{j}", [128, DT * C], BF16) for j in range(2)]
    a_sb = p.sbuf(f"{tag}_a", [128, HF * C], BF16)
    g_sb = p.sbuf(f"{tag}_g", [128, HF * C], BF16)
    t2 = [p.sbuf(f"{tag}_t2{j}", [128, C], F32) for j in range(2)]
    a_b = p.sbuf(f"{tag}_ab", [128, C], F32)
    b_b = p.sbuf(f"{tag}_bb", [128, C], F32)
    rw = p.sbuf(f"{tag}_rw", [1, 6 * C], F32)
    lnb = p.sbuf(f"{tag}_lnb", [128, 68], F32)
    ones_col = p.sbuf(f"{tag}_oc", [128, 1], BF16)
    ones_row = p.sbuf(f"{tag}_or", [1, 128], F32)

    ps_sx = p.psum(f"{tag}_psx", [1, C])
    ps_sxx = p.psum(f"{tag}_psxx", [1, C])
    ps_a = p.psum(f"{tag}_pa", [128, C])
    ps_b2 = p.psum(f"{tag}_pb", [128, C])
    ps_h = [p.psum(f"{tag}_ph{j}", [128, C]) for j in range(2)]
    ps_y = [p.psum(f"{tag}_py{j}", [128, C]) for j in range(2)]

    R_M, R_E, R_V, R_SD, R_RS, R_B = (i * C for i in range(6))

    def sb3(tile, w=C):
        return tile[:, :].rearrange("p (a t) -> p a t", t=w)

    with p.block() as blk:
        @blk.sync
        def _(sync):
            sync.wait_ge(done.h, 1)
            sync.dma_start(out=lnb[:, :], in_=lnb_hbm).then_inc(sini.h, 16)
            if acc_hbm is not None:
                sync.dma_start(out=zout_hbm, in_=acc_hbm).then_inc(sac.h, 16)
            sync.dma_start(out=w2b[:, :], in_=w2bf).then_inc(sw2l.h, 16)
            def zcload(sync, c):
                tcs = c % PSEQ
                c0 = c * C
                if c >= 2:
                    sync.wait_ge(sv.h, (c - 1) * SV_C)
                    sync.wait_ge(so2[c % 2].h, (c // 2) * 16)
                if tcs == 0:
                    sync.dma_start(
                        out=sb3(zc[c % 2], C + 1)[:, :, 1:C + 1],
                        in_=zA_hbm[:, c0:c0 + C].rearrange(
                            "(a p) t -> p a t", p=128)
                        ).then_inc(szc2[c % 2].h, 16)
                else:
                    sync.dma_start(
                        out=sb3(zc[c % 2], C + 1),
                        in_=zA_hbm[:, c0 - 1:c0 + C].rearrange(
                            "(a p) t -> p a t", p=128)
                        ).then_inc(szc2[c % 2].h, 16)

            zcload(sync, 0)
            for c in range(NCH):
                c0 = c * C
                if c + 1 < NCH:
                    zcload(sync, c + 1)
                for gi in range(NLG):
                    ggi = c * NLG + gi
                    if ggi >= 2:
                        if gi >= 2:
                            sync.wait_ge(st.h, st_w1(c, last_use[gi - 2]))
                        else:
                            sync.wait_ge(st.h,
                                         st_w1(c - 1, last_use[gi + 10]))
                    gg = LO[gi]
                    sync.dma_start(
                        out=sb3(w1g[ggi % 2]),
                        in_=w1bf[:, gg * DT * C:(gg + 1) * DT * C].rearrange(
                            "p (a m) -> p a m", m=C)
                        ).then_inc(swg[ggi % 2].h, 16)
                if acc_hbm is None:
                    sync.wait_ge(sv.h, (c + 1) * SV_C)
                    sync.dma_start(
                        out=zout_hbm[:, c0:c0 + C].rearrange(
                            "(a p) t -> p a t", p=128),
                        in_=sb3(zc[c % 2], C + 1)[:, :, 1:C + 1]
                        ).then_inc(so2[c % 2].h, 16)
            sync.wait_ge(so2[0].h, NCH * 8)
            sync.wait_ge(so2[1].h, NCH * 8)
            sync.nop().then_inc(sdone.h, 1)

        @blk.gpsimd
        def _(g):
            p.entry_wait(g)
            g.memset(ones_col[:, :], 1.0)
            g.memset(ones_row[:, :], 1.0)
            g.drain()
            g.engine_nop().then_inc(sgz.h, 1)
            for c in range(NCH + 1):
                if c < NCH and c % PSEQ == 0:
                    g.wait_ge(szc2[c % 2].h, (c // 2 + 1) * 16)
                    if c >= 2:
                        g.wait_ge(so2[c % 2].h, (c // 2) * 16)
                    for i in range(DT):
                        g.memset(zc[c % 2][:, i * (C + 1):i * (C + 1) + 1],
                                 0.0)
                    g.drain()
                    g.engine_nop().then_inc(sgz.h, 1)
                if acc_hbm is not None and c >= 1:
                    cc = c - 1
                    if cc == 0:
                        g.wait_ge(sac.h, 16)
                    g.wait_ge(sv.h, (cc + 1) * SV_C)
                    g.dma_start(
                        out=zout_hbm[:, cc * C:(cc + 1) * C].rearrange(
                            "(a p) t -> p a t", p=128),
                        in_=sb3(zc[cc % 2], C + 1)[:, :, 1:C + 1],
                        accum_op=ALU.add).then_inc(so2[cc % 2].h, 16)

        @blk.scalar
        def _(sc):
            p.entry_wait(sc)
            sc.wait_ge(sini.h, 16)
            for c in range(NCH):
                cs, cv = c * SS_C, c * SV_C
                sc.wait_ge(szc2[c % 2].h, (c // 2 + 1) * 16)
                sc.wait_ge(sgz.h, c // PSEQ + 2)
                if c >= 1:
                    sc.wait_ge(st.h, st_stats(c - 1))
                for i in range(DT):
                    off = i * (C + 1) + (1 if i < 4 else 0)
                    sc.copy(hb[:, i * C:(i + 1) * C],
                            zc[c % 2][:, off:off + C])
                    sc.activation(sqf[:, i * C:(i + 1) * C],
                                  zc[c % 2][:, off:off + C], AF.Square)
                sc.drain()
                sc.nop().then_inc(ss.h, 1)
                sc.wait_ge(sv.h, cv + 1)
                sc.activation(rw[0:1, R_SD:R_SD + C], rw[0:1, R_V:R_V + C],
                              AF.Sqrt).then_inc(ss.h, 1)
                for i in range(DT):
                    sc.wait_ge(sv.h, cv + 3 + i + 1)
                    sc.activation(hn[c % 2][:, i * C:(i + 1) * C],
                                  t2[i % 2][:, :],
                                  AF.Identity, bias=lnb[:, 8 + i:9 + i],
                                  scale=lnb[:, i:i + 1]).then_inc(ss.h, 1)
                for pos in range(FT1):
                    ft = pos_ft(pos)
                    sc.wait_ge(st.h, st_w1(c, pos))
                    if pos < HF:
                        sc.activation(g_sb[:, (ft - HF) * C:(ft - HF + 1) * C],
                                      ps_h[pos % 2][:, :],
                                      AF.Gelu, bias=lnb[:, 16 + ft:17 + ft]
                                      ).then_inc(ss.h, 1)
                    else:
                        sc.activation(a_sb[:, ft * C:(ft + 1) * C],
                                      ps_h[pos % 2][:, :], AF.Identity,
                                      bias=lnb[:, 16 + ft:17 + ft]
                                      ).then_inc(ss.h, 1)

        @blk.tensor
        def _(te):
            p.entry_wait(te)
            te.wait_ge(sgz.h, 1)
            te.wait_ge(sw2l.h, 16)

            def stats(c):
                cs = c * SS_C
                te.wait_ge(ss.h, cs + 1)
                if c >= 1:
                    te.wait_ge(sv.h, (c - 1) * SV_C + 1)
                for i in range(DT):
                    te.matmul(ps_sx[:, :], ones_col[:, :],
                              hb[:, i * C:(i + 1) * C],
                              start=(i == 0), stop=(i == DT - 1))
                    m2 = te.matmul(ps_sxx[:, :], ones_col[:, :],
                                   sqf[:, i * C:(i + 1) * C],
                                   start=(i == 0), stop=(i == DT - 1),
                                   skip_group_check=True)
                m2.then_inc(st.h, 1)

            def bcast(c):
                cv = c * SV_C
                te.wait_ge(sv.h, cv + 2)
                te.matmul(ps_a[:, :], ones_row[:, :], rw[0:1, R_RS:R_RS + C],
                          start=True, stop=True)
                te.matmul(ps_b2[:, :], ones_row[:, :], rw[0:1, R_B:R_B + C],
                          start=True, stop=True).then_inc(st.h, 1)

            stats(0)
            bcast(0)
            for c in range(NCH):
                cs, cv = c * SS_C, c * SV_C
                te.wait_ge(ss.h, cs + 2 + DT)
                for pos in range(FT1):
                    ft = pos_ft(pos)
                    oi = pos_oi(pos)
                    ggi2 = c * NLG + oi
                    if pos == first_use[oi]:
                        te.wait_ge(swg[ggi2 % 2].h, (ggi2 // 2 + 1) * 16)
                    if pos >= 2:
                        te.wait_ge(ss.h, cs + 2 + DT + pos - 1)
                    elif c >= 1:
                        te.wait_ge(ss.h, (c - 1) * SS_C + SS_C)
                    woff = (ft % 4)
                    for i in range(DT):
                        mm = te.matmul(
                            ps_h[pos % 2][:, :],
                            w1g[ggi2 % 2][:, i * C + woff * 128:
                                          i * C + (woff + 1) * 128],
                            hn[c % 2][:, i * C:(i + 1) * C],
                            start=(i == 0), stop=(i == DT - 1))
                    mm.then_inc(st.h, 1)
                if c + 1 < NCH:
                    stats(c + 1)
                else:
                    te.wait_ge(st.h, 2 + 54 * c + 44)
                    te.nop().then_inc(st.h, 1)
                te.wait_ge(sv.h, cv + 3 + DT + HF)
                for dt in range(DT):
                    if dt >= 2:
                        te.wait_ge(sv.h, cv + 3 + DT + HF + dt - 1)
                    elif c >= 1:
                        te.wait_ge(sv.h, c * SV_C)
                    for ftt in range(HF):
                        mm = te.matmul(
                            ps_y[dt % 2][:, :],
                            w2b[:, ftt * D + dt * 128:ftt * D + (dt + 1) * 128],
                            a_sb[:, ftt * C:(ftt + 1) * C],
                            start=(ftt == 0), stop=(ftt == HF - 1))
                    mm.then_inc(st.h, 1)
                if c + 1 < NCH:
                    bcast(c + 1)
                else:
                    te.wait_ge(st.h, 2 + 54 * c + 53)
                    te.nop().then_inc(st.h, 1)

        @blk.vector
        def _(v):
            p.entry_wait(v)
            for c in range(NCH):
                cs, cv = c * SS_C, c * SV_C
                v.wait_ge(st.h, st_stats(c))
                v.tensor_scalar_mul(rw[0:1, R_M:R_M + C], ps_sx[:, :], 1.0 / D)
                v.tensor_scalar_mul(rw[0:1, R_E:R_E + C], ps_sxx[:, :],
                                    1.0 / D)
                v.drain()
                v.tensor_tensor(rw[0:1, R_V:R_V + C], rw[0:1, R_M:R_M + C],
                                rw[0:1, R_M:R_M + C], ALU.mult)
                v.drain()
                v.tensor_tensor(rw[0:1, R_V:R_V + C], rw[0:1, R_E:R_E + C],
                                rw[0:1, R_V:R_V + C], ALU.subtract)
                v.drain()
                v.tensor_scalar_add(rw[0:1, R_V:R_V + C],
                                    rw[0:1, R_V:R_V + C], EPS
                                    ).then_inc(sv.h, 1)
                v.wait_ge(ss.h, cs + 2)
                v.reciprocal(rw[0:1, R_RS:R_RS + C], rw[0:1, R_SD:R_SD + C])
                v.drain()
                v.scalar_tensor_tensor(rw[0:1, R_B:R_B + C],
                                       rw[0:1, R_M:R_M + C], -1.0,
                                       rw[0:1, R_RS:R_RS + C],
                                       ALU.mult, ALU.mult).then_inc(sv.h, 1)
                v.wait_ge(st.h, st_bcast(c))
                v.tensor_copy(a_b[:, :], ps_a[:, :])
                v.tensor_copy(b_b[:, :], ps_b2[:, :])
                v.drain()
                v.engine_nop().then_inc(sv.h, 1)
                for i in range(DT):
                    off = i * (C + 1) + (1 if i < 4 else 0)
                    if i >= 2:
                        v.wait_ge(ss.h, cs + 2 + (i - 2) + 1)
                    v.tensor_tensor(t2[i % 2][:, :],
                                    zc[c % 2][:, off:off + C],
                                    a_b[:, :], ALU.mult)
                    v.drain()
                    v.tensor_tensor(t2[i % 2][:, :], t2[i % 2][:, :],
                                    b_b[:, :], ALU.add)
                    v.drain()
                    v.engine_nop().then_inc(sv.h, 1)
                for j in range(HF):
                    v.wait_ge(ss.h, cs + 2 + DT + HF + j + 1)
                    v.tensor_tensor(a_sb[:, j * C:(j + 1) * C],
                                    a_sb[:, j * C:(j + 1) * C],
                                    g_sb[:, j * C:(j + 1) * C], ALU.mult
                                    ).then_inc(sv.h, 1)
                if c >= 2:
                    v.wait_ge(so2[c % 2].h, (c // 2) * 16)
                for dt in range(DT):
                    off = dt * (C + 1) + 1
                    v.wait_ge(st.h, st_w2(c, dt))
                    v.scalar_tensor_tensor(zc[c % 2][:, off:off + C],
                                           ps_y[dt % 2][:, :],
                                           lnb[:, 60 + dt:61 + dt],
                                           zc[c % 2][:, off:off + C],
                                           ALU.add, ALU.add
                                           ).then_inc(sv.h, 1)

    p.pop_scope()
    return (sdone, 1)



def emit_add(p, a_hbm, b_hbm, y_hbm, tag="fad", gate=None):
    p.begin_stage()
    p.push_scope()
    C = 2048
    NCH = TT // C
    at = [p.sbuf(f"{tag}_a{j}", [128, DT * C // 2], F32) for j in range(2)]
    bt = [p.sbuf(f"{tag}_b{j}", [128, DT * C // 2], F32) for j in range(2)]
    yt = [p.sbuf(f"{tag}_y{j}", [128, DT * C // 2], F32) for j in range(2)]
    sa2 = [p.sem(), p.sem()]
    sb2 = [p.sem(), p.sem()]
    sv, so = p.sem(), p.sem()
    sdone = p.sem()
    HC = C // 2

    def fm(hbm, c):
        return hbm[:, c * HC:(c + 1) * HC].rearrange("(a p) t -> p a t", p=128)

    def sb3(tile):
        return tile[:, :].rearrange("p (a t) -> p a t", t=HC)

    NH = TT // HC
    with p.block() as blk:
        @blk.sync
        def _(sync):
            p.emit_entry(sync, gate)
            for c in range(NH):
                if c >= 2:
                    sync.wait_ge(sv.h, c - 1)
                sync.dma_start(out=sb3(at[c % 2]), in_=fm(a_hbm, c)
                               ).then_inc(sa2[c % 2].h, 16)
                sync.dma_start(out=sb3(bt[c % 2]), in_=fm(b_hbm, c)
                               ).then_inc(sb2[c % 2].h, 16)
                sync.wait_ge(sv.h, c + 1)
                sync.dma_start(out=fm(y_hbm, c), in_=sb3(yt[c % 2])
                               ).then_inc(so.h, 16)
            sync.wait_ge(so.h, NH * 16)
            sync.nop().then_inc(sdone.h, 1)

        @blk.vector
        def _(v):
            p.entry_wait(v)
            for c in range(NH):
                v.wait_ge(sa2[c % 2].h, (c // 2 + 1) * 16)
                v.wait_ge(sb2[c % 2].h, (c // 2 + 1) * 16)
                if c >= 2:
                    v.wait_ge(so.h, (c - 1) * 16)
                v.tensor_tensor(yt[c % 2][:, :], at[c % 2][:, :],
                                bt[c % 2][:, :], ALU.add).then_inc(sv.h, 1)

    p.pop_scope()
    return (sdone, 1)



def _to_fm(x_core):
    return np.ascontiguousarray(x_core.transpose(2, 0, 1).reshape(D, TT))


def _from_fm(y_fm):
    return np.ascontiguousarray(
        np.asarray(y_fm).reshape(D, B, T).transpose(1, 2, 0))


def _colpack(vec):
    n = vec.size // 128
    return np.ascontiguousarray(vec.reshape(n, 128).T).astype(np.float32)


def _prep_shared(I):
    HID = 2730
    sh = {}
    sh["ktT"] = np.ascontiguousarray(I["L0_attn_key_param_tokens"].T)
    sh["v0"] = np.ascontiguousarray(I["L0_attn_value_param_tokens"])
    sh["wr"] = np.ascontiguousarray(I["L0_ffn_W_router"])
    sh["br"] = np.ascontiguousarray(I["L0_ffn_b_router"].reshape(8, 1))
    sh["w1f"] = np.ascontiguousarray(I["L0_ffn_W1"].reshape(8 * D, 4096))
    sh["w2f"] = np.ascontiguousarray(I["L0_ffn_W2"].reshape(8 * 4096, D))
    sh["b1c"] = np.concatenate([_colpack(I["L0_ffn_b1"][e]) for e in range(8)],
                               axis=1)
    sh["b2c"] = np.concatenate([_colpack(I["L0_ffn_b2"][e]) for e in range(8)],
                               axis=1)
    for nm, key in (("wkT", "L1_attn_Wk"), ("wvT", "L1_attn_Wv"),
                    ("wrT", "L1_attn_Wr"), ("woT", "L1_attn_Wo")):
        sh[nm] = np.ascontiguousarray(I[key].T)
    sh["prm"] = np.concatenate([
        _colpack(I["L1_attn_time_mix_key"].ravel()),
        _colpack(I["L1_attn_time_mix_value"].ravel()),
        _colpack(I["L1_attn_time_mix_receptance"].ravel()),
        _colpack(I["L1_attn_time_decay"].ravel()),
        _colpack(I["L1_attn_time_first"].ravel()),
        _colpack(I["L1_attn_bo"].ravel())], axis=1)
    perm = lambda v: v[(np.arange(64) + 32) % 64]
    for l in range(2):
        qs = I["L1_ffn_q_scale"][l]
        ks = I["L1_ffn_k_scale"][l]
        sh[f"qkc{l}"] = np.ascontiguousarray(np.stack(
            [np.tile(qs, 2), np.tile(perm(qs), 2),
             np.tile(ks, 2), np.tile(perm(ks), 2)], axis=1)).astype(np.float32)
        sh[f"wqkv{l}"] = np.ascontiguousarray(I["L1_ffn_wqkv"][l])
        sh[f"wo{l}"] = np.ascontiguousarray(I["L1_ffn_wo"][l])
        w1 = I["L1_ffn_w1"][l]
        pad = np.zeros((D, 2816 - HID), np.float32)
        sh[f"w1p{l}"] = np.ascontiguousarray(
            np.concatenate([w1[:, :HID], pad, w1[:, HID:], pad], axis=1))
        sh[f"w2p{l}"] = np.ascontiguousarray(np.concatenate(
            [I["L1_ffn_w2"][l], np.zeros((2816 - HID, D), np.float32)],
            axis=0))
        b1 = I["L1_ffn_b1"][l]
        zpad = np.zeros(2816 - HID, np.float32)
        b1p = np.concatenate([b1[:HID], zpad, b1[HID:], zpad])
        sh[f"lnb{l}"] = np.concatenate(
            [_colpack(I["L1_ffn_ln_g"][l]), _colpack(I["L1_ffn_ln_b"][l]),
             _colpack(b1p), _colpack(I["L1_ffn_b2"][l])], axis=1)
    return sh


_DRAM_SPECS = dict(
    x=([D, TT], F32), xn_bf=([D, TT], BF16), ktT=([D, 4096], F32),
    v0=([4096, D], F32), acc0=([D, TT], F32), wr=([D, 8], F32),
    br=([8, 1], F32), wrows=([8, TT], F32), w1f=([8 * D, 4096], F32),
    w2f=([8 * 4096, D], F32), b1c=([128, 8 * 32], F32),
    b2c=([128, 8 * 8], F32), x1=([D, TT], F32), xn1_bf=([D, TT], BF16),
    xn1_f=([D, TT], F32), wkT=([D, D], F32), wvT=([D, D], F32),
    wrT=([D, D], F32), woT=([D, D], F32), prm=([128, 48], F32),
    acc1=([D, TT], F32), z1=([D, TT], F32), z2=([D, TT], F32),
    zA=([D, TT], F32), y=([D, TT], F32),
)
for _l in range(2):
    _DRAM_SPECS[f"wqkv{_l}"] = ([D, 1536], F32)
    _DRAM_SPECS[f"wo{_l}"] = ([512, D], F32)
    _DRAM_SPECS[f"qkc{_l}"] = ([128, 4], F32)
    _DRAM_SPECS[f"w1p{_l}"] = ([D, 5632], F32)
    _DRAM_SPECS[f"w2p{_l}"] = ([2816, D], F32)
    _DRAM_SPECS[f"lnb{_l}"] = ([128, 68], F32)

FUSED = True

_STAGES = [
    ("ln0", "ln", ["x"], ["xn_bf"]),
    ("l0a", "l0a", ["xn_bf", "x", "ktT", "v0"], ["acc0"]),
    ("rt", "rt", ["xn_bf", "wr", "br"], ["wrows"]),
    ("moe", "moe", ["xn_bf", "wrows", "w1f", "w2f", "b1c", "b2c", "acc0"],
     ["x1"]),
    ("ln1", "ln", ["x1"], ["xn1_bf", "xn1_f"]),
    ("l1a", "l1a", ["xn1_f", "x1", "wkT", "wvT", "wrT", "woT", "prm"],
     ["acc1"]),
    ("lfa0", "lfa", ["xn1_f", "wqkv0", "wo0", "qkc0"], ["zA"]),
    ("lff0", "lff", ["zA", "w1p0", "w2p0", "lnb0"], ["z1"]),
    ("lfa1", "lfa", ["z1", "wqkv1", "wo1", "qkc1"], ["zA"]),
    ("lff1", "lffy", ["zA", "w1p1", "w2p1", "lnb1", "acc1"], ["y"]),
]


def _emit_stage(p, key, aps, ins, outs, tag, gate, prep=None):
    a = [aps[n] for n in ins] + [aps[n] for n in outs]
    if key == "ln":
        if len(outs) == 2:
            return emit_ln(p, a[0], a[1], a[2], tag=tag, gate=gate)
        return emit_ln(p, a[0], a[1], tag=tag, gate=gate)
    if key == "l0a":
        return emit_l0_attn(p, a[0], a[1], a[2], a[3], a[4], tag=tag,
                            gate=gate)
    if key == "rt":
        return emit_router(p, a[0], a[1], a[2], a[3], tag=tag, gate=gate)
    if key == "moe":
        return emit_moe(p, a[0], a[1], prep, a[2], a[3], a[4], a[5],
                        tag=tag, gate=gate)
    if key == "l1a":
        return emit_l1_attn(p, a[0], a[1], a[2], a[3], a[4], a[5], a[6],
                            a[7], tag=tag, gate=gate)
    if key == "lfa":
        return emit_lff_attn(p, a[0], a[4], a[1], a[2], a[3], tag=tag,
                             gate=gate)
    if key == "lff":
        return emit_lff_ffn(p, a[0], a[4], a[1], a[2], a[3], tag=tag,
                            gate=gate)
    if key == "lffy":
        return emit_lff_ffn(p, a[0], a[5], a[1], a[2], a[3], acc_hbm=a[4],
                            tag=tag, gate=gate)
    if key == "add":
        return emit_add(p, a[0], a[1], a[2], tag=tag, gate=gate)
    raise KeyError(key)


def kernel(**inputs):
    sh = _prep_shared(inputs)
    x = inputs["L0_attn_inputs"]
    cores = list(range(N_CORES))
    per_core_x = [_to_fm(x[4 * c:4 * c + 4]) for c in cores]

    if FUSED:
        p = Prog()
        aps = {}
        ext_in = ["x"] + [n for n in sh]
        for n in ext_in:
            aps[n] = p.dram(n, _DRAM_SPECS[n][0], _DRAM_SPECS[n][1],
                            kind="ExternalInput")
        aps["y"] = p.dram("y", [D, TT], F32, kind="ExternalOutput")
        for n in ["xn_bf", "acc0", "wrows", "x1", "xn1_bf", "xn1_f", "acc1",
                  "z1", "z2", "zA"]:
            aps[n] = p.dram(n, _DRAM_SPECS[n][0], _DRAM_SPECS[n][1],
                            kind="Internal")
        w1bf = p.dram("moe_w1bf", [8 * D, 4096], BF16, kind="Internal")
        w2bf = p.dram("moe_w2bf", [8 * 4096, D], BF16, kind="Internal")
        stag = [p.sbuf(f"mwp_s{j}", [128, 1024], F32) for j in range(2)]
        stbf = [p.sbuf(f"mwp_b{j}", [128, 1024], BF16) for j in range(2)]
        gate = None
        prep = None
        for name, key, ins, outs in _STAGES:
            if key == "moe":
                ins = [n for n in ins if n not in ("w1f", "w2f")]
            gate = _emit_stage(p, key, aps, ins, outs, name, gate,
                               prep=prep)
            if name == "ln0":
                wpA = emit_moewp(p, aps["w1f"], aps["w2f"], w1bf, w2bf,
                                 stag, stbf, tag="mwpA", experts=range(0, 2))
            if name == "rt":
                wpB = emit_moewp(p, aps["w1f"], aps["w2f"], w1bf, w2bf,
                                 stag, stbf, tag="mwpB", experts=range(2, 8),
                                 prior=[(wpA[0], 64 * 16), (wpA[1], 64 * 16)])
                prep = (w1bf, w2bf,
                        [(wpA, lambda e: min(e + 1, 2) * 32 * 16),
                         (wpB, lambda e: max(e - 1, 0) * 32 * 16)])
        nc = p.close()
        in_maps = [dict(sh, x=per_core_x[c]) for c in cores]
        res = run_bass_kernel_spmd(nc, in_maps, core_ids=cores)
        return np.concatenate(
            [_from_fm(res.results[c]["y"]) for c in cores], axis=0)

    state = [dict(sh, x=per_core_x[c]) for c in cores]
    for name, key, ins, outs in _STAGES:
        p = Prog()
        aps = {}
        for n in ins:
            aps[n] = p.dram(n, _DRAM_SPECS[n][0], _DRAM_SPECS[n][1],
                            kind="ExternalInput")
        for n in outs:
            aps[n] = p.dram(n, _DRAM_SPECS[n][0], _DRAM_SPECS[n][1],
                            kind="ExternalOutput")
        prep = None
        map_ins = list(ins)
        if key == "moe":
            w1bf = p.dram("moe_w1bf", [8 * D, 4096], BF16, kind="Internal")
            w2bf = p.dram("moe_w2bf", [8 * 4096, D], BF16, kind="Internal")
            stag = [p.sbuf(f"mwp_s{j}", [128, 1024], F32) for j in range(2)]
            stbf = [p.sbuf(f"mwp_b{j}", [128, 1024], BF16) for j in range(2)]
            wpso2 = emit_moewp(p, aps["w1f"], aps["w2f"], w1bf, w2bf,
                               stag, stbf)
            prep = (w1bf, w2bf, [(wpso2, lambda e: (e + 1) * 32 * 16)])
            ins = [n for n in ins if n not in ("w1f", "w2f")]
        _emit_stage(p, key, aps, ins, outs, name, None, prep=prep)
        nc = p.close()
        in_maps = [{n: state[c][n] for n in map_ins} for c in cores]
        res = run_bass_kernel_spmd(nc, in_maps, core_ids=cores)
        for c in cores:
            for n in outs:
                state[c][n] = np.asarray(res.results[c][n])
    return np.concatenate([_from_fm(state[c]["y"]) for c in cores], axis=0)


# version 10
# speedup vs baseline: 1.0992x; 1.0992x over previous
import math
from contextlib import ExitStack

import numpy as np
import ml_dtypes

import concourse.bass as bass
import concourse.mybir as mybir
from concourse.bass_utils import run_bass_kernel_spmd

F32 = mybir.dt.float32
BF16 = mybir.dt.bfloat16
I32 = mybir.dt.int32
AF = mybir.ActivationFunctionType
ALU = mybir.AluOpType
AX = mybir.AxisListType

N_CORES = 8
B = 4
T = 2048
TT = B * T
D = 1024
DT = D // 128

BF16_NP = ml_dtypes.bfloat16



class Sem:

    def __init__(self, h):
        self.h = h


class Prog:
    def __init__(self, nc=None):
        self.nc = nc or bass.Bass("TRN2", target_bir_lowering=False)
        self.stack = ExitStack()
        self.scopes = []
        self._sem_i = 0
        self._free = []
        self._stage_sems = []
        self._sclear = None
        self.stage_idx = 0

    def _alloc_sem(self, name=None):
        self._sem_i += 1
        return self.stack.enter_context(
            self.nc.semaphore(name or f"s{self._sem_i}"))

    def sem(self, name=None):
        h = self._free.pop() if self._free else self._alloc_sem(name)
        self._stage_sems.append(h)
        return Sem(h)

    def begin_stage(self):
        self._free.extend(self._stage_sems)
        self._stage_sems = []
        self.stage_idx += 1
        self._entry_done = False
        if self._sclear is None:
            self._sclear = self._alloc_sem("sclear")

    def emit_entry(self, sync_eng, gate):
        if self._entry_done:
            return
        self._entry_done = True
        if gate is not None:
            sync_eng.wait_ge(gate[0].h, gate[1])
        for h in self._stage_sems:
            sync_eng.sem_clear(h)
        sync_eng.nop().then_inc(self._sclear, 1)

    def entry_wait(self, eng):
        eng.wait_ge(self._sclear, self.stage_idx)

    def push_scope(self):
        self.scopes.append(ExitStack())

    def pop_scope(self):
        self.scopes.pop().close()

    def _scope(self):
        return self.scopes[-1] if self.scopes else self.stack

    def sbuf(self, name, shape, dtype):
        return self._scope().enter_context(
            self.nc.sbuf_tensor(name, shape, dtype))

    def psum(self, name, shape, dtype=F32):
        return self._scope().enter_context(
            self.nc.psum_tensor(name, shape, dtype))

    def dram(self, name, shape, dtype, kind):
        return self.nc.dram_tensor(name, shape, dtype, kind=kind).ap()

    def block(self):
        return self.nc.Block()

    def close(self):
        self.stack.close()
        return self.nc



def emit_ln(p, x_hbm, xn_bf_hbm, xn_f32_hbm=None, tag="ln", gate=None,
            prep_emit=None):
    nc = p.nc
    p.begin_stage()
    p.push_scope()
    C = 512
    NCH = TT // C
    EPS = 1e-5
    wantf32 = xn_f32_hbm is not None
    SO = 16 * DT * (2 if wantf32 else 1)

    xc = [[p.sbuf(f"{tag}_x{j}_{i}", [128, C], F32) for i in range(DT)]
          for j in range(2)]
    xb = [p.sbuf(f"{tag}_xb{i}", [128, C], BF16) for i in range(DT)]
    sq = [p.sbuf(f"{tag}_sq{i}", [128, C], BF16) for i in range(DT)]
    ones_col = p.sbuf(f"{tag}_ones", [128, 1], BF16)
    ones_row = p.sbuf(f"{tag}_onesr", [1, 128], F32)
    rw = p.sbuf(f"{tag}_rows", [1, 6 * C], F32)
    a_b = p.sbuf(f"{tag}_ab", [128, C], F32)
    b_b = p.sbuf(f"{tag}_bb", [128, C], F32)
    tmp = p.sbuf(f"{tag}_tmp", [128, C], F32)
    out_bf = [p.sbuf(f"{tag}_ob{i}", [128, C], BF16) for i in range(DT)]
    out_f = [p.sbuf(f"{tag}_of{i}", [128, C], F32) for i in range(DT)] \
        if wantf32 else None

    ps_sx = p.psum(f"{tag}_psx", [1, C])
    ps_sxx = p.psum(f"{tag}_psxx", [1, C])
    ps_a = p.psum(f"{tag}_pa", [128, C])
    ps_b = p.psum(f"{tag}_pb", [128, C])

    sd2 = [p.sem(), p.sem()]
    ss, st, sv, so = (p.sem() for _ in range(4))
    sdone = p.sem()
    R_M, R_E, R_V, R_SD, R_RS, R_B = (i * C for i in range(6))

    with p.block() as blk:
        @blk.sync
        def _(sync):
            p.emit_entry(sync, gate)
            for c in range(NCH + 1):
                if c < NCH:
                    if c >= 2:
                        sync.wait_ge(sv.h, 3 * (c - 2) + 3)
                    for i in range(DT):
                        sync.dma_start(
                            out=xc[c % 2][i][:, :],
                            in_=x_hbm[i * 128:(i + 1) * 128, c * C:(c + 1) * C],
                        ).then_inc(sd2[c % 2].h, 16)
                if c >= 1:
                    cc = c - 1
                    sync.wait_ge(sv.h, 3 * cc + 3)
                    for i in range(DT):
                        sync.dma_start(
                            out=xn_bf_hbm[i * 128:(i + 1) * 128, cc * C:(cc + 1) * C],
                            in_=out_bf[i][:, :],
                        ).then_inc(so.h, 16)
                        if wantf32:
                            sync.dma_start(
                                out=xn_f32_hbm[i * 128:(i + 1) * 128, cc * C:(cc + 1) * C],
                                in_=out_f[i][:, :],
                            ).then_inc(so.h, 16)
            sync.wait_ge(so.h, NCH * SO)
            sync.nop().then_inc(sdone.h, 1)

        @blk.gpsimd
        def _(g):
            p.entry_wait(g)
            g.memset(ones_col[:, :], 1.0)
            g.memset(ones_row[:, :], 1.0).then_inc(st.h, 1)
            if prep_emit is not None:
                prep_emit(g)

        @blk.scalar
        def _(sc):
            p.entry_wait(sc)
            for c in range(NCH):
                sc.wait_ge(sd2[c % 2].h, (c // 2 + 1) * DT * 16)
                if c >= 1:
                    sc.wait_ge(st.h, 2 * (c - 1) + 2)
                for i in range(DT):
                    sc.copy(xb[i][:, :], xc[c % 2][i][:, :])
                    last = sc.activation(sq[i][:, :], xc[c % 2][i][:, :], AF.Square)
                last.then_inc(ss.h, 1)
                sc.wait_ge(sv.h, 3 * c + 1)
                sc.activation(rw[0:1, R_SD:R_SD + C], rw[0:1, R_V:R_V + C],
                              AF.Sqrt).then_inc(ss.h, 1)

        @blk.tensor
        def _(te):
            p.entry_wait(te)
            te.wait_ge(st.h, 1)
            for c in range(NCH):
                te.wait_ge(ss.h, 2 * c + 1)
                if c >= 1:
                    te.wait_ge(sv.h, 3 * (c - 1) + 1)
                for i in range(DT):
                    te.matmul(ps_sx[:, :], ones_col[:, :], xb[i][:, :],
                              start=(i == 0), stop=(i == DT - 1))
                    m2 = te.matmul(ps_sxx[:, :], ones_col[:, :], sq[i][:, :],
                                   start=(i == 0), stop=(i == DT - 1))
                m2.then_inc(st.h, 1)
                te.wait_ge(sv.h, 3 * c + 2)
                te.matmul(ps_a[:, :], ones_row[:, :], rw[0:1, R_RS:R_RS + C],
                          start=True, stop=True)
                te.matmul(ps_b[:, :], ones_row[:, :], rw[0:1, R_B:R_B + C],
                          start=True, stop=True).then_inc(st.h, 1)

        @blk.vector
        def _(v):
            p.entry_wait(v)
            for c in range(NCH):
                v.wait_ge(st.h, 2 * c + 1 + 1)
                v.tensor_scalar_mul(rw[0:1, R_M:R_M + C], ps_sx[:, :], 1.0 / D)
                v.tensor_scalar_mul(rw[0:1, R_E:R_E + C], ps_sxx[:, :], 1.0 / D)
                v.drain()
                v.tensor_tensor(rw[0:1, R_V:R_V + C], rw[0:1, R_M:R_M + C],
                                rw[0:1, R_M:R_M + C], ALU.mult)
                v.drain()
                v.tensor_tensor(rw[0:1, R_V:R_V + C], rw[0:1, R_E:R_E + C],
                                rw[0:1, R_V:R_V + C], ALU.subtract)
                v.drain()
                v.tensor_scalar_add(rw[0:1, R_V:R_V + C], rw[0:1, R_V:R_V + C],
                                    EPS).then_inc(sv.h, 1)
                v.wait_ge(ss.h, 2 * c + 2)
                v.reciprocal(rw[0:1, R_RS:R_RS + C], rw[0:1, R_SD:R_SD + C])
                v.drain()
                v.scalar_tensor_tensor(
                    rw[0:1, R_B:R_B + C], rw[0:1, R_M:R_M + C], -1.0,
                    rw[0:1, R_RS:R_RS + C], ALU.mult, ALU.mult).then_inc(sv.h, 1)
                v.wait_ge(st.h, 2 * c + 2 + 1)
                v.tensor_copy(a_b[:, :], ps_a[:, :])
                v.tensor_copy(b_b[:, :], ps_b[:, :])
                v.drain()
                if c >= 1:
                    v.wait_ge(so.h, c * SO)
                for i in range(DT):
                    v.tensor_tensor(tmp[:, :], xc[c % 2][i][:, :], a_b[:, :],
                                    ALU.mult)
                    v.drain()
                    last = v.tensor_tensor(out_bf[i][:, :], tmp[:, :], b_b[:, :],
                                           ALU.add)
                    if wantf32:
                        last = v.tensor_tensor(out_f[i][:, :], tmp[:, :],
                                               b_b[:, :], ALU.add)
                    v.drain()
                last.then_inc(sv.h, 1)

    p.pop_scope()
    return (sdone, 1)





def emit_l0_attn(p, xn_bf_hbm, x_hbm, kt_hbm, v_hbm, acc_hbm, tag="l0a",
                 gate=None, prep_emit=None):
    p.begin_stage()
    p.push_scope()
    NP = 4096
    NT = NP // 128
    C = 256
    NCH = TT // C
    ST_C, SS_C, SV_C = NT + 2 + DT, 2 * NT + 1, 2 * NT + 2 + DT

    KT = [p.sbuf(f"{tag}_kt{i}", [128, NP], BF16) for i in range(DT)]
    V = [p.sbuf(f"{tag}_v{j}", [128, D], BF16) for j in range(NT)]
    stag = [p.sbuf(f"{tag}_stag{j}", [128, 2048], F32) for j in range(2)]
    witems = [(KT[i][:, h * 2048:(h + 1) * 2048],
               kt_hbm[i * 128:(i + 1) * 128, h * 2048:(h + 1) * 2048], 2048)
              for i in range(DT) for h in range(2)] +              [(V[j][:, :], v_hbm[j * 128:(j + 1) * 128, :], D)
              for j in range(NT)]
    NWI = len(witems)
    xnc = [p.sbuf(f"{tag}_xn{j}", [128, DT * C], BF16) for j in range(2)]
    xcc = p.sbuf(f"{tag}_xc", [128, DT * C], F32)
    accs = xcc
    aw = p.sbuf(f"{tag}_aw", [128, NT * C], BF16)
    sqb = [p.sbuf(f"{tag}_sq{j}", [128, C], BF16) for j in range(2)]
    gtmp = [p.sbuf(f"{tag}_gt{j}", [128, C], BF16) for j in range(2)]
    s3b = p.sbuf(f"{tag}_s3b", [128, C], F32)
    rw = p.sbuf(f"{tag}_rw", [1, 2 * C], F32)
    ones_col = p.sbuf(f"{tag}_ones", [128, 1], BF16)
    ones_row = p.sbuf(f"{tag}_onesr", [1, 128], F32)

    ps_aw = [p.psum(f"{tag}_paw{j}", [128, C]) for j in range(2)]
    ps_ss = p.psum(f"{tag}_pss", [1, C])
    ps_bc = p.psum(f"{tag}_pbc", [128, C])
    ps_o = [p.psum(f"{tag}_po{j}", [128, C]) for j in range(2)]

    sdw2 = [p.sem(), p.sem()]
    sxn2 = [p.sem(), p.sem()]
    sx, sw, ss, st, sv, so = (p.sem() for _ in range(6))
    sdone = p.sem()

    def fm(hbm, c):
        return hbm[:, c * C:(c + 1) * C].rearrange("(a p) t -> p a t", p=128)

    def fm_sb(tile):
        return tile[:, :].rearrange("p (a t) -> p a t", t=C)

    with p.block() as blk:
        @blk.sync
        def _(sync):
            p.emit_entry(sync, gate)
            for i, (dstt, src, w_) in enumerate(witems):
                if i >= 2:
                    sync.wait_ge(sw.h, i - 1)
                sync.dma_start(out=stag[i % 2][:, 0:w_], in_=src
                               ).then_inc(sdw2[i % 2].h, 16)
            for c in range(NCH):
                if c >= 2:
                    sync.wait_ge(st.h, 1 + (c - 1) * ST_C)
                sync.dma_start(out=fm_sb(xnc[c % 2]), in_=fm(xn_bf_hbm, c)
                               ).then_inc(sxn2[c % 2].h, 16)
                if c >= 1:
                    sync.wait_ge(sv.h, c * SV_C)
                    sync.wait_ge(so.h, c * 16)
                sync.dma_start(out=fm_sb(xcc), in_=fm(x_hbm, c)
                               ).then_inc(sx.h, 16)
                sync.wait_ge(sv.h, (c + 1) * SV_C)
                sync.dma_start(out=fm(acc_hbm, c), in_=fm_sb(accs)
                               ).then_inc(so.h, 16)
            sync.wait_ge(so.h, NCH * 16)
            sync.nop().then_inc(sdone.h, 1)

        @blk.gpsimd
        def _(g):
            if prep_emit is not None:
                p.entry_wait(g)
                prep_emit(g)

        @blk.tensor
        def _(te):
            p.entry_wait(te)
            te.wait_ge(st.h, 1)
            te.wait_ge(sw.h, NWI)
            for c in range(NCH):
                cs, cv, ct = c * SS_C, c * SV_C, 1 + c * ST_C
                te.wait_ge(sxn2[c % 2].h, (c // 2 + 1) * 16)
                for nt in range(NT):
                    if nt >= 2:
                        te.wait_ge(ss.h, cs + nt - 1)
                    elif c >= 1:
                        te.wait_ge(ss.h, (c - 1) * SS_C + NT)
                    for i in range(DT):
                        mm = te.matmul(ps_aw[nt % 2][:, :],
                                       KT[i][:, nt * 128:(nt + 1) * 128],
                                       xnc[c % 2][:, i * C:(i + 1) * C],
                                       start=(i == 0), stop=(i == DT - 1))
                    mm.then_inc(st.h, 1)
                    for rnt in ([nt - 1] if nt >= 1 else []) + ([NT - 1] if nt == NT - 1 else []):
                        if rnt == 0 and c >= 1:
                            te.wait_ge(ss.h, (c - 1) * SS_C + NT + 1)
                        te.wait_ge(sv.h, cv + rnt + 1)
                        mm = te.matmul(ps_ss[:, :], ones_col[:, :],
                                       sqb[rnt % 2][:, :],
                                       start=(rnt == 0), stop=(rnt == NT - 1),
                                       skip_group_check=True)
                        if rnt == NT - 1:
                            mm.then_inc(st.h, 1)
                te.wait_ge(sv.h, cv + NT + 1)
                te.matmul(ps_bc[:, :], ones_row[:, :], rw[0:1, C:2 * C],
                          start=True, stop=True).then_inc(st.h, 1)
                te.wait_ge(ss.h, cs + SS_C)
                for j in range(DT):
                    if j >= 2:
                        te.wait_ge(sv.h, cv + NT + 2 + NT + j - 1)
                    elif c >= 1:
                        te.wait_ge(sv.h, c * SV_C)
                    for nt in range(NT):
                        mm = te.matmul(ps_o[j % 2][:, :],
                                       V[nt][:, j * 128:(j + 1) * 128],
                                       aw[:, nt * C:(nt + 1) * C],
                                       start=(nt == 0), stop=(nt == NT - 1))
                    mm.then_inc(st.h, 1)

        @blk.scalar
        def _(sc):
            p.entry_wait(sc)
            for i, (dstt, src, w_) in enumerate(witems):
                sc.wait_ge(sdw2[i % 2].h, (i // 2 + 1) * 16)
                sc.copy(dstt, stag[i % 2][:, 0:w_]).then_inc(sw.h, 1)
            for c in range(NCH):
                cs, cv, ct = c * SS_C, c * SV_C, 1 + c * ST_C
                for nt in range(NT):
                    sc.wait_ge(st.h, ct + nt + 1)
                    sc.copy(aw[:, nt * C:(nt + 1) * C],
                            ps_aw[nt % 2][:, :]).then_inc(ss.h, 1)
                sc.wait_ge(st.h, ct + NT + 1)
                sc.activation(rw[0:1, 0:C], ps_ss[:, :], AF.Sqrt
                              ).then_inc(ss.h, 1)
                for nt in range(NT):
                    sc.wait_ge(sv.h, cv + NT + 2 + nt + 1)
                    sc.activation(aw[:, nt * C:(nt + 1) * C],
                                  gtmp[nt % 2][:, :], AF.Gelu
                                  ).then_inc(ss.h, 1)

        @blk.vector
        def _(v):
            p.entry_wait(v)
            v.memset(ones_col[:, :], 1.0)
            v.memset(ones_row[:, :], 1.0).then_inc(st.h, 1)
            for c in range(NCH):
                cs, cv, ct = c * SS_C, c * SV_C, 1 + c * ST_C
                for nt in range(NT):
                    v.wait_ge(ss.h, cs + nt + 1)
                    v.tensor_tensor(sqb[nt % 2][:, :],
                                    aw[:, nt * C:(nt + 1) * C],
                                    aw[:, nt * C:(nt + 1) * C], ALU.mult
                                    ).then_inc(sv.h, 1)
                v.wait_ge(ss.h, cs + NT + 1)
                v.reciprocal(rw[0:1, C:2 * C], rw[0:1, 0:C])
                v.drain()
                v.tensor_scalar_mul(rw[0:1, C:2 * C], rw[0:1, C:2 * C],
                                    64.0).then_inc(sv.h, 1)
                v.wait_ge(st.h, ct + NT + 2)
                v.tensor_copy(s3b[:, :], ps_bc[:, :]).then_inc(sv.h, 1)
                v.drain()
                for nt in range(NT):
                    if nt >= 2:
                        v.wait_ge(ss.h, cs + NT + 1 + nt - 1)
                    v.tensor_tensor(gtmp[nt % 2][:, :],
                                    aw[:, nt * C:(nt + 1) * C], s3b[:, :],
                                    ALU.mult).then_inc(sv.h, 1)
                if c >= 1:
                    v.wait_ge(so.h, c * 16)
                v.wait_ge(sx.h, (c + 1) * 16)
                for j in range(DT):
                    v.wait_ge(st.h, ct + NT + 2 + j + 1)
                    v.tensor_tensor(xcc[:, j * C:(j + 1) * C],
                                    xcc[:, j * C:(j + 1) * C],
                                    ps_o[j % 2][:, :], ALU.add
                                    ).then_inc(sv.h, 1)

    p.pop_scope()
    return (sdone, 1)



def emit_router(p, xn_bf_hbm, wr_hbm, br_hbm, wrows_hbm, tag="rt",
                gate=None, prep_emit=None):
    p.begin_stage()
    p.push_scope()
    C = 512
    NCH = TT // C
    KT4 = C // 128

    wrf = p.sbuf(f"{tag}_wrf", [128, DT * 8], F32)
    wrb = p.sbuf(f"{tag}_wrb", [128, DT * 8], BF16)
    brc = p.sbuf(f"{tag}_brc", [8, 1], F32)
    xnc = [p.sbuf(f"{tag}_xn{j}", [128, DT * C], BF16) for j in range(2)]
    lg = p.sbuf(f"{tag}_lg", [8, C], F32)
    wrow = p.sbuf(f"{tag}_wrow", [8, C], F32)
    idp = p.sbuf(f"{tag}_idp", [128, 128], I32)
    idf = p.sbuf(f"{tag}_idf", [128, 128], I32)
    iden = p.sbuf(f"{tag}_iden", [128, 128], F32)
    lt = [p.sbuf(f"{tag}_lt{j}", [128, 8], F32) for j in range(4)]
    small = [p.sbuf(f"{tag}_sm{j}", [128, 8 * 8], F32) for j in range(4)]

    ps_lg = p.psum(f"{tag}_plg", [8, C])
    ps_t = [p.psum(f"{tag}_pt{j}", [128, 8]) for j in range(2)]
    ps_w = [p.psum(f"{tag}_pw{j}", [8, 128]) for j in range(2)]

    sini = p.sem()
    sxn2 = [p.sem(), p.sem()]
    ss, st, sv, so, sg = (p.sem() for _ in range(5))
    sdone = p.sem()
    ST_C, SS_C, SV_C = 1 + 2 * KT4, 1 + KT4, 3 * KT4

    def fm(hbm, c):
        return hbm[:, c * C:(c + 1) * C].rearrange("(a p) t -> p a t", p=128)

    with p.block() as blk:
        @blk.sync
        def _(sync):
            p.emit_entry(sync, gate)
            sync.dma_start(out=wrf[:, :].rearrange("p (a e) -> p a e", e=8),
                           in_=wr_hbm.rearrange("(a p) e -> p a e", p=128)
                           ).then_inc(sini.h, 16)
            sync.dma_start(out=brc[:, :], in_=br_hbm).then_inc(sini.h, 16)
            for c in range(NCH):
                if c >= 2:
                    sync.wait_ge(st.h, 1 + (c - 1) * ST_C)
                sync.dma_start(out=xnc[c % 2][:, :].rearrange(
                    "p (a t) -> p a t", t=C), in_=fm(xn_bf_hbm, c)
                    ).then_inc(sxn2[c % 2].h, 16)
                sync.wait_ge(sv.h, 1 + (c + 1) * SV_C)
                sync.dma_start(out=wrows_hbm[:, c * C:(c + 1) * C],
                               in_=wrow[:, :]).then_inc(so.h, 16)
            sync.wait_ge(so.h, NCH * 16)
            sync.nop().then_inc(sdone.h, 1)

        @blk.gpsimd
        def _(g):
            p.entry_wait(g)
            g.iota(idp[:, :], [[0, 128]], channel_multiplier=1)
            g.iota(idf[:, :], [[1, 128]], channel_multiplier=0).then_inc(sg.h, 1)
            g.wait_ge(sini.h, 32)
            g.tensor_copy(wrb[:, :], wrf[:, :]).then_inc(st.h, 1)
            if prep_emit is not None:
                prep_emit(g)

        @blk.vector
        def _(v):
            p.entry_wait(v)
            v.wait_ge(sg.h, 1)
            v.tensor_tensor(iden[:, :], idp[:, :], idf[:, :], ALU.is_equal
                            ).then_inc(sv.h, 1)
            for c in range(NCH):
                cv, ct = 1 + c * SV_C, 1 + c * ST_C
                for kk in range(KT4):
                    s = small[kk]
                    l = lt[kk]
                    v.wait_ge(st.h, ct + 1 + kk + 1)
                    v.tensor_copy(l[:, :], ps_t[kk % 2][:, :])
                    v.drain()
                    v.reduce_max(s[:, 0:1], l[:, :], AX.X)
                    v.drain()
                    v.tensor_scalar_mul(s[:, 1:2], s[:, 0:1], -1.0)
                    v.tensor_scalar(s[:, 8:16], l[:, :], s[:, 0:1], None,
                                    ALU.is_ge)
                    v.drain()
                    v.scalar_tensor_tensor(s[:, 16:24], s[:, 8:16], -1e30,
                                           l[:, :], ALU.mult, ALU.add)
                    v.drain()
                    v.reduce_max(s[:, 2:3], s[:, 16:24], AX.X)
                    v.drain()
                    v.tensor_scalar(s[:, 24:32], l[:, :], s[:, 2:3], None,
                                    ALU.is_ge).then_inc(sv.h, 1)
                    v.wait_ge(ss.h, c * SS_C + 2 + kk)
                    v.tensor_scalar_add(s[:, 4:5], s[:, 3:4], 1.0)
                    v.drain()
                    v.reciprocal(s[:, 5:6], s[:, 4:5])
                    v.drain()
                    v.tensor_tensor(s[:, 40:48], s[:, 32:40], s[:, 24:32],
                                    ALU.mult)
                    v.drain()
                    v.tensor_scalar(s[:, 48:56], s[:, 40:48], s[:, 5:6], None,
                                    ALU.mult).then_inc(sv.h, 1)
                if c >= 1:
                    v.wait_ge(so.h, c * 16)
                for kk in range(KT4):
                    v.wait_ge(st.h, ct + 1 + KT4 + kk + 1)
                    v.tensor_copy(wrow[:, kk * 128:(kk + 1) * 128],
                                  ps_w[kk % 2][:, :]).then_inc(sv.h, 1)

        @blk.scalar
        def _(sc):
            p.entry_wait(sc)
            sc.wait_ge(sini.h, 32)
            for c in range(NCH):
                cv, ct = 1 + c * SV_C, 1 + c * ST_C
                sc.wait_ge(st.h, ct + 1)
                sc.activation(lg[:, :], ps_lg[:, :], AF.Identity, bias=brc[:, 0:1]
                              ).then_inc(ss.h, 1)
                for kk in range(KT4):
                    s = small[kk]
                    l = lt[kk]
                    sc.wait_ge(sv.h, cv + 2 * kk + 1)
                    sc.activation(s[:, 32:40], l[:, :], AF.Exp, bias=s[:, 1:2])
                    sc.activation(s[:, 3:4], s[:, 2:3], AF.Exp, bias=s[:, 1:2]
                                  ).then_inc(ss.h, 1)

        @blk.tensor
        def _(te):
            p.entry_wait(te)
            te.wait_ge(st.h, 1)
            te.wait_ge(sv.h, 1)
            for c in range(NCH):
                cv, cs, ct = 1 + c * SV_C, 1 + c * SS_C, 1 + c * ST_C
                te.wait_ge(sxn2[c % 2].h, (c // 2 + 1) * 16)
                if c >= 1:
                    te.wait_ge(ss.h, (c - 1) * SS_C + 1)
                for i in range(DT):
                    mm = te.matmul(ps_lg[:, :], wrb[:, i * 8:(i + 1) * 8],
                                   xnc[c % 2][:, i * C:(i + 1) * C],
                                   start=(i == 0), stop=(i == DT - 1))
                mm.then_inc(st.h, 1)
                te.wait_ge(ss.h, c * SS_C + 1)
                for kk in range(KT4):
                    if kk >= 2:
                        te.wait_ge(sv.h, cv + 2 * (kk - 2) + 1)
                    te.transpose(ps_t[kk % 2][:, :],
                                 lg[0:8, kk * 128:(kk + 1) * 128],
                                 iden[0:8, 0:8]).then_inc(st.h, 1)
                for kk in range(KT4):
                    te.wait_ge(sv.h, cv + 2 * kk + 2)
                    if kk >= 2:
                        te.wait_ge(sv.h, cv + 2 * KT4 + (kk - 2) + 1)
                    te.transpose(ps_w[kk % 2][:, :],
                                 small[kk][:, 48:56], iden[:, :]
                                 ).then_inc(st.h, 1)

    p.pop_scope()
    return (sdone, 1)



def emit_wprep(p, pairs, tag="wp", gate=None, blocks=None):
    es = ExitStack()
    sb = lambda n, sh, dt: es.enter_context(p.nc.sbuf_tensor(n, sh, dt))
    stag = [sb(f"{tag}_s{j}", [128, 4096], F32) for j in range(2)]
    stbf = [sb(f"{tag}_b{j}", [128, 4096], BF16) for j in range(2)]
    sd2 = [Sem(p._alloc_sem()), Sem(p._alloc_sem())]
    so2 = [Sem(p._alloc_sem()), Sem(p._alloc_sem())]
    done = Sem(p._alloc_sem())

    if blocks is None:
        blocks = []
        for src, dst in pairs:
            R, Cc = src.shape
            for r in range(R // 128):
                for c0 in range(0, Cc, 4096):
                    w_ = min(4096, Cc - c0)
                    blocks.append((src[r * 128:(r + 1) * 128, c0:c0 + w_],
                                   dst[r * 128:(r + 1) * 128, c0:c0 + w_],
                                   w_, None))
    NB = len(blocks)

    with p.block() as blk:
        @blk.sync
        def _(sync):
            p.emit_entry(sync, gate)
            for i, (src, dst, Cc, _l) in enumerate(blocks):
                if i >= 2:
                    sync.wait_ge(so2[i % 2].h, (i // 2) * 16)
                sync.dma_start(out=stag[i % 2][:, 0:Cc], in_=src
                               ).then_inc(sd2[i % 2].h, 16)
            sync.wait_ge(so2[0].h, ((NB + 1) // 2) * 16)
            sync.wait_ge(so2[1].h, (NB // 2) * 16)
            sync.nop().then_inc(done.h, 1)

        @blk.gpsimd
        def _(g):
            p.entry_wait(g)
            for i, (src, dst, Cc, l_) in enumerate(blocks):
                g.wait_ge(sd2[i % 2].h, (i // 2 + 1) * 16)
                if i >= 2:
                    g.wait_ge(so2[i % 2].h, (i // 2) * 16)
                g.tensor_copy(stbf[i % 2][:, 0:Cc], stag[i % 2][:, 0:Cc])
                g.drain()
                src_sb = stbf[i % 2][:, 0:Cc]
                if l_ is not None:
                    src_sb = src_sb.rearrange("p (g l) -> p g l", l=l_)
                g.dma_start(out=dst, in_=src_sb
                            ).then_inc(so2[i % 2].h, 16)

    es.close()
    return done, so2


class PrepStream:

    def __init__(self, p, stag, stbf):
        self.p = p
        self.stag, self.stbf = stag, stbf
        self.spd2 = [Sem(p._alloc_sem()), Sem(p._alloc_sem())]
        self.spo2 = [Sem(p._alloc_sem()), Sem(p._alloc_sem())]
        self.blocks = []
        self.cur = 0
        self.head = False

    def add(self, blocks):
        self.blocks.extend(blocks)
        n = len(self.blocks)
        return (self.spo2, ((n + 1) // 2) * 16, (n // 2) * 16)

    def emit(self, g, n=None):
        end = len(self.blocks) if n is None else min(self.cur + n,
                                                     len(self.blocks))
        if not self.head and self.cur < end:
            for j in range(min(2, len(self.blocks))):
                src, dst, w_, l_ = self.blocks[j]
                g.dma_start(out=self.stag[j][:, 0:w_], in_=src
                            ).then_inc(self.spd2[j].h, 16)
            self.head = True
        while self.cur < end:
            i = self.cur
            src, dst, w_, l_ = self.blocks[i]
            g.wait_ge(self.spd2[i % 2].h, (i // 2 + 1) * 16)
            if i >= 2:
                g.wait_ge(self.spo2[i % 2].h, (i // 2) * 16)
            g.tensor_copy(self.stbf[i % 2][:, 0:w_],
                          self.stag[i % 2][:, 0:w_])
            g.drain()
            sb_src = self.stbf[i % 2][:, 0:w_]
            if l_ is not None:
                sb_src = sb_src.rearrange("p (g l) -> p g l", l=l_)
            g.dma_start(out=dst, in_=sb_src).then_inc(self.spo2[i % 2].h, 16)
            if i + 2 < len(self.blocks):
                nsrc, _nd, nw, _nl = self.blocks[i + 2]
                g.dma_start(out=self.stag[i % 2][:, 0:nw], in_=nsrc
                            ).then_inc(self.spd2[i % 2].h, 16)
            self.cur += 1


def moe_prep_blocks(w1f_hbm, w2f_hbm, w1bf, w2bf):
    blocks = []
    for e in range(8):
        for r in range(DT):
            row = e * D + r * 128
            for cc in (0, 1024, 2048, 3072):
                blocks.append((w1f_hbm[row:row + 128, cc:cc + 1024],
                               w1bf[row:row + 128, cc:cc + 1024], 1024, None))
        for r in range(4096 // 128):
            row = e * 4096 + r * 128
            blocks.append((w2f_hbm[row:row + 128, :],
                           w2bf[row:row + 128, :], D, None))
    return blocks


def lff_prep_blocks(w1p_ap, w2p_ap, w1bf, w2bf):
    HF, C = 22, 512
    w1bf4 = w1bf.rearrange("p (g a l) -> p g a l", a=DT, l=C)
    blocks = []
    for a in range(DT):
        for g0 in (0, 2, 4, 6, 8):
            blocks.append((w1p_ap[a * 128:(a + 1) * 128, g0 * C:(g0 + 2) * C],
                           w1bf4[:, g0:g0 + 2, a, :], 1024, C))
        blocks.append((w1p_ap[a * 128:(a + 1) * 128, 10 * C:11 * C],
                       w1bf4[:, 10:11, a, :], C, C))
    for ftt in range(HF):
        blocks.append((w2p_ap[ftt * 128:(ftt + 1) * 128, :],
                       w2bf[:, ftt * D:(ftt + 1) * D], D, None))
    return blocks



def emit_moewp(p, w1f_hbm, w2f_hbm, w1bf, w2bf, stag, stbf, tag="mwp",
               experts=range(8), prior=None):
    spd2 = [Sem(p._alloc_sem()), Sem(p._alloc_sem())]
    spo2 = [Sem(p._alloc_sem()), Sem(p._alloc_sem())]
    DFF = 4096
    blocks = []
    for e in experts:
        for r in range(DT):
            row = e * D + r * 128
            for cc in (0, 1024, 2048, 3072):
                blocks.append((w1f_hbm[row:row + 128, cc:cc + 1024],
                               w1bf[row:row + 128, cc:cc + 1024], 1024))
        for r in range(DFF // 128):
            row = e * DFF + r * 128
            blocks.append((w2f_hbm[row:row + 128, :],
                           w2bf[row:row + 128, :], D))
    assert len(blocks) == 64 * len(list(experts))

    with p.block() as blk:
        @blk.gpsimd
        def _(g):
            for sem, tgt in (prior or []):
                g.wait_ge(sem.h, tgt)
            g.dma_start(out=stag[0][:, 0:blocks[0][2]], in_=blocks[0][0]
                        ).then_inc(spd2[0].h, 16)
            g.dma_start(out=stag[1][:, 0:blocks[1][2]], in_=blocks[1][0]
                        ).then_inc(spd2[1].h, 16)
            for i, (src, dst, w_) in enumerate(blocks):
                g.wait_ge(spd2[i % 2].h, (i // 2 + 1) * 16)
                if i >= 2:
                    g.wait_ge(spo2[i % 2].h, (i // 2) * 16)
                g.tensor_copy(stbf[i % 2][:, 0:w_], stag[i % 2][:, 0:w_])
                g.drain()
                g.dma_start(out=dst, in_=stbf[i % 2][:, 0:w_]
                            ).then_inc(spo2[i % 2].h, 16)
                if i + 2 < len(blocks):
                    nsrc, ndst, nw = blocks[i + 2]
                    g.dma_start(out=stag[i % 2][:, 0:nw], in_=nsrc
                                ).then_inc(spd2[i % 2].h, 16)
    return spo2



def emit_moe(p, xn_bf_hbm, wrows_hbm, prep, b1c_hbm, b2c_hbm,
             acc_in_hbm, x1_hbm, tag="moe", gate=None, prep_emit=None):
    p.begin_stage()
    p.push_scope()
    E = 8
    DFF = 4096
    FT = DFF // 128
    C = 512
    NTC = TT // C
    ST_C, SS_C, SV_C = FT + DT, FT, DT

    y_scr = p.dram(f"{tag}_yscr", [D, TT], F32, kind="Internal")

    sini, sxn, sacc, swb = (p.sem() for _ in range(4))
    sw, ss, st, sv, so = (p.sem() for _ in range(5))
    sdone = p.sem()
    w1bf, w2bf, wpso2 = prep

    WA = [p.sbuf(f"{tag}_wa{i}", [128, DFF], BF16) for i in range(DT)]
    WB = [p.sbuf(f"{tag}_wb{i}", [128, D], BF16) for i in range(FT)]
    hsb = p.sbuf(f"{tag}_h", [128, FT * C], BF16)
    xnc = p.sbuf(f"{tag}_xn", [128, DT * C], BF16)
    acc = p.sbuf(f"{tag}_acc", [128, DT * C], F32)
    wbc = p.sbuf(f"{tag}_wbc", [128, C], F32)
    tmp = p.sbuf(f"{tag}_tmp", [128, C], F32)
    b1c = p.sbuf(f"{tag}_b1c", [128, E * FT], F32)
    b2c = p.sbuf(f"{tag}_b2c", [128, E * DT], F32)

    ps_h = [p.psum(f"{tag}_ph{j}", [128, C]) for j in range(2)]
    ps_y = [p.psum(f"{tag}_py{j}", [128, C]) for j in range(2)]


    def fm(hbm, c):
        return hbm[:, c * C:(c + 1) * C].rearrange("(a p) t -> p a t", p=128)

    def sb3(tile, w=C):
        return tile[:, :].rearrange("p (a t) -> p a t", t=w)

    with p.block() as blk:
        @blk.sync
        def _(sync):
            p.emit_entry(sync, gate)
            sync.dma_start(out=b1c[:, :], in_=b1c_hbm).then_inc(sini.h, 16)
            sync.dma_start(out=b2c[:, :], in_=b2c_hbm).then_inc(sini.h, 16)
            for e in range(E):
                for _sp, _fn in wpso2:
                    _t = _fn(e)
                    if _t > 0:
                        sync.wait_ge(_sp[0].h, _t)
                        sync.wait_ge(_sp[1].h, _t)
                if e >= 1:
                    sync.wait_ge(st.h, e * NTC * ST_C)
                for i in range(DT):
                    sync.dma_start(
                        out=WA[i][:, :],
                        in_=w1bf[(e * DT + i) * 128:(e * DT + i + 1) * 128, :]
                        ).then_inc(sw.h, 16)
                for f in range(FT):
                    sync.dma_start(
                        out=WB[f][:, :],
                        in_=w2bf[(e * FT + f) * 128:(e * FT + f + 1) * 128, :]
                        ).then_inc(sw.h, 16)
                for tc in range(NTC):
                    gi = e * NTC + tc
                    if gi >= 1:
                        sync.wait_ge(st.h, (gi - 1) * ST_C + FT)
                    sync.dma_start(out=sb3(xnc), in_=fm(xn_bf_hbm, tc)
                                   ).then_inc(sxn.h, 16)
                    if gi >= 1:
                        sync.wait_ge(so.h, gi * 16)
                    src = acc_in_hbm if e == 0 else y_scr
                    sync.dma_start(out=sb3(acc), in_=fm(src, tc)
                                   ).then_inc(sacc.h, 16)
                    if gi >= 1:
                        sync.wait_ge(sv.h, gi * SV_C)
                    sync.dma_start(
                        out=wbc[:, :],
                        in_=wrows_hbm[e:e + 1, tc * C:(tc + 1) * C]
                        .partition_broadcast(128)).then_inc(swb.h, 16)
                    dst = x1_hbm if e == E - 1 else y_scr
                    sync.wait_ge(sv.h, (gi + 1) * SV_C)
                    sync.dma_start(out=fm(dst, tc), in_=sb3(acc)
                                   ).then_inc(so.h, 16)
            sync.wait_ge(so.h, E * NTC * 16)
            sync.nop().then_inc(sdone.h, 1)

        @blk.gpsimd
        def _(g):
            if prep_emit is not None:
                p.entry_wait(g)
                prep_emit(g)

        @blk.tensor
        def _(te):
            p.entry_wait(te)
            for e in range(E):
                te.wait_ge(sw.h, (e + 1) * 640)
                for tc in range(NTC):
                    gi = e * NTC + tc
                    ct = gi * ST_C
                    te.wait_ge(sxn.h, (gi + 1) * 16)
                    for f in range(FT):
                        if f >= 2:
                            te.wait_ge(ss.h, gi * SS_C + f - 1)
                        elif gi >= 1:
                            te.wait_ge(ss.h, gi * SS_C)
                        for i in range(DT):
                            mm = te.matmul(ps_h[f % 2][:, :],
                                           WA[i][:, f * 128:(f + 1) * 128],
                                           xnc[:, i * C:(i + 1) * C],
                                           start=(i == 0), stop=(i == DT - 1))
                        mm.then_inc(st.h, 1)
                    te.wait_ge(ss.h, (gi + 1) * SS_C)
                    for dt in range(DT):
                        if dt >= 2:
                            te.wait_ge(sv.h, gi * SV_C + dt - 1)
                        elif gi >= 1:
                            te.wait_ge(sv.h, gi * SV_C)
                        for f in range(FT):
                            mm = te.matmul(ps_y[dt % 2][:, :],
                                           WB[f][:, dt * 128:(dt + 1) * 128],
                                           hsb[:, f * C:(f + 1) * C],
                                           start=(f == 0), stop=(f == FT - 1))
                        mm.then_inc(st.h, 1)

        @blk.scalar
        def _(sc):
            p.entry_wait(sc)
            sc.wait_ge(sini.h, 32)
            for e in range(E):
                for tc in range(NTC):
                    gi = e * NTC + tc
                    ct = gi * ST_C
                    for f in range(FT):
                        sc.wait_ge(st.h, ct + f + 1)
                        sc.activation(hsb[:, f * C:(f + 1) * C],
                                      ps_h[f % 2][:, :], AF.Relu,
                                      bias=b1c[:, e * FT + f:e * FT + f + 1]
                                      ).then_inc(ss.h, 1)

        @blk.vector
        def _(v):
            p.entry_wait(v)
            v.wait_ge(sini.h, 32)
            for e in range(E):
                for tc in range(NTC):
                    gi = e * NTC + tc
                    ct = gi * ST_C
                    v.wait_ge(sacc.h, (gi + 1) * 16)
                    v.wait_ge(swb.h, (gi + 1) * 16)
                    for dt in range(DT):
                        v.wait_ge(st.h, ct + FT + dt + 1)
                        v.scalar_tensor_tensor(
                            tmp[:, :], ps_y[dt % 2][:, :],
                            b2c[:, e * DT + dt:e * DT + dt + 1], wbc[:, :],
                            ALU.add, ALU.mult)
                        v.drain()
                        v.tensor_tensor(acc[:, dt * C:(dt + 1) * C],
                                        acc[:, dt * C:(dt + 1) * C],
                                        tmp[:, :], ALU.add)
                        v.drain()
                        v.engine_nop().then_inc(sv.h, 1)

    p.pop_scope()
    return (sdone, 1)



def emit_l1_attn(p, xn1f_hbm, x1_hbm, wkT_hbm, wvT_hbm, wrT_hbm, woT_hbm,
                 params_hbm, acc1_hbm, tag="l1a", gate=None):
    p.begin_stage()
    p.push_scope()
    C = 512
    NCH = getattr(p, "l1a_nch", TT // C)
    PSEQ = T // C
    ST_C, SS_C, SV_C = 3 * DT + DT, DT, 1 + DT + DT

    wbfs = {}
    for nm in ("wk", "wv", "wr", "wo"):
        wbfs[nm] = [p.sbuf(f"{tag}_{nm}{i}", [128, D], BF16) for i in range(DT)]
    xnf = p.sbuf(f"{tag}_xnf", [128, DT * (C + 1)], F32)
    kin = p.sbuf(f"{tag}_kin", [128, DT * C], BF16)
    vin = p.sbuf(f"{tag}_vin", [128, DT * C], BF16)
    rin = p.sbuf(f"{tag}_rin", [128, DT * C], BF16)
    dif = p.sbuf(f"{tag}_dif", [128, C], F32)
    ekb = [p.sbuf(f"{tag}_ek{j}", [128, C], F32) for j in range(2)]
    vvb = [p.sbuf(f"{tag}_vv{j}", [128, C], F32) for j in range(2)]
    rsb = [p.sbuf(f"{tag}_rs{j}", [128, C], BF16) for j in range(2)]
    ekvb = p.sbuf(f"{tag}_ekv", [128, C], F32)
    numb = p.sbuf(f"{tag}_num", [128, DT * (C + 1)], F32)
    denb = p.sbuf(f"{tag}_den", [128, DT * (C + 1)], F32)
    numer = p.sbuf(f"{tag}_numer", [128, C], F32)
    denom = p.sbuf(f"{tag}_denom", [128, C], F32)
    wkvb = p.sbuf(f"{tag}_wkv", [128, C], F32)
    rcpb = p.sbuf(f"{tag}_rcp", [128, C], F32)
    rwkv = p.sbuf(f"{tag}_rwkv", [128, DT * C], BF16)
    x1c = p.sbuf(f"{tag}_x1c", [128, DT * C], F32)
    prm = p.sbuf(f"{tag}_prm", [128, 48], F32)
    drv = p.sbuf(f"{tag}_drv", [128, 16], F32)

    ps_k = [p.psum(f"{tag}_pk{j}", [128, C]) for j in range(2)]
    ps_v = [p.psum(f"{tag}_pv{j}", [128, C]) for j in range(2)]
    ps_r = [p.psum(f"{tag}_pr{j}", [128, C]) for j in range(2)]
    ps_o = [p.psum(f"{tag}_po{j}", [128, C]) for j in range(2)]

    sini, sxnf, sx1 = (p.sem() for _ in range(3))
    sdw2 = [p.sem(), p.sem()]
    sw, ss, st, sv, so = (p.sem() for _ in range(5))
    sdone = p.sem()

    def xn3(c0, w):
        return xn1f_hbm[:, c0:c0 + w].rearrange("(a p) t -> p a t", p=128)

    def fm(hbm, c):
        return hbm[:, c * C:(c + 1) * C].rearrange("(a p) t -> p a t", p=128)

    def sb3(tile, w=C):
        return tile[:, :].rearrange("p (a t) -> p a t", t=w)

    es = ExitStack()
    stag = [es.enter_context(p.nc.sbuf_tensor(f"{tag}_st{j}", [128, D], F32))
            for j in range(2)]
    wl = [(wbfs[nm][i],
           {"wk": wkT_hbm, "wv": wvT_hbm, "wr": wrT_hbm, "wo": woT_hbm}[nm]
           [i * 128:(i + 1) * 128, :])
          for nm in ("wk", "wv", "wr", "wo") for i in range(DT)]
    NW = len(wl)

    with p.block() as blk:
        @blk.sync
        def _(sync):
            p.emit_entry(sync, gate)
            sync.dma_start(out=prm[:, :], in_=params_hbm).then_inc(sini.h, 16)
            for i, (dstt, src) in enumerate(wl):
                if i >= 2:
                    sync.wait_ge(sw.h, i - 1)
                sync.dma_start(out=stag[i % 2][:, :], in_=src
                               ).then_inc(sdw2[i % 2].h, 16)
            for c in range(NCH):
                seq, tcs = c // PSEQ, c % PSEQ
                c0 = c * C
                if c >= 1:
                    sync.wait_ge(sv.h, (c - 1) * SV_C + 1)
                if tcs == 0:
                    sync.dma_start(out=sb3(xnf, C + 1)[:, :, 1:C + 1],
                                   in_=xn3(c0, C)).then_inc(sxnf.h, 16)
                else:
                    sync.dma_start(out=sb3(xnf, C + 1),
                                   in_=xn3(c0 - 1, C + 1)).then_inc(sxnf.h, 16)
                if c >= 1:
                    sync.wait_ge(so.h, c * 16)
                sync.dma_start(out=sb3(x1c), in_=fm(x1_hbm, c)
                               ).then_inc(sx1.h, 16)
                sync.wait_ge(sv.h, (c + 1) * SV_C)
                sync.dma_start(out=fm(acc1_hbm, c), in_=sb3(x1c)
                               ).then_inc(so.h, 16)
            sync.wait_ge(so.h, NCH * 16)
            sync.nop().then_inc(sdone.h, 1)

        @blk.gpsimd
        def _(g):
            p.entry_wait(g)
            for i in range(NW):
                g.wait_ge(sdw2[i % 2].h, (i // 2 + 1) * 16)
                g.tensor_copy(wl[i][0][:, :], stag[i % 2][:, :]
                              ).then_inc(sw.h, 1)

        @blk.scalar
        def _(sc):
            p.entry_wait(sc)
            sc.wait_ge(sini.h, 16)
            sc.activation(drv[:, 0:8], prm[:, 24:32], AF.Exp)
            sc.drain()
            sc.activation(drv[:, 0:8], drv[:, 0:8], AF.Exp, scale=-1.0)
            sc.activation(drv[:, 8:16], prm[:, 32:40], AF.Exp).then_inc(ss.h, 1)
            SB = 1
            for c in range(NCH):
                cs, ct = SB + c * SS_C, c * ST_C
                for a in range(DT):
                    sc.wait_ge(st.h, ct + 3 * a + 3)
                    if a >= 2:
                        sc.wait_ge(sv.h, 1 + c * SV_C + 1 + (a - 2) + 1)
                    elif c >= 1:
                        sc.wait_ge(sv.h, 1 + (c - 1) * SV_C + 1 + DT)
                    sc.activation(ekb[a % 2][:, :], ps_k[a % 2][:, :], AF.Exp)
                    sc.copy(vvb[a % 2][:, :], ps_v[a % 2][:, :])
                    sc.activation(rsb[a % 2][:, :], ps_r[a % 2][:, :],
                                  AF.Sigmoid).then_inc(ss.h, 1)

        @blk.tensor
        def _(te):
            p.entry_wait(te)
            te.wait_ge(sw.h, NW)
            for c in range(NCH):
                cv, cs, ct = 1 + c * SV_C, 1 + c * SS_C, c * ST_C
                te.wait_ge(sv.h, cv)
                for a in range(DT):
                    if a >= 2:
                        te.wait_ge(ss.h, cs + a - 1)
                    elif c >= 1:
                        te.wait_ge(ss.h, 1 + (c - 1) * SS_C + DT)
                    for i in range(DT):
                        mm = te.matmul(ps_k[a % 2][:, :],
                                       wbfs["wk"][i][:, a * 128:(a + 1) * 128],
                                       kin[:, i * C:(i + 1) * C],
                                       start=(i == 0), stop=(i == DT - 1))
                    mm.then_inc(st.h, 1)
                    for i in range(DT):
                        mm = te.matmul(ps_v[a % 2][:, :],
                                       wbfs["wv"][i][:, a * 128:(a + 1) * 128],
                                       vin[:, i * C:(i + 1) * C],
                                       start=(i == 0), stop=(i == DT - 1))
                    mm.then_inc(st.h, 1)
                    for i in range(DT):
                        mm = te.matmul(ps_r[a % 2][:, :],
                                       wbfs["wr"][i][:, a * 128:(a + 1) * 128],
                                       rin[:, i * C:(i + 1) * C],
                                       start=(i == 0), stop=(i == DT - 1))
                    mm.then_inc(st.h, 1)
                te.wait_ge(sv.h, cv + DT)
                for m in range(DT):
                    if m >= 2:
                        te.wait_ge(sv.h, cv + 7 + m)
                    elif c >= 1:
                        te.wait_ge(sv.h, c * SV_C)
                    for i in range(DT):
                        mm = te.matmul(ps_o[m % 2][:, :],
                                       wbfs["wo"][i][:, m * 128:(m + 1) * 128],
                                       rwkv[:, i * C:(i + 1) * C],
                                       start=(i == 0), stop=(i == DT - 1))
                    mm.then_inc(st.h, 1)

        @blk.vector
        def _(v):
            p.entry_wait(v)
            v.wait_ge(ss.h, 1)
            for c in range(NCH):
                seq, tcs = c // PSEQ, c % PSEQ
                cv, cs, ct = 1 + c * SV_C, 1 + c * SS_C, c * ST_C
                v.drain()
                for a in range(DT):
                    if tcs == 0:
                        v.memset(numb[:, a * (C + 1):a * (C + 1) + 1], 0.0)
                        v.memset(denb[:, a * (C + 1):a * (C + 1) + 1], 0.0)
                    else:
                        v.tensor_copy(numb[:, a * (C + 1):a * (C + 1) + 1],
                                      numb[:, a * (C + 1) + C:a * (C + 1) + C + 1])
                        v.tensor_copy(denb[:, a * (C + 1):a * (C + 1) + 1],
                                      denb[:, a * (C + 1) + C:a * (C + 1) + C + 1])
                v.drain()
                v.wait_ge(sxnf.h, (c + 1) * 16)
                if c >= 1:
                    v.wait_ge(st.h, (c - 1) * ST_C + 3 * DT)
                if tcs == 0:
                    for a in range(DT):
                        v.memset(xnf[:, a * (C + 1):a * (C + 1) + 1], 0.0)
                    v.drain()
                for a in range(DT):
                    cur = xnf[:, a * (C + 1) + 1:a * (C + 1) + C + 1]
                    sh = xnf[:, a * (C + 1):a * (C + 1) + C]
                    v.tensor_tensor(dif[:, :], cur, sh, ALU.subtract)
                    v.drain()
                    v.scalar_tensor_tensor(kin[:, a * C:(a + 1) * C], dif[:, :],
                                           prm[:, a:a + 1], sh,
                                           ALU.mult, ALU.add)
                    v.scalar_tensor_tensor(vin[:, a * C:(a + 1) * C], dif[:, :],
                                           prm[:, 8 + a:9 + a], sh,
                                           ALU.mult, ALU.add)
                    v.scalar_tensor_tensor(rin[:, a * C:(a + 1) * C], dif[:, :],
                                           prm[:, 16 + a:17 + a], sh,
                                           ALU.mult, ALU.add)
                    v.drain()
                v.engine_nop().then_inc(sv.h, 1)
                for a in range(DT):
                    o = a * (C + 1)
                    v.wait_ge(ss.h, cs + a + 1)
                    v.tensor_tensor(ekvb[:, :], ekb[a % 2][:, :],
                                    vvb[a % 2][:, :], ALU.mult)
                    v.drain()
                    v.tensor_tensor_scan(
                        numb[:, o + 1:o + C + 1],
                        drv[:, a:a + 1].broadcast_to([128, C]), ekvb[:, :],
                        numb[:, o:o + 1], ALU.mult, ALU.add)
                    v.tensor_tensor_scan(
                        denb[:, o + 1:o + C + 1],
                        drv[:, a:a + 1].broadcast_to([128, C]),
                        ekb[a % 2][:, :],
                        denb[:, o:o + 1], ALU.mult, ALU.add)
                    v.drain()
                    v.scalar_tensor_tensor(numer[:, :], ekvb[:, :],
                                           drv[:, 8 + a:9 + a],
                                           numb[:, o:o + C],
                                           ALU.mult, ALU.add)
                    v.scalar_tensor_tensor(denom[:, :], ekb[a % 2][:, :],
                                           drv[:, 8 + a:9 + a],
                                           denb[:, o:o + C],
                                           ALU.mult, ALU.add)
                    v.drain()
                    v.reciprocal(rcpb[:, :], denom[:, :])
                    v.drain()
                    v.tensor_tensor(wkvb[:, :], numer[:, :], rcpb[:, :],
                                    ALU.mult)
                    v.drain()
                    v.tensor_tensor(rwkv[:, a * C:(a + 1) * C], wkvb[:, :],
                                    rsb[a % 2][:, :], ALU.mult
                                    ).then_inc(sv.h, 1)
                v.wait_ge(sx1.h, (c + 1) * 16)
                for m in range(DT):
                    v.wait_ge(st.h, ct + 3 * DT + m + 1)
                    v.scalar_tensor_tensor(x1c[:, m * C:(m + 1) * C],
                                           ps_o[m % 2][:, :],
                                           prm[:, 40 + m:41 + m],
                                           x1c[:, m * C:(m + 1) * C],
                                           ALU.add, ALU.add
                                           ).then_inc(sv.h, 1)

    es.close()
    p.pop_scope()
    return (sdone, 1)



def emit_lff_tables(p, qs_col, ks_col, Aq, Bq, Ak, Bk, Rbf, masks, sg, tag,
                    wait=None, gate=None):
    p.push_scope()
    scr = p.sbuf(f"{tag}_scr", [128, 8 * T], F32)
    col = p.sbuf(f"{tag}_col", [128, 8], F32)
    i128 = p.sbuf(f"{tag}_i128", [128, 1024], F32)
    S_T0, S_TH, S_SIN, S_COS, S_SC, S_T1, S_T2, S_FR = (i * T for i in range(8))
    TWO_PI = float(2.0 * math.pi)

    with p.block() as blk:
        @blk.gpsimd
        def _(g):
            p.entry_wait(g)
            g.iota(scr[:, S_T0:S_T0 + T], [[1, T]], channel_multiplier=0,
                   allow_small_or_imprecise_dtypes=True)
            g.iota(col[:, 0:1], [[0, 1]], channel_multiplier=1,
                   allow_small_or_imprecise_dtypes=True)
            g.iota(i128[:, 0:1024], [[1, 1024]], channel_multiplier=0,
                   allow_small_or_imprecise_dtypes=True)
            g.engine_nop().then_inc(sg.h, 1)

        @blk.vector
        def _(v):
            p.entry_wait(v)
            v.wait_ge(sg.h, 1)
            v.tensor_single_scalar(col[:, 1:2], col[:, 0:1], 64.0, ALU.is_ge)
            v.drain()
            v.scalar_tensor_tensor(col[:, 1:2], col[:, 1:2], -64.0,
                                   col[:, 0:1], ALU.mult, ALU.add)
            v.drain()
            v.tensor_single_scalar(col[:, 2:3], col[:, 1:2], 32.0, ALU.is_ge)
            v.drain()
            v.scalar_tensor_tensor(col[:, 2:3], col[:, 2:3], -32.0,
                                   col[:, 1:2], ALU.mult, ALU.add)
            v.drain()
            v.tensor_scalar_mul(col[:, 2:3], col[:, 2:3], 2.0)
            v.drain()
            v.tensor_scalar_mul(col[:, 3:4], col[:, 2:3],
                                -math.log(10000.0) / 64.0)
            v.tensor_scalar(col[:, 4:5], col[:, 2:3], 1.0 / 89.6, 25.6 / 89.6,
                            ALU.mult, ALU.add)
            v.tensor_single_scalar(col[:, 5:6], col[:, 1:2], 32.0, ALU.is_ge)
            v.drain()
            v.tensor_scalar(col[:, 5:6], col[:, 5:6], 2.0, -1.0,
                            ALU.mult, ALU.add)
            v.drain()
            v.engine_nop().then_inc(sg.h, 1)
            v.wait_ge(sg.h, 3)
            v.tensor_copy(scr[:, S_SC:S_SC + T], scr[:, S_T0:S_T0 + T])
            v.tensor_scalar(scr[:, S_FR:S_FR + T], scr[:, S_T0:S_T0 + T],
                            col[:, 3:4], None, ALU.mult)
            v.drain()
            v.tensor_scalar_mul(scr[:, S_T1:S_T1 + T], scr[:, S_FR:S_FR + T],
                                1.0 / TWO_PI)
            v.drain()
            v.engine_nop().then_inc(sg.h, 1)
            v.wait_ge(sg.h, 5)
            v.tensor_tensor(scr[:, S_T1:S_T1 + T], scr[:, S_T1:S_T1 + T],
                            scr[:, S_T2:S_T2 + T], ALU.subtract)
            v.drain()
            v.tensor_scalar_mul(scr[:, S_TH:S_TH + T], scr[:, S_T1:S_T1 + T],
                                TWO_PI)
            v.drain()
            v.tensor_single_scalar(scr[:, S_T1:S_T1 + T],
                                   scr[:, S_TH:S_TH + T], math.pi, ALU.is_ge)
            v.drain()
            v.scalar_tensor_tensor(scr[:, S_TH:S_TH + T],
                                   scr[:, S_T1:S_T1 + T], -TWO_PI,
                                   scr[:, S_TH:S_TH + T], ALU.mult, ALU.add)
            v.drain()
            v.tensor_scalar(scr[:, S_T1:S_T1 + T], scr[:, S_TH:S_TH + T],
                            -1.0, math.pi / 2.0, ALU.mult, ALU.add)
            v.drain()
            v.tensor_single_scalar(scr[:, S_T2:S_T2 + T],
                                   scr[:, S_T1:S_T1 + T], math.pi, ALU.is_ge)
            v.drain()
            v.scalar_tensor_tensor(scr[:, S_T1:S_T1 + T],
                                   scr[:, S_T2:S_T2 + T], -TWO_PI,
                                   scr[:, S_T1:S_T1 + T], ALU.mult, ALU.add)
            v.drain()
            v.tensor_scalar(scr[:, S_SC:S_SC + T], scr[:, S_SC:S_SC + T],
                            1.0 / 256.0, -4.0, ALU.mult, ALU.add)
            v.drain()
            v.tensor_scalar(scr[:, S_SC:S_SC + T], scr[:, S_SC:S_SC + T],
                            col[:, 4:5], None, ALU.mult)
            v.drain()
            v.engine_nop().then_inc(sg.h, 1)
            v.wait_ge(sg.h, 7)
            if wait is not None:
                v.wait_ge(wait[0].h, wait[1])
            v.tensor_tensor(scr[:, S_T1:S_T1 + T], scr[:, S_COS:S_COS + T],
                            scr[:, S_SC:S_SC + T], ALU.mult)
            v.tensor_tensor(scr[:, S_T2:S_T2 + T], scr[:, S_SIN:S_SIN + T],
                            scr[:, S_SC:S_SC + T], ALU.mult)
            v.drain()
            v.tensor_scalar(Aq[:, :], scr[:, S_T1:S_T1 + T], qs_col[:, 0:1],
                            None, ALU.mult)
            v.tensor_scalar(Bq[:, :], scr[:, S_T2:S_T2 + T],
                            qs_col[:, 1:2], None, ALU.mult)
            v.drain()
            v.reciprocal(scr[:, S_T1:S_T1 + T], scr[:, S_SC:S_SC + T])
            v.drain()
            v.tensor_tensor(scr[:, S_T2:S_T2 + T], scr[:, S_COS:S_COS + T],
                            scr[:, S_T1:S_T1 + T], ALU.mult)
            v.drain()
            v.tensor_scalar(Ak[:, :], scr[:, S_T2:S_T2 + T], ks_col[:, 0:1],
                            None, ALU.mult)
            v.drain()
            v.tensor_tensor(scr[:, S_T2:S_T2 + T], scr[:, S_SIN:S_SIN + T],
                            scr[:, S_T1:S_T1 + T], ALU.mult)
            v.drain()
            v.tensor_scalar(Bk[:, :], scr[:, S_T2:S_T2 + T],
                            ks_col[:, 1:2], None, ALU.mult)
            v.drain()
            v.tensor_single_scalar(col[:, 7:8], col[:, 1:2], 32.0, ALU.is_ge)
            v.drain()
            v.tensor_scalar(col[:, 6:7], col[:, 7:8], -2.0, 1.0,
                            ALU.mult, ALU.add)
            v.tensor_scalar(col[:, 5:6], col[:, 7:8], -1.0, 1.0,
                            ALU.mult, ALU.add)
            v.tensor_scalar(scr[:, S_T1:S_T1 + 128], i128[:, 0:128],
                            col[:, 0:1], None, ALU.subtract)
            v.drain()
            v.tensor_single_scalar(scr[:, S_T2:S_T2 + 128],
                                   scr[:, S_T1:S_T1 + 128], 32.0,
                                   ALU.is_equal)
            v.tensor_single_scalar(scr[:, S_T2 + 128:S_T2 + 256],
                                   scr[:, S_T1:S_T1 + 128], -32.0,
                                   ALU.is_equal)
            v.drain()
            v.tensor_scalar(scr[:, S_T2:S_T2 + 128], scr[:, S_T2:S_T2 + 128],
                            col[:, 5:6], None, ALU.mult)
            v.tensor_scalar(scr[:, S_T2 + 128:S_T2 + 256],
                            scr[:, S_T2 + 128:S_T2 + 256],
                            col[:, 7:8], None, ALU.mult)
            v.drain()
            v.tensor_tensor(scr[:, S_T1:S_T1 + 128], scr[:, S_T2:S_T2 + 128],
                            scr[:, S_T2 + 128:S_T2 + 256], ALU.add)
            v.drain()
            v.tensor_scalar(Rbf[:, :], scr[:, S_T1:S_T1 + 128], col[:, 6:7],
                            None, ALU.mult)
            v.tensor_scalar(i128[:, 0:1024], i128[:, 0:1024], col[:, 0:1],
                            None, ALU.subtract)
            v.drain()
            for a in range(4):
                v.tensor_single_scalar(scr[:, S_T1:S_T1 + 1024],
                                       i128[:, 0:1024], 128.0 * a,
                                       ALU.subtract)
                v.drain()
                v.tensor_single_scalar(scr[:, S_T2:S_T2 + 1024],
                                       scr[:, S_T1:S_T1 + 1024], 0.0,
                                       ALU.is_ge)
                v.tensor_single_scalar(scr[:, S_T1:S_T1 + 1024],
                                       scr[:, S_T1:S_T1 + 1024], 512.0,
                                       ALU.is_le)
                v.drain()
                v.tensor_tensor(scr[:, S_T2:S_T2 + 1024],
                                scr[:, S_T2:S_T2 + 1024],
                                scr[:, S_T1:S_T1 + 1024], ALU.mult)
                v.drain()
                v.tensor_scalar(masks[a][:, :], scr[:, S_T2:S_T2 + 1024],
                                1e30, -1e30, ALU.mult, ALU.add)
                v.drain()
            v.engine_nop().then_inc(sg.h, 1)

        @blk.scalar
        def _(sc):
            p.entry_wait(sc)
            sc.wait_ge(sg.h, 2)
            sc.activation(col[:, 3:4], col[:, 3:4], AF.Exp)
            sc.activation(col[:, 4:5], col[:, 4:5], AF.Ln)
            sc.drain()
            sc.nop().then_inc(sg.h, 1)
            sc.wait_ge(sg.h, 4)
            sc.copy(scr[:, S_T2:S_T2 + T].bitcast(I32), scr[:, S_T1:S_T1 + T])
            sc.drain()
            sc.copy(scr[:, S_T2:S_T2 + T],
                    scr[:, S_T2:S_T2 + T].bitcast(I32))
            sc.drain()
            sc.nop().then_inc(sg.h, 1)
            sc.wait_ge(sg.h, 6)
            sc.activation(scr[:, S_SIN:S_SIN + T], scr[:, S_TH:S_TH + T],
                          AF.Sin)
            sc.activation(scr[:, S_COS:S_COS + T], scr[:, S_T1:S_T1 + T],
                          AF.Sin)
            sc.activation(scr[:, S_SC:S_SC + T], scr[:, S_SC:S_SC + T],
                          AF.Exp)
            sc.drain()
            sc.nop().then_inc(sg.h, 1)
    p.pop_scope()
    return



def emit_lff_attn(p, z_hbm, zA_hbm, wqkv_hbm, wo_hbm, qk_cols_hbm,
                  tag="lfa", gate=None):
    p.begin_stage()
    p.push_scope()
    C = 512
    NW = 16
    WSEQ = 4

    Aq = p.sbuf(f"{tag}_Aq", [128, T], BF16)
    Bq = p.sbuf(f"{tag}_Bq", [128, T], BF16)
    Ak = p.sbuf(f"{tag}_Ak", [128, T], BF16)
    Bk = p.sbuf(f"{tag}_Bk", [128, T], BF16)
    Rbf = p.sbuf(f"{tag}_R", [128, 128], BF16)
    masks = [p.sbuf(f"{tag}_m{a}", [128, 1024], F32) for a in range(4)]
    qs_col = p.sbuf(f"{tag}_qsc", [128, 2], F32)
    ks_col = p.sbuf(f"{tag}_ksc", [128, 2], F32)

    sg, sqs = p.sem(), p.sem()
    sqk2 = [p.sem(), p.sem()]
    so2 = [p.sem(), p.sem()]
    szc, sw, ss, st, sv = (p.sem() for _ in range(5))
    sdone = p.sem()

    with p.block() as blk0:
        @blk0.sync
        def _(sync):
            p.emit_entry(sync, gate)
            sync.dma_start(out=qs_col[:, :], in_=qk_cols_hbm[:, 0:2]
                           ).then_inc(sqs.h, 16)
            sync.dma_start(out=ks_col[:, :], in_=qk_cols_hbm[:, 2:4]
                           ).then_inc(sqs.h, 16)

    emit_lff_tables(p, qs_col, ks_col, Aq, Bq, Ak, Bk, Rbf, masks, sg,
                    tag + "tb", wait=(sqs, 32), gate=gate)
    TBSG = 8

    wq = [p.sbuf(f"{tag}_wq{i}", [128, 1536], BF16) for i in range(DT)]
    wo_b = [p.sbuf(f"{tag}_wo{i}", [128, D], BF16) for i in range(4)]
    zc = [p.sbuf(f"{tag}_zc{j}", [128, DT * (C + 1)], F32) for j in range(2)]
    hq = p.sbuf(f"{tag}_hq", [128, DT * C], BF16)
    qkbf = p.sbuf(f"{tag}_qkbf", [128, 8 * C], BF16)
    sqb = p.sbuf(f"{tag}_sqb", [128, 8 * C], BF16)
    rows = p.sbuf(f"{tag}_rows", [2, 2 * C], F32)
    rows2 = p.sbuf(f"{tag}_rows2", [2, 2 * C], F32)
    rstd_b = [p.sbuf(f"{tag}_rsb{j}", [128, C], F32) for j in range(2)]
    rvb = [p.sbuf(f"{tag}_rvb{j}", [128, C], BF16) for j in range(2)]
    t1b = p.sbuf(f"{tag}_t1", [128, C], BF16)
    t2b = p.sbuf(f"{tag}_t2", [128, C], BF16)
    u_sb = p.sbuf(f"{tag}_u", [128, 4 * C], BF16)
    w_sb = [p.sbuf(f"{tag}_w{j}", [128, 4 * C], BF16) for j in range(2)]
    v_tm = [p.sbuf(f"{tag}_vtm{j}", [128, 4 * C], BF16) for j in range(2)]
    vtmp = [p.sbuf(f"{tag}_vt{j}", [128, C], BF16) for j in range(2)]
    S_sb = [p.sbuf(f"{tag}_S{j}", [128, 1024], F32) for j in range(2)]
    P_sb = p.sbuf(f"{tag}_P", [128, 4 * 1024], BF16)
    pcol = p.sbuf(f"{tag}_pcol", [128, 8], F32)
    PT_sb = p.sbuf(f"{tag}_PT", [128, 8 * C], BF16)
    osb = p.sbuf(f"{tag}_osb", [128, 4 * C], BF16)
    E2 = p.sbuf(f"{tag}_E2", [128, 2], BF16)
    E2f = p.sbuf(f"{tag}_E2f", [2, 128], F32)
    e2j = p.sbuf(f"{tag}_e2j", [128, 2], F32)
    e2p = p.sbuf(f"{tag}_e2p", [128, 1], F32)
    e2jv = p.sbuf(f"{tag}_e2jv", [2, 128], F32)
    e2pv = p.sbuf(f"{tag}_e2pv", [2, 1], F32)
    idp = p.sbuf(f"{tag}_idp", [128, 128], I32)
    idf = p.sbuf(f"{tag}_idf", [128, 128], I32)
    iden = p.sbuf(f"{tag}_iden", [128, 128], BF16)


    ps_f = [p.psum(f"{tag}_pf{j}", [128, C]) for j in range(4)]
    ps_b = [p.psum(f"{tag}_pb{j}", [128, C], BF16) for j in range(2)]
    ps_o = p.psum(f"{tag}_po", [64, C])
    ps_row = p.psum(f"{tag}_prow", [2, C])

    def mk_sched():
        st_n = ss_n = sv_n = 0
        L = []
        for wi in range(NW):
            w = wi % WSEQ
            ng = 8 if w > 0 else 4
            d = {"ng": ng}
            d["qkv"] = [st_n + m + 1 for m in range(12)]; st_n += 12
            d["vtr"] = [st_n + b + 1 for b in range(4)]; st_n += 4
            d["ssmm"], d["rmm"], d["bc"] = {}, {}, {}
            for t in range(8):
                d["ssmm"][t] = st_n + 1
                d["rmm"][t] = st_n + 2
                d["bc"][t] = st_n + 3
                st_n += 3
            d["smm"], d["pt"], d["omm"] = {}, {}, {}
            for h in range(8):
                for i4 in range(4):
                    d["smm"][(h, i4)] = st_n + 1; st_n += 1
                for g in range(ng):
                    d["pt"][(h, g)] = st_n + 1; st_n += 1
                d["omm"][h] = st_n + 1; st_n += 1
            d["wo"] = [st_n + m + 1 for m in range(DT)]; st_n += DT
            d["ev"] = {}
            for m in range(12):
                d["ev"][m] = ss_n + 1; ss_n += 1
            d["rsq"] = {}
            for t in range(8):
                d["rsq"][t] = ss_n + 1; ss_n += 1
            d["pexp"] = {}
            for h in range(8):
                for i4 in range(4):
                    d["pexp"][(h, i4)] = ss_n + 1; ss_n += 1
            d["mix"] = sv_n + 1; sv_n += 1
            d["rrec"], d["prep"], d["vte"] = {}, {}, {}
            for b in range(4):
                d["vte"][b] = sv_n + 1; sv_n += 1
            for t in range(8):
                d["rrec"][t] = sv_n + 1
                d["prep"][t] = sv_n + 2
                sv_n += 2
            d["sev"], d["pdiv"], d["ptev"], d["oev"] = {}, {}, {}, {}
            for h in range(8):
                for i4 in range(4):
                    d["sev"][(h, i4)] = sv_n + 1; sv_n += 1
                    d["pdiv"][(h, i4)] = sv_n + 1; sv_n += 1
                for g in range(ng):
                    d["ptev"][(h, g)] = sv_n + 1; sv_n += 1
                d["oev"][h] = sv_n + 1; sv_n += 1
            d["woev"] = [sv_n + m + 1 for m in range(DT)]; sv_n += DT
            L.append(d)
        return L

    SCH = mk_sched()
    HI = [(h, i4) for h in range(8) for i4 in range(4)]

    def sb3(tile, w=C):
        return tile[:, :].rearrange("p (a t) -> p a t", t=w)

    es = ExitStack()
    stag = [es.enter_context(p.nc.sbuf_tensor(f"{tag}_st{j}", [128, 768],
                                              F32)) for j in range(2)]
    wblocks = [(wq[i][:, h * 768:(h + 1) * 768],
                wqkv_hbm[i * 128:(i + 1) * 128, h * 768:(h + 1) * 768], 768)
               for i in range(DT) for h in range(2)] + \
              [(wo_b[i][:, h * 512:(h + 1) * 512],
                wo_hbm[i * 128:(i + 1) * 128, h * 512:(h + 1) * 512], 512)
               for i in range(4) for h in range(2)]

    with p.block() as blk:
        @blk.sync
        def _(sync):
            sync.wait_ge(sg.h, TBSG)
            for i, (dstt, src, w_) in enumerate(wblocks):
                if i >= 2:
                    sync.wait_ge(sw.h, i - 1)
                sync.dma_start(out=stag[i % 2][:, 0:w_], in_=src
                               ).then_inc(sqk2[i % 2].h, 16)
            for wi in range(NW):
                w = wi % WSEQ
                c0 = wi * C
                if wi >= 2:
                    sync.wait_ge(so2[wi % 2].h, (wi // 2) * 16)
                if w == 0:
                    sync.dma_start(
                        out=sb3(zc[wi % 2], C + 1)[:, :, 1:C + 1],
                        in_=z_hbm[:, c0:c0 + C].rearrange(
                            "(a p) t -> p a t", p=128)).then_inc(szc.h, 16)
                else:
                    sync.dma_start(
                        out=sb3(zc[wi % 2], C + 1),
                        in_=z_hbm[:, c0 - 1:c0 + C].rearrange(
                            "(a p) t -> p a t", p=128)).then_inc(szc.h, 16)
                sync.wait_ge(sv.h, SCH[wi]["woev"][-1])
                sync.dma_start(out=zA_hbm[:, c0:c0 + C].rearrange(
                    "(a p) t -> p a t", p=128),
                    in_=sb3(zc[wi % 2], C + 1)[:, :, 1:C + 1]
                    ).then_inc(so2[wi % 2].h, 16)
            sync.wait_ge(so2[0].h, NW * 8)
            sync.wait_ge(so2[1].h, NW * 8)
            sync.nop().then_inc(sdone.h, 1)

        @blk.gpsimd
        def _(g):
            p.entry_wait(g)
            g.wait_ge(sg.h, TBSG)
            g.iota(e2j[:, :], [[1, 2]], channel_multiplier=0,
                   allow_small_or_imprecise_dtypes=True)
            g.iota(e2p[:, :], [[0, 1]], channel_multiplier=1,
                   allow_small_or_imprecise_dtypes=True)
            g.iota(e2jv[:, :], [[1, 128]], channel_multiplier=0,
                   allow_small_or_imprecise_dtypes=True)
            g.iota(e2pv[:, :], [[0, 1]], channel_multiplier=1,
                   allow_small_or_imprecise_dtypes=True)
            g.iota(idp[:, :], [[0, 128]], channel_multiplier=1)
            g.iota(idf[:, :], [[1, 128]], channel_multiplier=0)
            g.drain()
            g.engine_nop().then_inc(st.h, 1)
            for i, (dstt, src, w_) in enumerate(wblocks):
                g.wait_ge(sqk2[i % 2].h, (i // 2 + 1) * 16)
                g.tensor_copy(dstt, stag[i % 2][:, 0:w_]
                              ).then_inc(sw.h, 1)

        @blk.scalar
        def _(sc):
            p.entry_wait(sc)
            for wi in range(NW):
                w = wi % WSEQ
                d = SCH[wi]
                pd = SCH[wi - 1] if wi >= 1 else None
                for m in range(12):
                    sc.wait_ge(st.h, 1 + d["qkv"][m])
                    if m < 8:
                        if pd is not None:
                            sc.wait_ge(st.h, 1 + pd["ssmm"][m])
                        if pd is not None and m == 0:
                            sc.wait_ge(sv.h, pd["prep"][7])
                        sc.copy(qkbf[:, m * C:(m + 1) * C],
                                ps_f[m % 2][:, :])
                        sc.activation(sqb[:, m * C:(m + 1) * C],
                                      ps_f[m % 2][:, :],
                                      AF.Square).then_inc(ss.h, 1)
                    else:
                        b = m - 8
                        if b >= 2:
                            sc.wait_ge(st.h, 1 + d["vtr"][b - 2])
                        elif pd is not None:
                            sc.wait_ge(st.h, 1 + pd["vtr"][3])
                        sc.copy(vtmp[b % 2][:, :], ps_f[m % 2][:, :]
                                ).then_inc(ss.h, 1)
                for t in range(8):
                    sc.wait_ge(st.h, 1 + d["ssmm"][t])
                    if t >= 2:
                        sc.wait_ge(sv.h, d["rrec"][t - 2])
                    elif pd is not None:
                        sc.wait_ge(sv.h, pd["rrec"][6 + t])
                    sc.activation(rows[0:2, (t % 2) * C:(t % 2 + 1) * C],
                                  ps_row[:, :], AF.Sqrt).then_inc(ss.h, 1)
                for h in range(8):
                    for i4 in range(4):
                        hi = h * 4 + i4
                        sc.wait_ge(sv.h, d["sev"][(h, i4)])
                        if h >= 1:
                            sc.wait_ge(st.h, 1 + d["pt"][(h - 1, d["ng"] - 1)])
                        elif pd is not None:
                            sc.wait_ge(st.h, 1 + pd["pt"][(7, pd["ng"] - 1)])
                        if hi >= 2:
                            sc.wait_ge(sv.h, d["pdiv"][HI[hi - 2]])
                        elif pd is not None:
                            sc.wait_ge(sv.h, pd["pdiv"][(7, 3)])
                        lo = 0 if w > 0 else C
                        pc = (hi % 2) * 4
                        sc.activation(
                            P_sb[:, i4 * 1024 + lo:i4 * 1024 + 1024],
                            S_sb[hi % 2][:, lo:1024], AF.Exp,
                            bias=pcol[:, pc + 1:pc + 2],
                            accum_out=pcol[:, pc + 2:pc + 3]
                            ).then_inc(ss.h, 1)

        @blk.tensor
        def _(te):
            p.entry_wait(te)
            te.wait_ge(st.h, 1)
            te.wait_ge(sg.h, TBSG + 1)
            te.wait_ge(sw.h, len(wblocks))
            for wi in range(NW):
                w = wi % WSEQ
                d = SCH[wi]
                pd = SCH[wi - 1] if wi >= 1 else None
                te.wait_ge(sv.h, d["mix"])
                for m in range(12):
                    if m >= 2:
                        te.wait_ge(ss.h, d["ev"][m - 2])
                    elif pd is not None:
                        te.wait_ge(ss.h, pd["ev"][11])
                    for i in range(DT):
                        mm = te.matmul(ps_f[m % 2][:, :],
                                       wq[i][:, m * 128:(m + 1) * 128],
                                       hq[:, i * C:(i + 1) * C],
                                       start=(i == 0), stop=(i == DT - 1))
                    mm.then_inc(st.h, 1)
                for b in range(4):
                    te.wait_ge(ss.h, d["ev"][8 + b])
                    if b >= 2:
                        te.wait_ge(sv.h, d["vte"][b - 2])
                    elif pd is not None:
                        te.wait_ge(sv.h, pd["vte"][3])
                    for i4 in range(4):
                        mm = te.transpose(
                            ps_b[b % 2][:, i4 * 128:(i4 + 1) * 128],
                            vtmp[b % 2][:, i4 * 128:(i4 + 1) * 128],
                            iden[:, :])
                    mm.then_inc(st.h, 1)
                for t in range(8):
                    te.wait_ge(ss.h, d["ev"][t])
                    if t >= 1:
                        te.wait_ge(ss.h, d["rsq"][t - 1])
                    elif pd is not None:
                        te.wait_ge(ss.h, pd["rsq"][7])
                    te.matmul(ps_row[:, :], E2[:, :],
                              sqb[:, t * C:(t + 1) * C],
                              start=True, stop=True).then_inc(st.h, 1)
                    if pd is not None and t < 2:
                        te.wait_ge(sv.h, pd["prep"][6 + t])
                    elif t >= 2:
                        te.wait_ge(sv.h, d["prep"][t - 2])
                    te.matmul(ps_f[2 + t % 2][:, :], Rbf[:, :],
                              qkbf[:, t * C:(t + 1) * C],
                              start=True, stop=True).then_inc(st.h, 1)
                    te.wait_ge(sv.h, d["rrec"][t])
                    te.matmul(ps_f[t % 2][:, :], E2f[:, :],
                              rows2[0:2, (t % 2) * C:(t % 2 + 1) * C],
                              start=True, stop=True).then_inc(st.h, 1)
                te.wait_ge(sv.h, d["prep"][7])
                for h in range(8):
                    hb = (h % 2) * 64
                    hc = (h // 2) * C
                    for i4 in range(4):
                        hi = h * 4 + i4
                        if hi >= 2:
                            te.wait_ge(sv.h, d["sev"][HI[hi - 2]])
                        elif pd is not None:
                            te.wait_ge(sv.h, pd["sev"][(7, 3)])
                        lhs = u_sb[hb:hb + 64, hc + i4 * 128:hc + (i4 + 1) * 128]
                        pr = 2 * (hi % 2)
                        if w > 0:
                            te.matmul(ps_f[pr][:, :],
                                      lhs,
                                      w_sb[(wi - 1) % 2][hb:hb + 64,
                                                         hc:hc + C],
                                      start=True, stop=True)
                        mm = te.matmul(ps_f[pr + 1][:, :], lhs,
                                       w_sb[wi % 2][hb:hb + 64, hc:hc + C],
                                       start=True, stop=True)
                        mm.then_inc(st.h, 1)
                    te.wait_ge(sv.h, d["pdiv"][(h, 3)])
                    for g in range(d["ng"]):
                        gj = g if w > 0 else g + 4
                        if g >= 2:
                            te.wait_ge(sv.h, d["ptev"][(h, g - 2)])
                        elif h >= 1:
                            te.wait_ge(sv.h, d["ptev"][(h - 1, d["ng"] - 1 - (1 - g))])
                        elif pd is not None:
                            te.wait_ge(sv.h, pd["ptev"][(7, pd["ng"] - 1 - (1 - g))])
                        for i4 in range(4):
                            mm = te.transpose(
                                ps_b[g % 2][:, i4 * 128:(i4 + 1) * 128],
                                P_sb[:, i4 * 1024 + gj * 128:
                                     i4 * 1024 + (gj + 1) * 128],
                                iden[:, :])
                        mm.then_inc(st.h, 1)
                    te.wait_ge(sv.h, d["ptev"][(h, d["ng"] - 1)])
                    if h >= 1:
                        te.wait_ge(sv.h, d["oev"][h - 1])
                    elif pd is not None:
                        te.wait_ge(sv.h, pd["oev"][7])
                    for g in range(d["ng"]):
                        if w > 0:
                            vsrc = v_tm[(wi - 1) % 2] if g < 4 else v_tm[wi % 2]
                            vt = g % 4
                        else:
                            vsrc = v_tm[wi % 2]
                            vt = g
                        vcol = (h // 2) * C + vt * 128 + hb
                        mm = te.matmul(ps_o[:, :],
                                       vsrc[:, vcol:vcol + 64],
                                       PT_sb[:, g * C:(g + 1) * C],
                                       start=(g == 0),
                                       stop=(g == d["ng"] - 1))
                    mm.then_inc(st.h, 1)
                te.wait_ge(sv.h, d["oev"][7])
                for m in range(DT):
                    if m >= 2:
                        te.wait_ge(sv.h, d["woev"][m - 2])
                    elif pd is not None:
                        te.wait_ge(sv.h, pd["woev"][DT - 1])
                    for i in range(4):
                        mm = te.matmul(ps_f[m % 2][:, :],
                                       wo_b[i][:, m * 128:(m + 1) * 128],
                                       osb[:, i * C:(i + 1) * C],
                                       start=(i == 0), stop=(i == 3))
                    mm.then_inc(st.h, 1)

        @blk.vector
        def _(v):
            p.entry_wait(v)
            v.wait_ge(st.h, 1)
            v.tensor_tensor(iden[:, :], idp[:, :], idf[:, :], ALU.is_equal)
            v.tensor_single_scalar(e2p[:, :], e2p[:, :], 64.0, ALU.is_ge)
            v.tensor_single_scalar(e2jv[:, :], e2jv[:, :], 64.0, ALU.is_ge)
            v.drain()
            v.tensor_scalar(E2[:, :], e2j[:, :], e2p[:, 0:1], None,
                            ALU.is_equal)
            v.tensor_scalar(E2f[:, :], e2jv[:, :], e2pv[:, 0:1], None,
                            ALU.is_equal)
            v.drain()
            v.engine_nop().then_inc(sg.h, 1)
            for wi in range(NW):
                w = wi % WSEQ
                d = SCH[wi]
                pd = SCH[wi - 1] if wi >= 1 else None
                v.wait_ge(szc.h, (wi + 1) * 16)
                if pd is not None:
                    v.wait_ge(st.h, 1 + pd["qkv"][11])
                if w == 0:
                    for i in range(DT):
                        v.memset(zc[wi % 2][:, i * (C + 1):i * (C + 1) + 1],
                                 0.0)
                    v.drain()
                for i in range(DT):
                    off = i * (C + 1) + (1 if i < 4 else 0)
                    v.tensor_copy(hq[:, i * C:(i + 1) * C],
                                  zc[wi % 2][:, off:off + C])
                v.drain()
                v.engine_nop().then_inc(sv.h, 1)
                for b in range(4):
                    v.wait_ge(st.h, 1 + d["vtr"][b])
                    v.tensor_copy(v_tm[wi % 2][:, b * C:(b + 1) * C],
                                  ps_b[b % 2][:, :]).then_inc(sv.h, 1)
                for t in range(8):
                    v.wait_ge(ss.h, d["rsq"][t])
                    if t >= 2:
                        v.wait_ge(st.h, 1 + d["bc"][t - 2])
                    elif pd is not None:
                        v.wait_ge(st.h, 1 + pd["bc"][6 + t])
                    v.reciprocal(rows2[0:2, (t % 2) * C:(t % 2 + 1) * C],
                                 rows[0:2, (t % 2) * C:(t % 2 + 1) * C])
                    v.drain()
                    if t < 4:
                        v.tensor_scalar_mul(
                            rows2[0:2, (t % 2) * C:(t % 2 + 1) * C],
                            rows2[0:2, (t % 2) * C:(t % 2 + 1) * C], 8.0)
                        v.drain()
                    v.engine_nop().then_inc(sv.h, 1)
                    A = Aq if t < 4 else Ak
                    Bt = Bq if t < 4 else Bk
                    v.wait_ge(st.h, 1 + d["bc"][t])
                    v.tensor_copy(rvb[t % 2][:, :], ps_f[2 + t % 2][:, :])
                    v.tensor_copy(rstd_b[t % 2][:, :], ps_f[t % 2][:, :])
                    v.drain()
                    v.tensor_tensor(t1b[:, :], qkbf[:, t * C:(t + 1) * C],
                                    A[:, w * C:(w + 1) * C], ALU.mult)
                    v.tensor_tensor(t2b[:, :], rvb[t % 2][:, :],
                                    Bt[:, w * C:(w + 1) * C], ALU.mult)
                    v.drain()
                    v.tensor_tensor(t1b[:, :], t1b[:, :], t2b[:, :], ALU.add)
                    v.drain()
                    dst = u_sb if t < 4 else w_sb[wi % 2]
                    v.tensor_tensor(dst[:, (t % 4) * C:(t % 4 + 1) * C],
                                    t1b[:, :], rstd_b[t % 2][:, :], ALU.mult)
                    v.drain()
                    v.engine_nop().then_inc(sv.h, 1)
                for h in range(8):
                    for i4 in range(4):
                        hi = h * 4 + i4
                        pc = (hi % 2) * 4
                        v.wait_ge(st.h, 1 + d["smm"][(h, i4)])
                        if hi >= 2:
                            v.wait_ge(ss.h, d["pexp"][HI[hi - 2]])
                        elif pd is not None:
                            v.wait_ge(ss.h, pd["pexp"][(7, 3)])
                        pr = 2 * (hi % 2)
                        if w > 0:
                            v.tensor_tensor(S_sb[hi % 2][:, 0:C],
                                            ps_f[pr][:, :],
                                            masks[i4][:, 0:C], ALU.add)
                        v.tensor_tensor(S_sb[hi % 2][:, C:1024],
                                        ps_f[pr + 1][:, :],
                                        masks[i4][:, C:1024], ALU.add)
                        v.drain()
                        lo = 0 if w > 0 else C
                        v.reduce_max(pcol[:, pc:pc + 1],
                                     S_sb[hi % 2][:, lo:1024], AX.X)
                        v.drain()
                        v.tensor_scalar_mul(pcol[:, pc + 1:pc + 2],
                                            pcol[:, pc:pc + 1], -1.0)
                        v.drain()
                        v.engine_nop().then_inc(sv.h, 1)
                        v.wait_ge(ss.h, d["pexp"][(h, i4)])
                        v.reciprocal(pcol[:, pc + 3:pc + 4],
                                     pcol[:, pc + 2:pc + 3])
                        v.drain()
                        v.tensor_scalar(
                            P_sb[:, i4 * 1024 + lo:i4 * 1024 + 1024],
                            P_sb[:, i4 * 1024 + lo:i4 * 1024 + 1024],
                            pcol[:, pc + 3:pc + 4], None, ALU.mult)
                        v.drain()
                        v.engine_nop().then_inc(sv.h, 1)
                    for g in range(d["ng"]):
                        v.wait_ge(st.h, 1 + d["pt"][(h, g)])
                        v.tensor_copy(PT_sb[:, g * C:(g + 1) * C],
                                      ps_b[g % 2][:, :]).then_inc(sv.h, 1)
                    v.wait_ge(st.h, 1 + d["omm"][h])
                    v.tensor_copy(osb[(h % 2) * 64:(h % 2) * 64 + 64,
                                      (h // 2) * C:(h // 2 + 1) * C],
                                  ps_o[:, :]).then_inc(sv.h, 1)
                if wi >= 2:
                    v.wait_ge(so2[wi % 2].h, (wi // 2) * 16)
                for m in range(DT):
                    v.wait_ge(st.h, 1 + d["wo"][m])
                    v.tensor_tensor(
                        zc[wi % 2][:, m * (C + 1) + 1:m * (C + 1) + C + 1],
                        ps_f[m % 2][:, :],
                        zc[wi % 2][:, m * (C + 1) + 1:m * (C + 1) + C + 1],
                        ALU.add).then_inc(sv.h, 1)

    es.close()
    p.pop_scope()
    return (sdone, 1)



def emit_lff_ffn(p, zA_hbm, zout_hbm, w1p_hbm, w2p_hbm, lnb_hbm, acc_hbm=None,
                 tag="lff", gate=None, wp=None):
    p.begin_stage()
    p.push_scope()
    C = 512
    NCH = TT // C
    PSEQ = T // C
    FT1 = 44
    HF = 22
    EPS = 1e-5
    SS_C = 2 + DT + FT1
    SV_C = 3 + DT + HF + DT

    def st_w1(c, pos):
        return 2 + 54 * c + pos + 1

    def st_w2(c, dt):
        return 2 + 54 * c + 45 + dt + 1

    def st_stats(c):
        return 1 if c == 0 else 2 + 54 * (c - 1) + 45

    def st_bcast(c):
        return 2 if c == 0 else 2 + 54 * (c - 1) + 54

    LO = [5, 6, 7, 8, 9, 10, 0, 1, 2, 3, 4, 5]
    NLG = len(LO)

    def pos_ft(pos):
        return 22 + pos if pos < HF else pos - HF

    def pos_oi(pos):
        ft = pos_ft(pos)
        if pos < HF:
            return 0 if ft < 24 else 1 + (ft - 24) // 4
        return 11 if ft >= 20 else 6 + ft // 4

    first_use, last_use = {}, {}
    for _pos in range(FT1):
        _o = pos_oi(_pos)
        first_use.setdefault(_o, _pos)
        last_use[_o] = _pos

    if wp is None:
        w1bf = p.dram(f"{tag}_w1bf", [128, 11 * DT * C], BF16,
                      kind="Internal")
        w2bf = p.dram(f"{tag}_w2bf", [128, HF * D], BF16, kind="Internal")
    else:
        w1bf, w2bf = wp[0], wp[1]
    w1bf4 = w1bf.rearrange("p (g a l) -> p g a l", a=DT, l=C)
    sini = p.sem()
    swg = [p.sem(), p.sem()]
    sw2l = p.sem()
    szc2 = [p.sem(), p.sem()]
    so2 = [Sem(p._alloc_sem()), Sem(p._alloc_sem())]
    sac = p.sem()
    ss, st, sv = (p.sem() for _ in range(3))
    sdone = p.sem()
    sgz = p.sem()
    if wp is None:
        wp_blocks = []
        for a in range(DT):
            wp_blocks.append((w1p_hbm[a * 128:(a + 1) * 128, 0:4096],
                              w1bf4[:, 0:8, a, :], 4096, C))
            wp_blocks.append((w1p_hbm[a * 128:(a + 1) * 128, 4096:5632],
                              w1bf4[:, 8:11, a, :], 1536, C))
        for ftt in range(HF):
            wp_blocks.append((w2p_hbm[ftt * 128:(ftt + 1) * 128, :],
                              w2bf[:, ftt * D:(ftt + 1) * D], D, None))
        done, _ = emit_wprep(p, None, tag=tag + "wp", gate=gate,
                             blocks=wp_blocks)
        wp_waits = [(done, 1)]
    else:
        _spo2, _t0, _t1 = wp[2]
        wp_waits = [(_spo2[0], _t0), (_spo2[1], _t1)]

    w1g = [p.sbuf(f"{tag}_w1g{j}", [128, DT * C], BF16) for j in range(2)]
    w2b = p.sbuf(f"{tag}_w2b", [128, HF * D], BF16)
    zc = [p.sbuf(f"{tag}_zc{j}", [128, DT * (C + 1)], F32) for j in range(2)]
    hb = p.sbuf(f"{tag}_hb", [128, DT * C], BF16)
    sqf = p.sbuf(f"{tag}_sqf", [128, DT * C], BF16)
    hn = [p.sbuf(f"{tag}_hn{j}", [128, DT * C], BF16) for j in range(2)]
    a_sb = p.sbuf(f"{tag}_a", [128, HF * C], BF16)
    g_sb = p.sbuf(f"{tag}_g", [128, HF * C], BF16)
    t2 = [p.sbuf(f"{tag}_t2{j}", [128, C], F32) for j in range(2)]
    a_b = p.sbuf(f"{tag}_ab", [128, C], F32)
    b_b = p.sbuf(f"{tag}_bb", [128, C], F32)
    rw = p.sbuf(f"{tag}_rw", [1, 6 * C], F32)
    lnb = p.sbuf(f"{tag}_lnb", [128, 68], F32)
    ones_col = p.sbuf(f"{tag}_oc", [128, 1], BF16)
    ones_row = p.sbuf(f"{tag}_or", [1, 128], F32)

    ps_sx = p.psum(f"{tag}_psx", [1, C])
    ps_sxx = p.psum(f"{tag}_psxx", [1, C])
    ps_a = p.psum(f"{tag}_pa", [128, C])
    ps_b2 = p.psum(f"{tag}_pb", [128, C])
    ps_h = [p.psum(f"{tag}_ph{j}", [128, C]) for j in range(2)]
    ps_y = [p.psum(f"{tag}_py{j}", [128, C]) for j in range(2)]

    R_M, R_E, R_V, R_SD, R_RS, R_B = (i * C for i in range(6))

    def sb3(tile, w=C):
        return tile[:, :].rearrange("p (a t) -> p a t", t=w)

    with p.block() as blk:
        @blk.sync
        def _(sync):
            if wp is not None:
                p.emit_entry(sync, gate)
            for _sem, _tgt in wp_waits:
                sync.wait_ge(_sem.h, _tgt)
            sync.dma_start(out=lnb[:, :], in_=lnb_hbm).then_inc(sini.h, 16)
            if acc_hbm is not None:
                sync.dma_start(out=zout_hbm, in_=acc_hbm).then_inc(sac.h, 16)
            sync.dma_start(out=w2b[:, :], in_=w2bf).then_inc(sw2l.h, 16)
            def zcload(sync, c):
                tcs = c % PSEQ
                c0 = c * C
                if c >= 2:
                    sync.wait_ge(sv.h, (c - 1) * SV_C)
                    sync.wait_ge(so2[c % 2].h, (c // 2) * 16)
                if tcs == 0:
                    sync.dma_start(
                        out=sb3(zc[c % 2], C + 1)[:, :, 1:C + 1],
                        in_=zA_hbm[:, c0:c0 + C].rearrange(
                            "(a p) t -> p a t", p=128)
                        ).then_inc(szc2[c % 2].h, 16)
                else:
                    sync.dma_start(
                        out=sb3(zc[c % 2], C + 1),
                        in_=zA_hbm[:, c0 - 1:c0 + C].rearrange(
                            "(a p) t -> p a t", p=128)
                        ).then_inc(szc2[c % 2].h, 16)

            zcload(sync, 0)
            for c in range(NCH):
                c0 = c * C
                if c + 1 < NCH:
                    zcload(sync, c + 1)
                for gi in range(NLG):
                    ggi = c * NLG + gi
                    if ggi >= 2:
                        if gi >= 2:
                            sync.wait_ge(st.h, st_w1(c, last_use[gi - 2]))
                        else:
                            sync.wait_ge(st.h,
                                         st_w1(c - 1, last_use[gi + 10]))
                    gg = LO[gi]
                    sync.dma_start(
                        out=sb3(w1g[ggi % 2]),
                        in_=w1bf[:, gg * DT * C:(gg + 1) * DT * C].rearrange(
                            "p (a m) -> p a m", m=C)
                        ).then_inc(swg[ggi % 2].h, 16)
                if acc_hbm is None:
                    sync.wait_ge(sv.h, (c + 1) * SV_C)
                    sync.dma_start(
                        out=zout_hbm[:, c0:c0 + C].rearrange(
                            "(a p) t -> p a t", p=128),
                        in_=sb3(zc[c % 2], C + 1)[:, :, 1:C + 1]
                        ).then_inc(so2[c % 2].h, 16)
            sync.wait_ge(so2[0].h, NCH * 8)
            sync.wait_ge(so2[1].h, NCH * 8)
            sync.nop().then_inc(sdone.h, 1)

        @blk.gpsimd
        def _(g):
            p.entry_wait(g)
            g.memset(ones_col[:, :], 1.0)
            g.memset(ones_row[:, :], 1.0)
            g.drain()
            g.engine_nop().then_inc(sgz.h, 1)
            for c in range(NCH + 1):
                if c < NCH and c % PSEQ == 0:
                    g.wait_ge(szc2[c % 2].h, (c // 2 + 1) * 16)
                    if c >= 2:
                        g.wait_ge(so2[c % 2].h, (c // 2) * 16)
                    for i in range(DT):
                        g.memset(zc[c % 2][:, i * (C + 1):i * (C + 1) + 1],
                                 0.0)
                    g.drain()
                    g.engine_nop().then_inc(sgz.h, 1)
                if acc_hbm is not None and c >= 1:
                    cc = c - 1
                    if cc == 0:
                        g.wait_ge(sac.h, 16)
                    g.wait_ge(sv.h, (cc + 1) * SV_C)
                    g.dma_start(
                        out=zout_hbm[:, cc * C:(cc + 1) * C].rearrange(
                            "(a p) t -> p a t", p=128),
                        in_=sb3(zc[cc % 2], C + 1)[:, :, 1:C + 1],
                        accum_op=ALU.add).then_inc(so2[cc % 2].h, 16)

        @blk.scalar
        def _(sc):
            p.entry_wait(sc)
            sc.wait_ge(sini.h, 16)
            for c in range(NCH):
                cs, cv = c * SS_C, c * SV_C
                sc.wait_ge(szc2[c % 2].h, (c // 2 + 1) * 16)
                sc.wait_ge(sgz.h, c // PSEQ + 2)
                if c >= 1:
                    sc.wait_ge(st.h, st_stats(c - 1))
                for i in range(DT):
                    off = i * (C + 1) + (1 if i < 4 else 0)
                    sc.copy(hb[:, i * C:(i + 1) * C],
                            zc[c % 2][:, off:off + C])
                    sc.activation(sqf[:, i * C:(i + 1) * C],
                                  zc[c % 2][:, off:off + C], AF.Square)
                sc.drain()
                sc.nop().then_inc(ss.h, 1)
                sc.wait_ge(sv.h, cv + 1)
                sc.activation(rw[0:1, R_SD:R_SD + C], rw[0:1, R_V:R_V + C],
                              AF.Sqrt).then_inc(ss.h, 1)
                for i in range(DT):
                    sc.wait_ge(sv.h, cv + 3 + i + 1)
                    sc.activation(hn[c % 2][:, i * C:(i + 1) * C],
                                  t2[i % 2][:, :],
                                  AF.Identity, bias=lnb[:, 8 + i:9 + i],
                                  scale=lnb[:, i:i + 1]).then_inc(ss.h, 1)
                for pos in range(FT1):
                    ft = pos_ft(pos)
                    sc.wait_ge(st.h, st_w1(c, pos))
                    if pos < HF:
                        sc.activation(g_sb[:, (ft - HF) * C:(ft - HF + 1) * C],
                                      ps_h[pos % 2][:, :],
                                      AF.Gelu, bias=lnb[:, 16 + ft:17 + ft]
                                      ).then_inc(ss.h, 1)
                    else:
                        sc.activation(a_sb[:, ft * C:(ft + 1) * C],
                                      ps_h[pos % 2][:, :], AF.Identity,
                                      bias=lnb[:, 16 + ft:17 + ft]
                                      ).then_inc(ss.h, 1)

        @blk.tensor
        def _(te):
            p.entry_wait(te)
            te.wait_ge(sgz.h, 1)
            te.wait_ge(sw2l.h, 16)

            def stats(c):
                cs = c * SS_C
                te.wait_ge(ss.h, cs + 1)
                if c >= 1:
                    te.wait_ge(sv.h, (c - 1) * SV_C + 1)
                for i in range(DT):
                    te.matmul(ps_sx[:, :], ones_col[:, :],
                              hb[:, i * C:(i + 1) * C],
                              start=(i == 0), stop=(i == DT - 1))
                    m2 = te.matmul(ps_sxx[:, :], ones_col[:, :],
                                   sqf[:, i * C:(i + 1) * C],
                                   start=(i == 0), stop=(i == DT - 1),
                                   skip_group_check=True)
                m2.then_inc(st.h, 1)

            def bcast(c):
                cv = c * SV_C
                te.wait_ge(sv.h, cv + 2)
                te.matmul(ps_a[:, :], ones_row[:, :], rw[0:1, R_RS:R_RS + C],
                          start=True, stop=True)
                te.matmul(ps_b2[:, :], ones_row[:, :], rw[0:1, R_B:R_B + C],
                          start=True, stop=True).then_inc(st.h, 1)

            stats(0)
            bcast(0)
            for c in range(NCH):
                cs, cv = c * SS_C, c * SV_C
                te.wait_ge(ss.h, cs + 2 + DT)
                for pos in range(FT1):
                    ft = pos_ft(pos)
                    oi = pos_oi(pos)
                    ggi2 = c * NLG + oi
                    if pos == first_use[oi]:
                        te.wait_ge(swg[ggi2 % 2].h, (ggi2 // 2 + 1) * 16)
                    if pos >= 2:
                        te.wait_ge(ss.h, cs + 2 + DT + pos - 1)
                    elif c >= 1:
                        te.wait_ge(ss.h, (c - 1) * SS_C + SS_C)
                    woff = (ft % 4)
                    for i in range(DT):
                        mm = te.matmul(
                            ps_h[pos % 2][:, :],
                            w1g[ggi2 % 2][:, i * C + woff * 128:
                                          i * C + (woff + 1) * 128],
                            hn[c % 2][:, i * C:(i + 1) * C],
                            start=(i == 0), stop=(i == DT - 1))
                    mm.then_inc(st.h, 1)
                if c + 1 < NCH:
                    stats(c + 1)
                else:
                    te.wait_ge(st.h, 2 + 54 * c + 44)
                    te.nop().then_inc(st.h, 1)
                te.wait_ge(sv.h, cv + 3 + DT + HF)
                for dt in range(DT):
                    if dt >= 2:
                        te.wait_ge(sv.h, cv + 3 + DT + HF + dt - 1)
                    elif c >= 1:
                        te.wait_ge(sv.h, c * SV_C)
                    for ftt in range(HF):
                        mm = te.matmul(
                            ps_y[dt % 2][:, :],
                            w2b[:, ftt * D + dt * 128:ftt * D + (dt + 1) * 128],
                            a_sb[:, ftt * C:(ftt + 1) * C],
                            start=(ftt == 0), stop=(ftt == HF - 1))
                    mm.then_inc(st.h, 1)
                if c + 1 < NCH:
                    bcast(c + 1)
                else:
                    te.wait_ge(st.h, 2 + 54 * c + 53)
                    te.nop().then_inc(st.h, 1)

        @blk.vector
        def _(v):
            p.entry_wait(v)
            for c in range(NCH):
                cs, cv = c * SS_C, c * SV_C
                v.wait_ge(st.h, st_stats(c))
                v.tensor_scalar_mul(rw[0:1, R_M:R_M + C], ps_sx[:, :], 1.0 / D)
                v.tensor_scalar_mul(rw[0:1, R_E:R_E + C], ps_sxx[:, :],
                                    1.0 / D)
                v.drain()
                v.tensor_tensor(rw[0:1, R_V:R_V + C], rw[0:1, R_M:R_M + C],
                                rw[0:1, R_M:R_M + C], ALU.mult)
                v.drain()
                v.tensor_tensor(rw[0:1, R_V:R_V + C], rw[0:1, R_E:R_E + C],
                                rw[0:1, R_V:R_V + C], ALU.subtract)
                v.drain()
                v.tensor_scalar_add(rw[0:1, R_V:R_V + C],
                                    rw[0:1, R_V:R_V + C], EPS
                                    ).then_inc(sv.h, 1)
                v.wait_ge(ss.h, cs + 2)
                v.reciprocal(rw[0:1, R_RS:R_RS + C], rw[0:1, R_SD:R_SD + C])
                v.drain()
                v.scalar_tensor_tensor(rw[0:1, R_B:R_B + C],
                                       rw[0:1, R_M:R_M + C], -1.0,
                                       rw[0:1, R_RS:R_RS + C],
                                       ALU.mult, ALU.mult).then_inc(sv.h, 1)
                v.wait_ge(st.h, st_bcast(c))
                v.tensor_copy(a_b[:, :], ps_a[:, :])
                v.tensor_copy(b_b[:, :], ps_b2[:, :])
                v.drain()
                v.engine_nop().then_inc(sv.h, 1)
                for i in range(DT):
                    off = i * (C + 1) + (1 if i < 4 else 0)
                    if i >= 2:
                        v.wait_ge(ss.h, cs + 2 + (i - 2) + 1)
                    v.tensor_tensor(t2[i % 2][:, :],
                                    zc[c % 2][:, off:off + C],
                                    a_b[:, :], ALU.mult)
                    v.drain()
                    v.tensor_tensor(t2[i % 2][:, :], t2[i % 2][:, :],
                                    b_b[:, :], ALU.add)
                    v.drain()
                    v.engine_nop().then_inc(sv.h, 1)
                for j in range(HF):
                    v.wait_ge(ss.h, cs + 2 + DT + HF + j + 1)
                    v.tensor_tensor(a_sb[:, j * C:(j + 1) * C],
                                    a_sb[:, j * C:(j + 1) * C],
                                    g_sb[:, j * C:(j + 1) * C], ALU.mult
                                    ).then_inc(sv.h, 1)
                if c >= 2:
                    v.wait_ge(so2[c % 2].h, (c // 2) * 16)
                for dt in range(DT):
                    off = dt * (C + 1) + 1
                    v.wait_ge(st.h, st_w2(c, dt))
                    v.scalar_tensor_tensor(zc[c % 2][:, off:off + C],
                                           ps_y[dt % 2][:, :],
                                           lnb[:, 60 + dt:61 + dt],
                                           zc[c % 2][:, off:off + C],
                                           ALU.add, ALU.add
                                           ).then_inc(sv.h, 1)

    p.pop_scope()
    return (sdone, 1)



def emit_add(p, a_hbm, b_hbm, y_hbm, tag="fad", gate=None):
    p.begin_stage()
    p.push_scope()
    C = 2048
    NCH = TT // C
    at = [p.sbuf(f"{tag}_a{j}", [128, DT * C // 2], F32) for j in range(2)]
    bt = [p.sbuf(f"{tag}_b{j}", [128, DT * C // 2], F32) for j in range(2)]
    yt = [p.sbuf(f"{tag}_y{j}", [128, DT * C // 2], F32) for j in range(2)]
    sa2 = [p.sem(), p.sem()]
    sb2 = [p.sem(), p.sem()]
    sv, so = p.sem(), p.sem()
    sdone = p.sem()
    HC = C // 2

    def fm(hbm, c):
        return hbm[:, c * HC:(c + 1) * HC].rearrange("(a p) t -> p a t", p=128)

    def sb3(tile):
        return tile[:, :].rearrange("p (a t) -> p a t", t=HC)

    NH = TT // HC
    with p.block() as blk:
        @blk.sync
        def _(sync):
            p.emit_entry(sync, gate)
            for c in range(NH):
                if c >= 2:
                    sync.wait_ge(sv.h, c - 1)
                sync.dma_start(out=sb3(at[c % 2]), in_=fm(a_hbm, c)
                               ).then_inc(sa2[c % 2].h, 16)
                sync.dma_start(out=sb3(bt[c % 2]), in_=fm(b_hbm, c)
                               ).then_inc(sb2[c % 2].h, 16)
                sync.wait_ge(sv.h, c + 1)
                sync.dma_start(out=fm(y_hbm, c), in_=sb3(yt[c % 2])
                               ).then_inc(so.h, 16)
            sync.wait_ge(so.h, NH * 16)
            sync.nop().then_inc(sdone.h, 1)

        @blk.vector
        def _(v):
            p.entry_wait(v)
            for c in range(NH):
                v.wait_ge(sa2[c % 2].h, (c // 2 + 1) * 16)
                v.wait_ge(sb2[c % 2].h, (c // 2 + 1) * 16)
                if c >= 2:
                    v.wait_ge(so.h, (c - 1) * 16)
                v.tensor_tensor(yt[c % 2][:, :], at[c % 2][:, :],
                                bt[c % 2][:, :], ALU.add).then_inc(sv.h, 1)

    p.pop_scope()
    return (sdone, 1)



def _to_fm(x_core):
    return np.ascontiguousarray(x_core.transpose(2, 0, 1).reshape(D, TT))


def _from_fm(y_fm):
    return np.ascontiguousarray(
        np.asarray(y_fm).reshape(D, B, T).transpose(1, 2, 0))


def _colpack(vec):
    n = vec.size // 128
    return np.ascontiguousarray(vec.reshape(n, 128).T).astype(np.float32)


def _prep_shared(I):
    HID = 2730
    sh = {}
    sh["ktT"] = np.ascontiguousarray(I["L0_attn_key_param_tokens"].T)
    sh["v0"] = np.ascontiguousarray(I["L0_attn_value_param_tokens"])
    sh["wr"] = np.ascontiguousarray(I["L0_ffn_W_router"])
    sh["br"] = np.ascontiguousarray(I["L0_ffn_b_router"].reshape(8, 1))
    sh["w1f"] = np.ascontiguousarray(I["L0_ffn_W1"].reshape(8 * D, 4096))
    sh["w2f"] = np.ascontiguousarray(I["L0_ffn_W2"].reshape(8 * 4096, D))
    sh["b1c"] = np.concatenate([_colpack(I["L0_ffn_b1"][e]) for e in range(8)],
                               axis=1)
    sh["b2c"] = np.concatenate([_colpack(I["L0_ffn_b2"][e]) for e in range(8)],
                               axis=1)
    for nm, key in (("wkT", "L1_attn_Wk"), ("wvT", "L1_attn_Wv"),
                    ("wrT", "L1_attn_Wr"), ("woT", "L1_attn_Wo")):
        sh[nm] = np.ascontiguousarray(I[key].T)
    sh["prm"] = np.concatenate([
        _colpack(I["L1_attn_time_mix_key"].ravel()),
        _colpack(I["L1_attn_time_mix_value"].ravel()),
        _colpack(I["L1_attn_time_mix_receptance"].ravel()),
        _colpack(I["L1_attn_time_decay"].ravel()),
        _colpack(I["L1_attn_time_first"].ravel()),
        _colpack(I["L1_attn_bo"].ravel())], axis=1)
    perm = lambda v: v[(np.arange(64) + 32) % 64]
    for l in range(2):
        qs = I["L1_ffn_q_scale"][l]
        ks = I["L1_ffn_k_scale"][l]
        sh[f"qkc{l}"] = np.ascontiguousarray(np.stack(
            [np.tile(qs, 2), np.tile(perm(qs), 2),
             np.tile(ks, 2), np.tile(perm(ks), 2)], axis=1)).astype(np.float32)
        sh[f"wqkv{l}"] = np.ascontiguousarray(I["L1_ffn_wqkv"][l])
        sh[f"wo{l}"] = np.ascontiguousarray(I["L1_ffn_wo"][l])
        w1 = I["L1_ffn_w1"][l]
        pad = np.zeros((D, 2816 - HID), np.float32)
        sh[f"w1p{l}"] = np.ascontiguousarray(
            np.concatenate([w1[:, :HID], pad, w1[:, HID:], pad], axis=1))
        sh[f"w2p{l}"] = np.ascontiguousarray(np.concatenate(
            [I["L1_ffn_w2"][l], np.zeros((2816 - HID, D), np.float32)],
            axis=0))
        b1 = I["L1_ffn_b1"][l]
        zpad = np.zeros(2816 - HID, np.float32)
        b1p = np.concatenate([b1[:HID], zpad, b1[HID:], zpad])
        sh[f"lnb{l}"] = np.concatenate(
            [_colpack(I["L1_ffn_ln_g"][l]), _colpack(I["L1_ffn_ln_b"][l]),
             _colpack(b1p), _colpack(I["L1_ffn_b2"][l])], axis=1)
    return sh


_DRAM_SPECS = dict(
    x=([D, TT], F32), xn_bf=([D, TT], BF16), ktT=([D, 4096], F32),
    v0=([4096, D], F32), acc0=([D, TT], F32), wr=([D, 8], F32),
    br=([8, 1], F32), wrows=([8, TT], F32), w1f=([8 * D, 4096], F32),
    w2f=([8 * 4096, D], F32), b1c=([128, 8 * 32], F32),
    b2c=([128, 8 * 8], F32), x1=([D, TT], F32), xn1_bf=([D, TT], BF16),
    xn1_f=([D, TT], F32), wkT=([D, D], F32), wvT=([D, D], F32),
    wrT=([D, D], F32), woT=([D, D], F32), prm=([128, 48], F32),
    acc1=([D, TT], F32), z1=([D, TT], F32), z2=([D, TT], F32),
    zA=([D, TT], F32), y=([D, TT], F32),
)
for _l in range(2):
    _DRAM_SPECS[f"wqkv{_l}"] = ([D, 1536], F32)
    _DRAM_SPECS[f"wo{_l}"] = ([512, D], F32)
    _DRAM_SPECS[f"qkc{_l}"] = ([128, 4], F32)
    _DRAM_SPECS[f"w1p{_l}"] = ([D, 5632], F32)
    _DRAM_SPECS[f"w2p{_l}"] = ([2816, D], F32)
    _DRAM_SPECS[f"lnb{_l}"] = ([128, 68], F32)

FUSED = True

_STAGES = [
    ("ln0", "ln", ["x"], ["xn_bf"]),
    ("l0a", "l0a", ["xn_bf", "x", "ktT", "v0"], ["acc0"]),
    ("rt", "rt", ["xn_bf", "wr", "br"], ["wrows"]),
    ("moe", "moe", ["xn_bf", "wrows", "w1f", "w2f", "b1c", "b2c", "acc0"],
     ["x1"]),
    ("ln1", "ln", ["x1"], ["xn1_bf", "xn1_f"]),
    ("l1a", "l1a", ["xn1_f", "x1", "wkT", "wvT", "wrT", "woT", "prm"],
     ["acc1"]),
    ("lfa0", "lfa", ["xn1_f", "wqkv0", "wo0", "qkc0"], ["zA"]),
    ("lff0", "lff", ["zA", "w1p0", "w2p0", "lnb0"], ["z1"]),
    ("lfa1", "lfa", ["z1", "wqkv1", "wo1", "qkc1"], ["zA"]),
    ("lff1", "lffy", ["zA", "w1p1", "w2p1", "lnb1", "acc1"], ["y"]),
]


def _emit_stage(p, key, aps, ins, outs, tag, gate, prep=None, **kw):
    a = [aps[n] for n in ins] + [aps[n] for n in outs]
    if key == "ln":
        if len(outs) == 2:
            return emit_ln(p, a[0], a[1], a[2], tag=tag, gate=gate, **kw)
        return emit_ln(p, a[0], a[1], tag=tag, gate=gate, **kw)
    if key == "l0a":
        return emit_l0_attn(p, a[0], a[1], a[2], a[3], a[4], tag=tag,
                            gate=gate, **kw)
    if key == "rt":
        return emit_router(p, a[0], a[1], a[2], a[3], tag=tag, gate=gate,
                           **kw)
    if key == "moe":
        return emit_moe(p, a[0], a[1], prep, a[2], a[3], a[4], a[5],
                        tag=tag, gate=gate, **kw)
    if key == "l1a":
        return emit_l1_attn(p, a[0], a[1], a[2], a[3], a[4], a[5], a[6],
                            a[7], tag=tag, gate=gate)
    if key == "lfa":
        return emit_lff_attn(p, a[0], a[4], a[1], a[2], a[3], tag=tag,
                             gate=gate)
    if key == "lff":
        return emit_lff_ffn(p, a[0], a[4], a[1], a[2], a[3], tag=tag,
                            gate=gate, **kw)
    if key == "lffy":
        return emit_lff_ffn(p, a[0], a[5], a[1], a[2], a[3], acc_hbm=a[4],
                            tag=tag, gate=gate, **kw)
    if key == "add":
        return emit_add(p, a[0], a[1], a[2], tag=tag, gate=gate)
    raise KeyError(key)


def kernel(**inputs):
    sh = _prep_shared(inputs)
    x = inputs["L0_attn_inputs"]
    cores = list(range(N_CORES))
    per_core_x = [_to_fm(x[4 * c:4 * c + 4]) for c in cores]

    if FUSED:
        p = Prog()
        aps = {}
        ext_in = ["x"] + [n for n in sh]
        for n in ext_in:
            aps[n] = p.dram(n, _DRAM_SPECS[n][0], _DRAM_SPECS[n][1],
                            kind="ExternalInput")
        aps["y"] = p.dram("y", [D, TT], F32, kind="ExternalOutput")
        for n in ["xn_bf", "acc0", "wrows", "x1", "xn1_bf", "xn1_f", "acc1",
                  "z1", "z2", "zA"]:
            aps[n] = p.dram(n, _DRAM_SPECS[n][0], _DRAM_SPECS[n][1],
                            kind="Internal")
        w1bf = p.dram("moe_w1bf", [8 * D, 4096], BF16, kind="Internal")
        w2bf = p.dram("moe_w2bf", [8 * 4096, D], BF16, kind="Internal")
        stag = [p.sbuf(f"mwp_s{j}", [128, 1024], F32) for j in range(2)]
        stbf = [p.sbuf(f"mwp_b{j}", [128, 1024], BF16) for j in range(2)]
        ps = PrepStream(p, stag, stbf)
        ps.add(moe_prep_blocks(aps["w1f"], aps["w2f"], w1bf, w2bf))
        prep = (w1bf, w2bf, [(ps.spo2, lambda e: (e + 1) * 32 * 16)])
        wpl = {}
        for li in range(2):
            w1L = p.dram(f"lff{li}_w1bf", [128, 11 * DT * 512], BF16,
                         kind="Internal")
            w2L = p.dram(f"lff{li}_w2bf", [128, 22 * D], BF16,
                         kind="Internal")
            gt = ps.add(lff_prep_blocks(aps[f"w1p{li}"], aps[f"w2p{li}"],
                                        w1L, w2L))
            wpl[li] = (w1L, w2L, gt)
        extra = {
            "ln0": dict(prep_emit=lambda g: ps.emit(g, 24)),
            "l0a": dict(prep_emit=lambda g: ps.emit(g, 160)),
            "rt": dict(prep_emit=lambda g: ps.emit(g, 16)),
            "moe": dict(prep_emit=lambda g: ps.emit(g)),
            "lff0": dict(wp=wpl[0]),
            "lff1": dict(wp=wpl[1]),
        }
        gate = None
        for name, key, ins, outs in _STAGES:
            if key == "moe":
                ins = [n for n in ins if n not in ("w1f", "w2f")]
            gate = _emit_stage(p, key, aps, ins, outs, name, gate,
                               prep=prep, **extra.get(name, {}))
        nc = p.close()
        in_maps = [dict(sh, x=per_core_x[c]) for c in cores]
        res = run_bass_kernel_spmd(nc, in_maps, core_ids=cores)
        return np.concatenate(
            [_from_fm(res.results[c]["y"]) for c in cores], axis=0)

    state = [dict(sh, x=per_core_x[c]) for c in cores]
    for name, key, ins, outs in _STAGES:
        p = Prog()
        aps = {}
        for n in ins:
            aps[n] = p.dram(n, _DRAM_SPECS[n][0], _DRAM_SPECS[n][1],
                            kind="ExternalInput")
        for n in outs:
            aps[n] = p.dram(n, _DRAM_SPECS[n][0], _DRAM_SPECS[n][1],
                            kind="ExternalOutput")
        prep = None
        map_ins = list(ins)
        if key == "moe":
            w1bf = p.dram("moe_w1bf", [8 * D, 4096], BF16, kind="Internal")
            w2bf = p.dram("moe_w2bf", [8 * 4096, D], BF16, kind="Internal")
            stag = [p.sbuf(f"mwp_s{j}", [128, 1024], F32) for j in range(2)]
            stbf = [p.sbuf(f"mwp_b{j}", [128, 1024], BF16) for j in range(2)]
            wpso2 = emit_moewp(p, aps["w1f"], aps["w2f"], w1bf, w2bf,
                               stag, stbf)
            prep = (w1bf, w2bf, [(wpso2, lambda e: (e + 1) * 32 * 16)])
            ins = [n for n in ins if n not in ("w1f", "w2f")]
        _emit_stage(p, key, aps, ins, outs, name, None, prep=prep)
        nc = p.close()
        in_maps = [{n: state[c][n] for n in map_ins} for c in cores]
        res = run_bass_kernel_spmd(nc, in_maps, core_ids=cores)
        for c in cores:
            for n in outs:
                state[c][n] = np.asarray(res.results[c][n])
    return np.concatenate([_from_fm(state[c]["y"]) for c in cores], axis=0)
